# version 05 (final)
# baseline (speedup 1.0000x reference)
import dataclasses
from contextlib import ExitStack

import numpy as np
import ml_dtypes

import concourse.bass as bass
import concourse.mybir as mybir
from concourse.bass_utils import run_bass_kernel_spmd

F32 = mybir.dt.float32
BF16 = mybir.dt.bfloat16
I32 = mybir.dt.int32
AF = mybir.ActivationFunctionType
ALU = mybir.AluOpType
AX = mybir.AxisListType

B, T, D = 2, 8192, 1024
H, DH, NPAIR = 16, 64, 8
NBLK, CH = 8, 128
NCORES, SL = 8, 2048
HIDDEN, NE = 2816, 8
HQ_TILES = [6, 5, 6, 5]
HQ_OFF = [0, 768, 1408, 2176]
GN_N = float(DH * T)
EPS = 1e-5

O_IDNB, O_IDNF = 0, 256
O_GNW, O_GNB, O_EB = 768, 800, 832
O_ONES1 = 864
O_G128, O_HALF = 1376, 1408
O_EPSC = 1424
O_SPD, O_BPD, O_REXP, O_MEXP = 1440, 1472, 1504, 1536
O_SLOFF, O_SROW, O_SSUM, O_STK = 1568, 1600, 1728, 1760
O_WSEL = 2048
O_RBUF = 2560
O_WROW = 10752
O_GLOG, O_GSC, O_X1S = 18944, 19456, 20992
O_DT, O_GROW, O_GKT = 25088, 33280, 37376
O_S, O_SBF, O_STATS = 41472, 43520, 44544
O_KG, O_QG, O_AH, O_TRASH = 52736, 53248, 53760, 54784
O_XBLK, O_XT, O_QT, O_KT = 55040, 71424, 87808, 104192
O_VT, O_YB, O_WQ, O_WK, O_WV = 120576, 136960, 153344, 169728, 186112
O_ACC, O_X1TB, O_YSL, O_YN, O_WO = 25088, 90624, 123392, 156160, 188928
O_SMOE, O_WBC = 123392, 147968
O_SLA, O_SLB, O_SLC = 156160, 168448, 180736
O_H1W, O_T1 = 193024, 201216
O_OUTS = 123392
ARENA = 209408

_BF = ml_dtypes.bfloat16


def _tables():
    g = 1.0 - 2.0 ** (-5.0 - np.arange(H, dtype=np.float64))
    i = np.arange(CH, dtype=np.float64)
    diff = i[None, :] - i[:, None]
    dt = np.zeros((CH, NPAIR * 2 * CH), np.float64)
    grow = np.zeros((CH, NPAIR * CH), np.float64)
    gkt = np.zeros((CH, NPAIR * CH), np.float64)
    g128 = np.zeros((CH, NPAIR), np.float64)
    for p_ in range(NPAIR):
        for half in range(2):
            h = 2 * p_ + half
            m = np.where(diff >= 0, g[h] ** np.maximum(diff, 0.0), 0.0)
            dt[:, p_ * 256 + half * 128:p_ * 256 + half * 128 + 128] = m
            rows = slice(half * 64, half * 64 + 64)
            grow[rows, p_ * 128:(p_ + 1) * 128] = (g[h] ** (i + 1.0))[None, :]
            gkt[:, p_ * 128 + half * 64:p_ * 128 + half * 64 + 64] = \
                (g[h] ** (127.0 - i))[:, None]
            g128[rows, p_] = g[h] ** 128.0
    return (dt.astype(np.float32), grow.astype(np.float32),
            gkt.astype(np.float32), g128.astype(np.float32))


class Prog:
    ENGS = ("sync", "gpsimd", "tensor", "vector", "scalar")

    def __init__(self):
        self.q = {e: [] for e in self.ENGS}
        self.cnt = {"ds": 0, "dg": 0, "pe": 0, "dv": 0, "ac": 0}

    def op(self, eng, fn, waits=(), inc=None):
        n = 16 if inc in ("ds", "dg") else 1
        w = [(s, v) for (s, v) in waits if s is not None and v > 0]
        self.seq = getattr(self, "seq", 0) + 1
        self.q[eng].append((w, fn, inc, n, self.seq))
        if inc is not None:
            self.cnt[inc] += n
            return (inc, self.cnt[inc])
        return None

    def ev(self, sem):
        return (sem, self.cnt[sem])


def build_program(cutoff=None):
    nc = bass.Bass("TRN2", target_bir_lowering=False)

    xb = nc.dram_tensor("xb", [T, D], F32, kind="ExternalInput").ap()
    xsl = nc.dram_tensor("xsl", [SL, D], F32, kind="ExternalInput").ap()
    sloff = nc.dram_tensor("sloff", [1, 4], I32, kind="ExternalInput").ap()
    wq_d = nc.dram_tensor("wq", [D, D], F32, kind="ExternalInput").ap()
    wk_d = nc.dram_tensor("wk", [D, D], F32, kind="ExternalInput").ap()
    wv_d = nc.dram_tensor("wv", [D, D], F32, kind="ExternalInput").ap()
    wo_d = nc.dram_tensor("wo", [D, D], F32, kind="ExternalInput").ap()
    gw_d = nc.dram_tensor("gw", [D, NE], F32, kind="ExternalInput").ap()
    pc_d = nc.dram_tensor("pconsts", [128, 24], F32, kind="ExternalInput").ap()
    w1_d = nc.dram_tensor("w1", [NE, D, HIDDEN], F32, kind="ExternalInput").ap()
    w2_d = nc.dram_tensor("w2", [NE, HIDDEN, D], F32, kind="ExternalInput").ap()
    w3_d = nc.dram_tensor("w3", [NE, D, HIDDEN], F32, kind="ExternalInput").ap()
    out_d = nc.dram_tensor("out", [SL, D], F32, kind="ExternalOutput").ap()
    yhbm = nc.dram_tensor("yhbm", [128, 8 * T], BF16).ap()

    dt_np, grow_np, gkt_np, g128_np = _tables()
    c_dt = nc.inline_tensor(dt_np, name="c_dt").ap()
    c_grow = nc.inline_tensor(grow_np, name="c_grow").ap()
    c_gkt = nc.inline_tensor(gkt_np, name="c_gkt").ap()
    c_g128 = nc.inline_tensor(g128_np, name="c_g128").ap()
    c_idnb = nc.inline_tensor(np.eye(128).astype(_BF), name="c_idnb").ap()
    c_idnf = nc.inline_tensor(np.eye(128, dtype=np.float32), name="c_idnf").ap()
    half_np = np.zeros((128, 2), np.float32)
    half_np[0:64, 0] = 1.0
    half_np[64:128, 1] = 1.0
    c_half = nc.inline_tensor(half_np, name="c_half").ap()

    p = Prog()
    op, ev = p.op, p.ev

    with ExitStack() as ctx:
        arena = nc.alloc_sbuf_tensor("arena", [128, ARENA], mybir.dt.uint8).ap()
        gw_sb = nc.alloc_sbuf_tensor("gwsbuf", [128, 8 * NE], F32).ap() \
            .rearrange("p (a f) -> p a f", a=8)

        def v_(off, n, dtype):
            sz = mybir.dt.size(dtype)
            return arena[:, off:off + n * sz].bitcast(dtype)

        def v3(off, n, dtype, a):
            return v_(off, n, dtype).rearrange("p (a f) -> p a f", a=a)

        idnb = v_(O_IDNB, 128, BF16)
        idnf = v_(O_IDNF, 128, F32)
        gnw = v_(O_GNW, 8, F32)
        gnb = v_(O_GNB, 8, F32)
        ebias = v_(O_EB, 8, F32)
        ones1 = v_(O_ONES1, 128, F32)[0:1, :]
        g128c = v_(O_G128, 8, F32)
        halfo = v_(O_HALF, 2, F32)
        epsc = v_(O_EPSC, 1, F32)
        spd = v_(O_SPD, 8, F32)
        bpd = v_(O_BPD, 8, F32)
        rexp = v_(O_REXP, 8, F32)
        mexp = v_(O_MEXP, 8, F32)
        sloff_sb = arena[:, O_SLOFF:O_SLOFF + 16].bitcast(I32)[0:1, 0:4]
        srow = v_(O_SROW, 32, F32)[0:1, :]
        ssum = v_(O_SSUM, 2, F32)[0:16, :]
        stk = v_(O_STK, 16, F32)
        wsel = v_(O_WSEL, 128, F32)
        rbuf = [v_(O_RBUF + i * 4096, 1024, F32) for i in range(2)]
        wt8 = v_(O_RBUF, 2048, F32)[0:8, :]
        wrow = v_(O_WROW, 2048, F32)[0:1, :]
        xslt = [v_(O_WROW + i * 4096, 1024, F32) for i in range(2)]
        glog = v_(O_GLOG, 128, F32)
        gsc = v_(O_GSC, 384, F32)
        bn6 = gsc[:, 288:300]
        bn2 = gsc[:, 304:306]
        lncol = gsc[:, 308:312]
        x1s = v3(O_X1S, 1024, F32, 8)
        dtt = v3(O_DT, 2048, F32, 8)
        growt = v3(O_GROW, 1024, F32, 8)
        gktt = v3(O_GKT, 1024, F32, 8)
        st = v3(O_S, 512, F32, 8)
        sbf = v3(O_SBF, 512, BF16, 8)
        stats = v_(O_STATS, 2048, F32)
        kg = [v_(O_KG + i * 256, 128, BF16) for i in range(2)]
        qg = [v_(O_QG + i * 256, 128, BF16) for i in range(2)]
        ah = [v_(O_AH + i * 512, 256, BF16) for i in range(2)]
        trash = [v_(O_TRASH, 128, BF16), v_(O_STATS + 4096, 128, BF16)]
        xblk = v3(O_XBLK, 8192, BF16, 8)
        xt = v3(O_XT, 8192, BF16, 8)
        qt = v3(O_QT, 8192, BF16, 8)
        kt = v3(O_KT, 8192, BF16, 8)
        vt = v3(O_VT, 8192, BF16, 8)
        yb = v3(O_YB, 8192, BF16, 8)
        wqs = v3(O_WQ, 8192, BF16, 8)
        wks = v3(O_WK, 8192, BF16, 8)
        wvs = v3(O_WV, 8192, BF16, 8)
        acc = v3(O_ACC, 16384, F32, 16)
        x1tb = v3(O_X1TB, 16384, BF16, 8)
        ysl = v3(O_YSL, 16384, BF16, 8)
        x1bscr = v_(O_YSL, 1024, BF16)
        yn = v3(O_YN, 16384, BF16, 8)
        wos = v3(O_WO, 8192, BF16, 8)
        smoe = v3(O_SMOE, 12288, BF16, 6)
        wbc = v_(O_WBC, 2048, F32)
        h1w = [v_(O_H1W + i * 4096, 1024, F32) for i in range(2)]
        t1 = [v_(O_T1 + i * 4096, 1024, F32) for i in range(2)]
        outs = [v_(O_OUTS + i * 4096, 1024, F32) for i in range(2)]

        def slA(q):
            return v3(O_SLA, 8 * HQ_TILES[q] * 128, BF16, 8)

        def slB(q):
            return v3(O_SLB, 8 * HQ_TILES[q] * 128, BF16, 8)

        def slC(q):
            return v3(O_SLC, HQ_TILES[q] * 1024, BF16, HQ_TILES[q])

        pb_bf = [ctx.enter_context(nc.psum_tensor(f"pbb{i}", [128, 1024], BF16))
                 for i in range(2)]
        pb = [ctx.enter_context(nc.psum_tensor(f"pb{i}", [128, 512], F32))
              for i in range(6)]

        sems = {s: ctx.enter_context(nc.semaphore(s))
                for s in ("ds", "dg", "pe", "dv", "ac")}
        block = ctx.enter_context(nc.Block())

        def dma(eng, out, in_, waits=()):
            return op(eng, lambda e, o=out, i=in_: e.dma_start(out=o, in_=i),
                      waits, inc="ds" if eng == "sync" else "dg")

        dma("sync", v_(O_DT, 2048, F32), c_dt)
        dma("sync", v_(O_GROW, 1024, F32), c_grow)
        dma("sync", v_(O_GKT, 1024, F32), c_gkt)
        dma("sync", g128c, c_g128)
        dma("sync", idnb, c_idnb)
        dma("sync", idnf, c_idnf)
        dma("sync", halfo, c_half)
        dma("sync", arena[:, O_GNW:O_GNW + 96].bitcast(F32), pc_d)
        dma("sync", sloff_sb, sloff)
        ev_consts = dma("sync", gw_sb, gw_d.rearrange("(a p) f -> p a f", p=128))
        for wsb, wdr in ((wqs, wq_d), (wks, wk_d), (wvs, wv_d)):
            dma("gpsimd", wsb, wdr.rearrange("(a p) f -> p a f", p=128))
        ev_wqkv = ev("dg")

        op("vector", lambda v: v.memset(v_(O_S, 512, F32), 0.0))
        op("vector", lambda v: v.memset(epsc, EPS))
        op("vector", lambda v: v.memset(v_(O_SBF, 512, BF16), 0.0))
        ev_sbzero = op("vector", lambda v: v.memset(ones1, 1.0), inc="dv")

        xb3 = xb.rearrange("(a p) d -> p a d", p=128)
        yhbm3 = yhbm.rearrange("p (a t) -> p a t", a=8)

        ev_par_post = [ev_sbzero, ev_sbzero]
        ev_par_sq = [None, None]
        ev_kg_last = [ev_sbzero, ev_sbzero]
        ev_yb_dma = None
        ev_projbank = {0: None, 1: None}
        ev_xtrans_done = None
        unit_list = [(c_, j_) for c_ in range(8) for j_ in range(NPAIR)]
        projs = [(wqs, qt, 0.125, False), (wks, kt, None, False),
                 (wvs, vt, None, True)]

        for k in range(NBLK):
            w_ = [ev_consts]
            if ev_xtrans_done is not None:
                w_.append(ev_xtrans_done)
            evx = dma("gpsimd", xblk, xb3[:, 8 * k:8 * k + 8, :], waits=w_)

            ev_xt_cp = [None, None]
            for tt in range(8):
                par = tt % 2
                w_t = []
                if tt == 0:
                    w_t = [evx, ev_consts]
                if tt < 2:
                    w_t.append(ev_kg_last[par])
                else:
                    w_t.append(ev_xt_cp[par])
                for dt_i in range(8):
                    op("tensor",
                       lambda t, tt_=tt, d_=dt_i, par_=par:
                       t.transpose(pb_bf[par_][:, d_ * 128:(d_ + 1) * 128],
                                   xblk[:, tt_, d_ * 128:(d_ + 1) * 128],
                                   idnb),
                       w_t if dt_i == 0 else [],
                       inc="pe" if dt_i == 7 else None)
                ev_tt = ev("pe")
                ev_xt_cp[par] = op(
                    "vector",
                    lambda v, tt_=tt, par_=par:
                    v.tensor_copy(xt[:, :, tt_ * 128:(tt_ + 1) * 128],
                                  pb_bf[par_].rearrange("p (a f) -> p a f",
                                                        a=8)),
                    [ev_tt], inc="dv")
            ev_xtrans_done = ev("pe")
            ev_xt_all = ev_xt_cp[1]

            for which, (wsb, dst, scale, tok_major) in enumerate(projs):
                for m in range(8):
                    mp = m % 2
                    bank0, bank1 = (pb[0], pb[1]) if mp == 0 else (pb[2], pb[3])
                    w_m = []
                    if which == 0 and m == 0:
                        w_m = [ev_xt_all, ev_wqkv]
                        if ev_par_sq[0] is not None:
                            w_m += [ev_par_sq[0], ev_par_sq[1]]
                        w_m.append(ev("dv"))
                    if ev_projbank[mp] is not None:
                        w_m.append(ev_projbank[mp])
                    for kk in range(8):
                        st_, sp_ = kk == 0, kk == 7
                        if tok_major:
                            l0 = lambda kk_, m_: xt[:, kk_,
                                                    m_ * 128:(m_ + 1) * 128]
                            r0 = lambda kk_, n_, w__=wsb: \
                                w__[:, kk_, n_ * 512:(n_ + 1) * 512]
                        else:
                            l0 = lambda kk_, m_, w__=wsb: \
                                w__[:, kk_, m_ * 128:(m_ + 1) * 128]
                            r0 = lambda kk_, n_: xt[:, kk_,
                                                    n_ * 512:(n_ + 1) * 512]
                        op("tensor",
                           lambda t, kk_=kk, m_=m, b_=bank0, l_=l0, r_=r0,
                                  st__=st_, sp__=sp_:
                           t.matmul(b_[:, :], l_(kk_, m_), r_(kk_, 0),
                                    start=st__, stop=sp__),
                           w_m if kk == 0 else [])
                        op("tensor",
                           lambda t, kk_=kk, m_=m, b_=bank1, l_=l0, r_=r0,
                                  st__=st_, sp__=sp_:
                           t.matmul(b_[:, :], l_(kk_, m_), r_(kk_, 1),
                                    start=st__, stop=sp__),
                           inc="pe" if kk == 7 else None)
                    ev_mm = ev("pe")

                    def evac(v, b_, dst_, m_, n_, sc_):
                        o_ = dst_[:, m_, n_ * 512:(n_ + 1) * 512]
                        if sc_ is None:
                            return v.tensor_copy(o_, b_[:, :])
                        return v.tensor_scalar(o_, b_[:, :], sc_, None,
                                               ALU.mult)
                    op("vector",
                       lambda v, b_=bank0, d_=dst, m_=m, sc_=scale:
                       evac(v, b_, d_, m_, 0, sc_), [ev_mm])
                    ev_projbank[mp] = op(
                        "vector",
                        lambda v, b_=bank1, d_=dst, m_=m, sc_=scale:
                        evac(v, b_, d_, m_, 1, sc_), inc="dv")
            ev_projevac = ev("dv")

            NU = len(unit_list)

            def emit_TA(u):
                c_, j_ = unit_list[u]
                par = u % 2
                w_ta = [ev_par_post[par], ev_kg_last[par]]
                if ev_par_sq[par] is not None:
                    w_ta.append(ev_par_sq[par])
                if u == 0:
                    w_ta.append(ev_projevac)
                op("tensor",
                   lambda t, c__=c_, j__=j_, par_=par:
                   t.transpose(pb_bf[par_][:, 0:128],
                               kt[:, j__, c__ * 128:(c__ + 1) * 128], idnb),
                   w_ta)
                op("tensor",
                   lambda t, c__=c_, j__=j_, par_=par:
                   t.matmul(pb[4 + par_][0:128, 0:128],
                            kt[0:64, j__, c__ * 128:(c__ + 1) * 128],
                            qt[0:64, j__, c__ * 128:(c__ + 1) * 128],
                            start=True, stop=True))
                return op("tensor",
                          lambda t, c__=c_, j__=j_, par_=par:
                          t.matmul(pb[2 + par_][0:128, 0:128],
                                   kt[64:128, j__, c__ * 128:(c__ + 1) * 128],
                                   qt[64:128, j__, c__ * 128:(c__ + 1) * 128],
                                   start=True, stop=True,
                                   tile_position=(64, 0)),
                          inc="pe")

            def emit_pre(u, ev_ta):
                c_, j_ = unit_list[u]
                par = u % 2
                ev_kg_last[par] = op(
                    "vector",
                    lambda v, j__=j_, par_=par:
                    v.tensor_tensor(kg[par_], pb_bf[par_][:, 0:128],
                                    gktt[:, j__, :], ALU.mult),
                    [ev_ta], inc="dv")
                op("vector",
                   lambda v, j__=j_, par_=par:
                   v.tensor_tensor(ah[par_][:, 0:128], pb[4 + par_][:, 0:128],
                                   dtt[:, j__, 0:128], ALU.mult))
                op("vector",
                   lambda v, j__=j_, par_=par:
                   v.tensor_tensor(ah[par_][:, 128:256], pb[2 + par_][:, 0:128],
                                   dtt[:, j__, 128:256], ALU.mult))
                return op("vector",
                          lambda v, c__=c_, j__=j_, par_=par:
                          v.tensor_tensor(qg[par_],
                                          qt[:, j__,
                                             c__ * 128:(c__ + 1) * 128],
                                          growt[:, j__, :], ALU.mult),
                          inc="dv")

            def emit_YU(u, ev_pre):
                c_, j_ = unit_list[u]
                par = u % 2
                y_ = pb[par]
                op("tensor",
                   lambda t, c__=c_, j__=j_, par_=par, yy=y_:
                   t.matmul(yy[0:64, 0:128],
                            vt[:, c__, j__ * 128:j__ * 128 + 64],
                            ah[par_][:, 0:128], start=True, stop=False),
                   [ev_pre])
                op("tensor",
                   lambda t, j__=j_, par_=par, yy=y_:
                   t.matmul(yy[0:64, 0:128], sbf[0:64, j__, :],
                            qg[par_][0:64, :], start=False, stop=True))
                op("tensor",
                   lambda t, c__=c_, j__=j_, par_=par, yy=y_:
                   t.matmul(yy[64:128, 0:128],
                            vt[:, c__, j__ * 128 + 64:(j__ + 1) * 128],
                            ah[par_][:, 128:256], start=True, stop=False,
                            tile_position=(0, 64)))
                op("tensor",
                   lambda t, j__=j_, par_=par, yy=y_:
                   t.matmul(yy[64:128, 0:128], sbf[64:128, j__, :],
                            qg[par_][64:128, :], start=False, stop=True,
                            tile_position=(64, 64)))
                op("tensor",
                   lambda t, c__=c_, j__=j_, par_=par:
                   t.matmul(pb[4 + par_][0:64, 256:320], kg[par_][:, 0:64],
                            vt[:, c__, j__ * 128:j__ * 128 + 64],
                            start=True, stop=True))
                return op("tensor",
                          lambda t, c__=c_, j__=j_, par_=par:
                          t.matmul(pb[4 + par_][64:128, 256:320],
                                   kg[par_][:, 64:128],
                                   vt[:, c__, j__ * 128 + 64:(j__ + 1) * 128],
                                   start=True, stop=True,
                                   tile_position=(0, 64)),
                          inc="pe")

            def emit_post(u, ev_yu, mglob):
                c_, j_ = unit_list[u]
                par = u % 2
                y_ = pb[par]
                w_p = [ev_yu]
                if u == 0 and ev_yb_dma is not None:
                    w_p.append(ev_yb_dma)
                ev_ye = op("vector",
                           lambda v, c__=c_, j__=j_, yy=y_, m_=mglob:
                           v.tensor_scalar(
                               yb[:, j__, c__ * 128:(c__ + 1) * 128],
                               yy[:, 0:128], 1.0, None, ALU.mult, ALU.add,
                               accum_out=stats[:, j__ * 64 + m_:
                                               j__ * 64 + m_ + 1]),
                           w_p, inc="dv")
                ev_su = op("vector",
                           lambda v, j__=j_, par_=par:
                           v.scalar_tensor_tensor(st[:, j__, :], st[:, j__, :],
                                                  g128c[:, j__:j__ + 1],
                                                  pb[4 + par_][:, 256:320],
                                                  ALU.mult, ALU.add),
                           inc="dv")
                ev_par_post[par] = op(
                    "vector",
                    lambda v, j__=j_: v.tensor_copy(sbf[:, j__, :],
                                                    st[:, j__, :]),
                    [ev_su], inc="dv")
                ev_par_sq[par] = op(
                    "scalar",
                    lambda s, j__=j_, yy=y_, m_=mglob, par_=par:
                    s.activation(trash[par_], yy[:, 0:128], AF.Square,
                                 accum_out=stats[:, 512 + j__ * 64 + m_:
                                                 512 + j__ * 64 + m_ + 1]),
                    [ev_ye], inc="ac")

            ev_ta_l = [None] * (NU + 1)
            ev_pre_l = [None] * (NU + 1)
            ev_ta_l[0] = emit_TA(0)
            ev_pre_l[0] = emit_pre(0, ev_ta_l[0])
            for u in range(NU):
                if u + 1 < NU:
                    ev_ta_l[u + 1] = emit_TA(u + 1)
                ev_yu = emit_YU(u, ev_pre_l[u])
                if u + 1 < NU:
                    ev_pre_l[u + 1] = emit_pre(u + 1, ev_ta_l[u + 1])
                emit_post(u, ev_yu, 8 * k + unit_list[u][0])

            ev_yb_dma = dma("sync", yhbm3[:, :, k * 1024:(k + 1) * 1024], yb,
                            waits=[ev("dv")])

        ev_wo = dma("gpsimd", wos, wo_d.rearrange("(a p) f -> p a f", p=128),
                    waits=[ev("pe")])

        op("vector",
           lambda v: v.tensor_reduce(
               stk[:, 0:8],
               stats[:, 0:512].rearrange("p (a f) -> p a f", a=8),
               AX.X, ALU.add),
           [ev("dv"), ev("ac")])
        ev_red = op("vector",
                    lambda v: v.tensor_reduce(
                        stk[:, 8:16],
                        stats[:, 512:1024].rearrange("p (a f) -> p a f", a=8),
                        AX.X, ALU.add),
                    inc="dv")
        ev_gs = op("tensor",
                   lambda t: t.matmul(pb[4][0:16, 0:2], stk, halfo,
                                      start=True, stop=True),
                   [ev_red], inc="pe")
        ev_sc = op("vector", lambda v: v.tensor_copy(ssum, pb[4][0:16, 0:2]),
                   [ev_gs], inc="dv")
        ev_sr = dma("sync", srow.rearrange("q (a f) -> q a f", a=16), ssum,
                    waits=[ev_sc])
        ev_g1 = op("vector",
           lambda v: v.tensor_scalar(srow, srow, 1.0 / GN_N, None, ALU.mult),
           [ev_sr], inc="dv")
        ev_g2 = op("vector",
           lambda v: v.tensor_tensor(gsc[0:1, 0:16], srow[0:1, 0:16],
                                     srow[0:1, 0:16], ALU.mult),
           [ev_g1], inc="dv")
        ev_var = op("vector",
                    lambda v: v.tensor_tensor(gsc[0:1, 16:32],
                                              srow[0:1, 16:32],
                                              gsc[0:1, 0:16], ALU.subtract),
                    [ev_g2], inc="dv")
        ev_gln = op("scalar",
           lambda s: s.activation(gsc[0:1, 32:48], gsc[0:1, 16:32], AF.Ln,
                                  bias=epsc[0:1, :]),
           [ev_var], inc="ac")
        ev_rstd = op("scalar",
                     lambda s: s.activation(gsc[0:1, 48:64], gsc[0:1, 32:48],
                                            AF.Exp, scale=-0.5),
                     [ev_gln], inc="ac")
        exp_srcs = [gsc[0:1, 48:64][:, 0:16:2], gsc[0:1, 48:64][:, 1:16:2],
                    srow[0:1, 0:16][:, 0:16:2], srow[0:1, 0:16][:, 1:16:2]]
        for i_, src_ in enumerate(exp_srcs):
            op("tensor",
               lambda t, i__=i_, s_=src_:
               t.matmul(pb[5][:, 328 + i__ * 8:336 + i__ * 8], ones1, s_,
                        start=True, stop=True),
               [ev_rstd] if i_ == 0 else [],
               inc="pe" if i_ == 3 else None)
        ev_expmm = ev("pe")
        op("vector",
           lambda v: v.tensor_copy(rexp[0:64, :], pb[5][0:64, 328:336]),
           [ev_expmm])
        op("vector",
           lambda v: v.tensor_copy(rexp[64:128, :], pb[5][64:128, 336:344]))
        op("vector",
           lambda v: v.tensor_copy(mexp[0:64, :], pb[5][0:64, 344:352]))
        ev_e4 = op("vector",
                   lambda v: v.tensor_copy(mexp[64:128, :],
                                           pb[5][64:128, 352:360]),
                   inc="dv")
        ev_a1 = op("vector", lambda v: v.tensor_tensor(spd, gnw, rexp,
                                                       ALU.mult),
                   [ev_e4], inc="dv")
        ev_a2 = op("vector", lambda v: v.tensor_tensor(gsc[:, 64:72], mexp,
                                                       spd, ALU.mult),
                   [ev_a1], inc="dv")
        ev_aff = op("vector",
                    lambda v: v.tensor_tensor(bpd, gnb, gsc[:, 64:72],
                                              ALU.subtract),
                    [ev_a2], inc="dv")

        def dyn_read(g):
            r = g.alloc_register("roff")
            g.reg_load(r, sloff_sb[0:1, 0:1])
            src = dataclasses.replace(yhbm3[:, :, 0:SL], offset=r)
            return g.dma_start(out=ysl, in_=src)

        ev_ysl = op("gpsimd", dyn_read, [ev_consts, ev_yb_dma], inc="dg")

        for dt_i in range(8):
            op("scalar",
               lambda s, d_=dt_i:
               s.activation(yn[:, d_, :], ysl[:, d_, :], AF.Identity,
                            bias=bpd[:, d_:d_ + 1], scale=spd[:, d_:d_ + 1]),
               [ev_ysl, ev_aff] if dt_i == 0 else [],
               inc="ac" if dt_i == 7 else None)
        ev_yn_all = ev("ac")

        xsl3 = xsl.rearrange("(a p) d -> p a d", p=128)
        out3 = out_d.rearrange("(a p) d -> p a d", p=128)
        ev_xslt = [None, None]
        ev_rb = [None, None]
        ev_x1_prev = [None, None]
        ev_x1tb_cp = None
        ev_tb_prev = None
        ev_strip_cp = None
        ev_glog = None
        for m in range(16):
            mp = m % 2
            w_x = [ev_rb[mp]] if ev_rb[mp] else []
            ev_xslt[mp] = dma("sync", xslt[mp], xsl3[:, m, :], waits=w_x)
            bank0, bank1 = (pb[0], pb[1]) if mp == 0 else (pb[2], pb[3])
            w_m = []
            if m == 0:
                w_m = [ev_yn_all, ev_wo, ev("dv")]
            if ev_rb[mp]:
                w_m.append(ev_rb[mp])
            for kk in range(8):
                st_, sp_ = kk == 0, kk == 7
                op("tensor",
                   lambda t, kk_=kk, m_=m, b_=bank0, st__=st_, sp__=sp_:
                   t.matmul(b_[:, :], yn[:, kk_, m_ * 128:(m_ + 1) * 128],
                            wos[:, kk_, 0:512], start=st__, stop=sp__),
                   w_m if kk == 0 else [])
                op("tensor",
                   lambda t, kk_=kk, m_=m, b_=bank1, st__=st_, sp__=sp_:
                   t.matmul(b_[:, :], yn[:, kk_, m_ * 128:(m_ + 1) * 128],
                            wos[:, kk_, 512:1024], start=st__, stop=sp__),
                   inc="pe" if kk == 7 else None)
            ev_oproj = ev("pe")
            w_r = [ev_oproj, ev_xslt[mp]]
            if ev_x1_prev[mp]:
                w_r.append(ev_x1_prev[mp])
            op("vector",
               lambda v, b_=bank0, mp_=mp:
               v.tensor_tensor(rbuf[mp_][:, 0:512], b_[:, :],
                               xslt[mp_][:, 0:512], ALU.add),
               w_r)
            ev_ra = op("vector",
                       lambda v, b_=bank1, mp_=mp:
                       v.tensor_tensor(rbuf[mp_][:, 512:1024], b_[:, :],
                                       xslt[mp_][:, 512:1024], ALU.add),
                       inc="dv")
            op("vector",
               lambda v, mp_=mp: v.bn_stats(bn6[:, 0:6], rbuf[mp_][:, 0:512]),
               [ev_ra])
            ev_bs = op("vector",
                       lambda v, mp_=mp: v.bn_stats(bn6[:, 6:12],
                                                    rbuf[mp_][:, 512:1024]),
                       inc="dv")
            ev_bn = op("vector", lambda v: v.bn_aggr(bn2, bn6), [ev_bs],
                       inc="dv")
            ev_rb[mp] = ev_bn
            ev_l1 = op("scalar",
                       lambda s: s.activation(lncol[:, 0:1], bn2[:, 1:2],
                                              AF.Ln, bias=epsc),
                       [ev_bn], inc="ac")
            ev_ln = op("scalar",
                       lambda s: s.activation(lncol[:, 1:2], lncol[:, 0:1],
                                              AF.Exp, scale=-0.5),
                       [ev_l1], inc="ac")
            ev_nmr = op("vector",
                        lambda v:
                        v.scalar_tensor_tensor(lncol[:, 2:3], bn2[:, 0:1],
                                               -1.0, lncol[:, 1:2],
                                               ALU.mult, ALU.mult),
                        [ev_ln], inc="dv")
            ev_x1 = op("scalar",
                       lambda s, m_=m, mp_=mp:
                       s.activation(acc[:, m_, :], rbuf[mp_], AF.Identity,
                                    bias=lncol[:, 2:3], scale=lncol[:, 1:2]),
                       [ev_nmr], inc="ac")
            ev_x1_prev[mp] = ev_x1
            w_c = [ev_x1]
            if ev_tb_prev:
                w_c.append(ev_tb_prev)
            ev_c = op("vector",
                      lambda v, m_=m: v.tensor_copy(x1bscr, acc[:, m_, :]),
                      w_c, inc="dv")
            for dt_i in range(8):
                w_t = []
                if dt_i == 0:
                    w_t = [ev_c]
                    if ev_x1tb_cp:
                        w_t.append(ev_x1tb_cp)
                op("tensor",
                   lambda t, d_=dt_i:
                   t.transpose(pb_bf[0][:, d_ * 128:(d_ + 1) * 128],
                               x1bscr[:, d_ * 128:(d_ + 1) * 128], idnb),
                   w_t, inc="pe" if dt_i == 7 else None)
            ev_tb_prev = ev("pe")
            ev_x1tb_cp = op(
                "vector",
                lambda v, m_=m:
                v.tensor_copy(x1tb[:, :, m_ * 128:(m_ + 1) * 128],
                              pb_bf[0].rearrange("p (a f) -> p a f", a=8)),
                [ev_tb_prev], inc="dv")
            for dt_i in range(8):
                bsel = pb[4] if dt_i < 4 else pb[5]
                w_t = []
                if dt_i == 0:
                    w_t = [ev_x1]
                    if ev_strip_cp:
                        w_t.append(ev_strip_cp)
                    if ev_glog:
                        w_t.append(ev_glog)
                op("tensor",
                   lambda t, m_=m, d_=dt_i, b_=bsel:
                   t.transpose(b_[:, (d_ % 4) * 128:(d_ % 4 + 1) * 128],
                               acc[:, m_, d_ * 128:(d_ + 1) * 128], idnf),
                   w_t, inc="pe" if dt_i == 7 else None)
            ev_tf = ev("pe")
            op("vector",
               lambda v: v.tensor_copy(
                   x1s[:, 0:4, :].rearrange("p a f -> p (a f)"), pb[4][:, :]),
               [ev_tf])
            ev_strip_cp = op(
                "vector",
                lambda v: v.tensor_copy(
                    x1s[:, 4:8, :].rearrange("p a f -> p (a f)"), pb[5][:, :]),
                inc="dv")
            for kk in range(8):
                op("tensor",
                   lambda t, kk_=kk, st__=(kk == 0), sp__=(kk == 7):
                   t.matmul(pb[4][0:128, 320:328], x1s[:, kk_, :],
                            gw_sb[:, kk_, :], start=st__, stop=sp__),
                   [ev_strip_cp] if kk == 0 else [],
                   inc="pe" if kk == 7 else None)
            ev_gmm = ev("pe")
            ev_glog = op("vector",
                         lambda v, m_=m:
                         v.tensor_copy(glog[:, m_ * 8:(m_ + 1) * 8],
                                       pb[4][:, 320:328]),
                         [ev_gmm], inc="dv")

        g3 = glog.rearrange("p (a f) -> p a f", a=16)

        def bc16(ap_):
            return ap_.rearrange("p (a f) -> p a f", a=16) \
                .broadcast_to([128, 16, 8])

        def t3(ap_):
            return ap_.rearrange("p (a f) -> p a f", a=16)

        def chain_v(fn, waits=()):
            return op("vector", fn, waits, inc="dv")

        e1 = chain_v(lambda v: v.tensor_reduce(gsc[:, 0:16], g3, AX.X,
                                               ALU.max), [ev_glog])
        ev_sub = chain_v(lambda v: v.tensor_tensor(g3, g3, bc16(gsc[:, 0:16]),
                                                   ALU.subtract), [e1])
        ev_exp = op("scalar", lambda s: s.activation(glog, glog, AF.Exp),
                    [ev_sub], inc="ac")
        e2 = chain_v(lambda v: v.tensor_reduce(gsc[:, 0:16], g3, AX.X,
                                               ALU.add), [ev_exp])
        e3 = chain_v(lambda v: v.reciprocal(gsc[:, 0:16], gsc[:, 0:16]), [e2])
        e4 = chain_v(lambda v: v.tensor_tensor(g3, g3, bc16(gsc[:, 0:16]),
                                               ALU.mult), [e3])
        e5 = chain_v(lambda v: v.tensor_tensor(
            t3(gsc[:, 16:144]), g3,
            ebias.rearrange("p (a f) -> p a f", a=1)
            .broadcast_to([128, 16, 8]), ALU.add), [e4])
        sb3 = gsc[:, 16:144].rearrange("p (a f) -> p a f", a=16)
        e6 = chain_v(lambda v: v.tensor_reduce(gsc[:, 0:16], sb3, AX.X,
                                               ALU.max), [e5])
        e7 = chain_v(lambda v: v.tensor_tensor(t3(gsc[:, 144:272]),
                                               t3(gsc[:, 16:144]),
                                               bc16(gsc[:, 0:16]), ALU.is_ge),
                     [e6])
        e8 = chain_v(lambda v: v.scalar_tensor_tensor(
            gsc[:, 144:272], gsc[:, 144:272], 1e30, gsc[:, 16:144],
            ALU.mult, ALU.subtract), [e7])
        e9 = chain_v(lambda v: v.tensor_reduce(
            gsc[:, 0:16],
            gsc[:, 144:272].rearrange("p (a f) -> p a f", a=16),
            AX.X, ALU.min), [e8])
        e10 = chain_v(lambda v: v.tensor_scalar(gsc[:, 0:16], gsc[:, 0:16],
                                                -1.0, None, ALU.mult), [e9])
        e11 = chain_v(lambda v: v.tensor_tensor(t3(gsc[:, 144:272]),
                                                t3(gsc[:, 16:144]),
                                                bc16(gsc[:, 0:16]),
                                                ALU.is_ge), [e10])
        ev_wsel = chain_v(lambda v: v.tensor_tensor(wsel, glog,
                                                    gsc[:, 144:272],
                                                    ALU.mult), [e11])
        ev_wt8 = None
        for m in range(16):
            op("tensor",
               lambda t, m_=m:
               t.transpose(pb[4][0:8, 328:456],
                           wsel[:, m_ * 8:(m_ + 1) * 8], idnf),
               [ev_wsel] if m == 0 else [ev_wt8],
               inc="pe")
            ev_tw = ev("pe")
            ev_wt8 = op("vector",
                        lambda v, m_=m:
                        v.tensor_copy(wt8[:, m_ * 128:(m_ + 1) * 128],
                                      pb[4][0:8, 328:456]),
                        [ev_tw], inc="dv")

        pairs3 = [(pb[0], pb[1]), (pb[2], pb[3]), (pb[4], pb[5])]

        def w13_src(wd, e, q):
            return wd[e:e + 1, :, HQ_OFF[q]:HQ_OFF[q] + HQ_TILES[q] * 128] \
                .rearrange("e (a p) h -> p (e a) h", p=128)

        def w2_src(e, q):
            return w2_d[e:e + 1, HQ_OFF[q]:HQ_OFF[q] + HQ_TILES[q] * 128, :] \
                .rearrange("e (a p) d -> p (e a) d", p=128)

        fetch_q = [(e, q) for e in range(NE) for q in range(4)]
        ev_fA, ev_fB, ev_fC = {}, {}, {}
        e0, q0 = fetch_q[0]
        ev_fA[(e0, q0)] = dma("gpsimd", slA(q0), w13_src(w1_d, e0, q0),
                              waits=[ev("pe")])
        ev_fB[(e0, q0)] = dma("gpsimd", slB(q0), w13_src(w3_d, e0, q0))
        ev_fC[(e0, q0)] = dma("gpsimd", slC(q0), w2_src(e0, q0))
        fetch_idx = 1

        ev_wbmm_prev = None
        ev_acc_upd = [None, None]
        last_h1_mm = None
        last_h3_mm = None
        for e in range(NE):
            w_wr = [ev_wt8] if e == 0 else []
            if ev_wbmm_prev:
                w_wr.append(ev_wbmm_prev)
            ev_wr = dma("sync", wrow, wt8[e:e + 1, :], waits=w_wr)
            for nchunk in range(4):
                op("tensor",
                   lambda t, n_=nchunk:
                   t.matmul(pb[n_][:, :], ones1,
                            wrow[0:1, n_ * 512:(n_ + 1) * 512],
                            start=True, stop=True),
                   [ev_wr, ev("dv")] if nchunk == 0 else [],
                   inc="pe" if nchunk == 3 else None)
            ev_wbmm_prev = ev("pe")
            ev_wbc = None
            for nchunk in range(4):
                ev_wbc = op("vector",
                            lambda v, n_=nchunk:
                            v.tensor_copy(wbc[:, n_ * 512:(n_ + 1) * 512],
                                          pb[n_][:, :]),
                            [ev_wbmm_prev] if nchunk == 0 else [], inc="dv")
            job = 0
            job_ev = [ev_wbc, ev_wbc, ev_wbc]
            for q in range(4):
                nt = HQ_TILES[q]
                ev_s = None
                h1_pair = None
                for tcp in range(2):
                    for hp in range(nt):
                        for mat in range(2):
                            pr = pairs3[job % 3]
                            w_j = []
                            if job_ev[job % 3] is not None:
                                w_j.append(job_ev[job % 3])
                            if hp == 0 and tcp == 0 and mat == 0:
                                w_j += [ev_fA[(e, q)], ev_fB[(e, q)], ev_wbc]
                            src = (slA(q) if mat == 0 else slB(q))
                            for kk in range(8):
                                st_, sp_ = kk == 0, kk == 7
                                op("tensor",
                                   lambda t, kk_=kk, h_=hp, tc_=tcp, s_=src,
                                          b_=pr[0], st__=st_, sp__=sp_:
                                   t.matmul(b_[:, :],
                                            s_[:, kk_, h_ * 128:(h_ + 1) * 128],
                                            x1tb[:, kk_,
                                                 tc_ * 1024:tc_ * 1024 + 512],
                                            start=st__, stop=sp__),
                                   w_j if kk == 0 else [])
                                op("tensor",
                                   lambda t, kk_=kk, h_=hp, tc_=tcp, s_=src,
                                          b_=pr[1], st__=st_, sp__=sp_:
                                   t.matmul(b_[:, :],
                                            s_[:, kk_, h_ * 128:(h_ + 1) * 128],
                                            x1tb[:, kk_, tc_ * 1024 + 512:
                                                 (tc_ + 1) * 1024],
                                            start=st__, stop=sp__),
                                   inc="pe" if kk == 7 else None)
                            if mat == 0:
                                h1_pair = pr
                                h1_job = job
                                last_h1_mm = ev("pe")
                            else:
                                last_h3_mm = ev("pe")
                                hb = tcp
                                w_h1 = [last_h3_mm]
                                if job <= 2:
                                    w_h1.append(ev_wbc)
                                op("vector",
                                   lambda v, b_=h1_pair[0], hb_=hb, tc_=tcp:
                                   v.tensor_tensor(
                                       h1w[hb_][:, 0:512], b_[:, :],
                                       wbc[:, tc_ * 1024:tc_ * 1024 + 512],
                                       ALU.mult),
                                   w_h1)
                                ev_h1w = op(
                                    "vector",
                                    lambda v, b_=h1_pair[1], hb_=hb, tc_=tcp:
                                    v.tensor_tensor(
                                        h1w[hb_][:, 512:1024], b_[:, :],
                                        wbc[:, tc_ * 1024 + 512:
                                            (tc_ + 1) * 1024], ALU.mult),
                                    inc="dv")
                                ev_t1 = op(
                                    "scalar",
                                    lambda s, hb_=hb:
                                    s.activation(t1[hb_], h1w[hb_], AF.Silu),
                                    [ev_h1w], inc="ac")
                                op("vector",
                                   lambda v, b_=pr[0], hb_=hb, h_=hp, tc_=tcp:
                                   v.tensor_tensor(
                                       smoe[:, h_,
                                            tc_ * 1024:tc_ * 1024 + 512],
                                       t1[hb_][:, 0:512], b_[:, :], ALU.mult),
                                   [ev_t1])
                                ev_s = op(
                                    "vector",
                                    lambda v, b_=pr[1], hb_=hb, h_=hp, tc_=tcp:
                                    v.tensor_tensor(
                                        smoe[:, h_, tc_ * 1024 + 512:
                                             (tc_ + 1) * 1024],
                                        t1[hb_][:, 512:1024], b_[:, :],
                                        ALU.mult),
                                    inc="dv")
                                job_ev[h1_job % 3] = ev_h1w
                                job_ev[job % 3] = ev_s
                            job += 1
                for m in range(16):
                    mp = m % 2
                    pr2 = (pb[0], pb[1]) if mp == 0 else (pb[2], pb[3])
                    w_m = []
                    if m == 0:
                        w_m = [ev_s, ev_fC[(e, q)], ev("dv")]
                    if ev_acc_upd[mp] is not None:
                        w_m.append(ev_acc_upd[mp])
                    for kk in range(nt):
                        st_, sp_ = kk == 0, kk == nt - 1
                        op("tensor",
                           lambda t, kk_=kk, m_=m, b_=pr2[0], q_=q,
                                  st__=st_, sp__=sp_:
                           t.matmul(b_[:, :],
                                    smoe[:, kk_, m_ * 128:(m_ + 1) * 128],
                                    slC(q_)[:, kk_, 0:512],
                                    start=st__, stop=sp__),
                           w_m if kk == 0 else [])
                        op("tensor",
                           lambda t, kk_=kk, m_=m, b_=pr2[1], q_=q,
                                  st__=st_, sp__=sp_:
                           t.matmul(b_[:, :],
                                    smoe[:, kk_, m_ * 128:(m_ + 1) * 128],
                                    slC(q_)[:, kk_, 512:1024],
                                    start=st__, stop=sp__),
                           inc="pe" if kk == nt - 1 else None)
                    ev_o2 = ev("pe")
                    w_st = [ev_o2]
                    if ev_acc_upd[mp] is not None:
                        w_st.append(ev_acc_upd[mp])
                    op("vector",
                       lambda v, b_=pr2[0], m_=m, e_=e:
                       v.scalar_tensor_tensor(
                           acc[:, m_, 0:512], b_[:, :],
                           wsel[:, m_ * 8 + e_:m_ * 8 + e_ + 1],
                           acc[:, m_, 0:512], ALU.mult, ALU.add),
                       w_st)
                    ev_acc_upd[mp] = op(
                        "vector",
                        lambda v, b_=pr2[1], m_=m, e_=e:
                        v.scalar_tensor_tensor(
                            acc[:, m_, 512:1024], b_[:, :],
                            wsel[:, m_ * 8 + e_:m_ * 8 + e_ + 1],
                            acc[:, m_, 512:1024], ALU.mult, ALU.add),
                        inc="dv")
                if fetch_idx < len(fetch_q):
                    en, qn = fetch_q[fetch_idx]
                    ev_fA[(en, qn)] = dma("gpsimd", slA(qn),
                                          w13_src(w1_d, en, qn),
                                          waits=[last_h1_mm])
                    ev_fB[(en, qn)] = dma("gpsimd", slB(qn),
                                          w13_src(w3_d, en, qn),
                                          waits=[last_h3_mm])
                    ev_fC[(en, qn)] = dma("gpsimd", slC(qn), w2_src(en, qn),
                                          waits=[ev("pe")])
                    fetch_idx += 1
                job_ev = [ev("dv")] * 3

        ev_out_dma = [None, None]
        for m in range(16):
            mp = m % 2
            w_m = [ev_acc_upd[0], ev_acc_upd[1]] if m == 0 else []
            op("vector",
               lambda v, m_=m: v.bn_stats(bn6[:, 0:6], acc[:, m_, 0:512]),
               w_m)
            ev_bs = op("vector",
                       lambda v, m_=m: v.bn_stats(bn6[:, 6:12],
                                                  acc[:, m_, 512:1024]),
                       inc="dv")
            ev_bn = op("vector", lambda v: v.bn_aggr(bn2, bn6), [ev_bs],
                       inc="dv")
            ev_l1 = op("scalar",
                       lambda s: s.activation(lncol[:, 0:1], bn2[:, 1:2],
                                              AF.Ln, bias=epsc),
                       [ev_bn], inc="ac")
            ev_ln = op("scalar",
                       lambda s: s.activation(lncol[:, 1:2], lncol[:, 0:1],
                                              AF.Exp, scale=-0.5),
                       [ev_l1], inc="ac")
            ev_nmr = op("vector",
                        lambda v:
                        v.scalar_tensor_tensor(lncol[:, 2:3], bn2[:, 0:1],
                                               -1.0, lncol[:, 1:2],
                                               ALU.mult, ALU.mult),
                        [ev_ln], inc="dv")
            w_o = [ev_nmr]
            if ev_out_dma[mp]:
                w_o.append(ev_out_dma[mp])
            ev_o = op("scalar",
                      lambda s, m_=m, mp_=mp:
                      s.activation(outs[mp_], acc[:, m_, :], AF.Identity,
                                   bias=lncol[:, 2:3], scale=lncol[:, 1:2]),
                      w_o, inc="ac")
            ev_out_dma[mp] = dma("sync", out3[:, m, :], outs[mp],
                                 waits=[ev_o])

        if cutoff is not None:
            for e_ in p.q:
                p.q[e_] = [o_ for o_ in p.q[e_] if o_[4] <= cutoff]
        final_ds = sum(o_[3] for o_ in p.q["sync"] if o_[2] == "ds")

        dma_waited = {"ds": set(), "dg": set()}
        for q_ in p.q.values():
            for (waits, _f, _i, _n, _sq) in q_:
                for (s_n, val) in waits:
                    if s_n in dma_waited:
                        dma_waited[s_n].add(val)
        dma_waited = {k: sorted(v) for k, v in dma_waited.items()}

        def body(name):
            own = {"sync": "ds", "gpsimd": "dg"}.get(name)
            def _(eng):
                seen = {}
                cnt = 0
                for (waits, fn, inc, n, _sq) in p.q[name]:
                    if own is not None and inc == own:
                        need = 0
                        for v_ in dma_waited[own]:
                            if v_ <= cnt:
                                need = v_
                            else:
                                break
                        if need > seen.get(own, -1):
                            eng.wait_ge(sems[own], need)
                            seen[own] = need
                        cnt += n
                    for (s_n, val) in waits:
                        if seen.get(s_n, -1) >= val:
                            continue
                        eng.wait_ge(sems[s_n], val)
                        seen[s_n] = val
                    instr = fn(eng)
                    if inc is not None:
                        instr.then_inc(sems[inc], n)
                if name == "sync":
                    eng.wait_ge(sems["ds"], final_ds)
            return _

        block.sync(body("sync"))
        block.gpsimd(body("gpsimd"))
        block.tensor(body("tensor"))
        block.vector(body("vector"))
        block.scalar(body("scalar"))
    return nc


_NC_CACHE = None


def kernel(L0_attn_x, L0_attn_W_q, L0_attn_W_k, L0_attn_W_v, L0_attn_W_o,
           L0_attn_gn_weight, L0_attn_gn_bias, L0_ffn_gate_w,
           L0_ffn_expert_bias, L0_ffn_w1, L0_ffn_w2, L0_ffn_w3):
    global _NC_CACHE
    if _NC_CACHE is None:
        _NC_CACHE = build_program()
    nc = _NC_CACHE

    x = np.asarray(L0_attn_x, np.float32)
    wq = np.ascontiguousarray(np.asarray(L0_attn_W_q, np.float32).T)
    wk = np.ascontiguousarray(np.asarray(L0_attn_W_k, np.float32).T)
    wv = np.ascontiguousarray(np.asarray(L0_attn_W_v, np.float32).T)
    wo = np.ascontiguousarray(np.asarray(L0_attn_W_o, np.float32).T)
    gw = np.ascontiguousarray(np.asarray(L0_ffn_gate_w, np.float32).T)
    pconsts = np.zeros((128, 24), np.float32)
    pconsts[:, 0:8] = np.asarray(L0_attn_gn_weight, np.float32).reshape(8, 128).T
    pconsts[:, 8:16] = np.asarray(L0_attn_gn_bias, np.float32).reshape(8, 128).T
    pconsts[:, 16:24] = np.broadcast_to(
        np.asarray(L0_ffn_expert_bias, np.float32), (128, 8))
    w1 = np.ascontiguousarray(np.asarray(L0_ffn_w1, np.float32))
    w2 = np.ascontiguousarray(np.asarray(L0_ffn_w2, np.float32))
    w3 = np.ascontiguousarray(np.asarray(L0_ffn_w3, np.float32))

    in_maps = []
    for c in range(NCORES):
        b, s = c // 4, c % 4
        in_maps.append({
            "xb": np.ascontiguousarray(x[b]),
            "xsl": np.ascontiguousarray(x[b, s * SL:(s + 1) * SL]),
            "sloff": np.array([[s * SL, 0, 0, 0]], np.int32),
            "wq": wq, "wk": wk, "wv": wv, "wo": wo, "gw": gw,
            "pconsts": pconsts, "w1": w1, "w2": w2, "w3": w3,
        })
    res = run_bass_kernel_spmd(nc, in_maps, core_ids=list(range(NCORES)))
    out = np.empty((B, T, D), np.float32)
    for c in range(NCORES):
        b, s = c // 4, c % 4
        out[b, s * SL:(s + 1) * SL] = res.results[c]["out"]
    return out


# version 06
# speedup vs baseline: 1.0161x; 1.0161x over previous
import dataclasses
from contextlib import ExitStack

import numpy as np
import ml_dtypes

import concourse.bass as bass
import concourse.mybir as mybir
from concourse.bass_utils import run_bass_kernel_spmd

F32 = mybir.dt.float32
BF16 = mybir.dt.bfloat16
I32 = mybir.dt.int32
AF = mybir.ActivationFunctionType
ALU = mybir.AluOpType
AX = mybir.AxisListType

B, T, D = 2, 8192, 1024
H, DH, NPAIR = 16, 64, 8
NBLK, CH = 8, 128
NCORES, SL = 8, 2048
HIDDEN, NE = 2816, 8
HQ_TILES = [6, 5, 6, 5]
HQ_OFF = [0, 768, 1408, 2176]
GN_N = float(DH * T)
EPS = 1e-5

O_IDNB, O_IDNF = 0, 256
O_GNW, O_GNB, O_EB = 768, 800, 832
O_ONES1 = 864
O_G128, O_HALF = 1376, 1408
O_EPSC = 1424
O_SPD, O_BPD, O_REXP, O_MEXP = 1440, 1472, 1504, 1536
O_SLOFF, O_SROW, O_SSUM, O_STK = 1568, 1600, 1728, 1760
O_WSEL = 2048
O_RBUF = 2560
O_WROW = 10752
O_GLOG, O_GSC, O_X1S = 18944, 19456, 20992
O_DT, O_GROW, O_GKT = 25088, 33280, 37376
O_S, O_SBF, O_STATS = 41472, 43520, 44544
O_KG, O_QG, O_AH, O_TRASH = 52736, 53248, 53760, 54784
O_XBLK, O_XT, O_QT, O_KT = 55040, 71424, 87808, 104192
O_VT, O_YB, O_WQ, O_WK, O_WV = 120576, 136960, 153344, 169728, 186112
O_ACC, O_X1TB, O_YSL, O_YN, O_WO = 25088, 90624, 123392, 156160, 188928
O_SMOE, O_WBC = 123392, 147968
O_SLA, O_SLB, O_SLC = 156160, 168448, 180736
O_H1W, O_T1 = 193024, 201216
O_OUTS = 123392
ARENA = 209408

_BF = ml_dtypes.bfloat16


def _tables():
    g = 1.0 - 2.0 ** (-5.0 - np.arange(H, dtype=np.float64))
    i = np.arange(CH, dtype=np.float64)
    diff = i[None, :] - i[:, None]
    dt = np.zeros((CH, NPAIR * 2 * CH), np.float64)
    grow = np.zeros((CH, NPAIR * CH), np.float64)
    gkt = np.zeros((CH, NPAIR * CH), np.float64)
    g128 = np.zeros((CH, NPAIR), np.float64)
    for p_ in range(NPAIR):
        for half in range(2):
            h = 2 * p_ + half
            m = np.where(diff >= 0, g[h] ** np.maximum(diff, 0.0), 0.0)
            dt[:, p_ * 256 + half * 128:p_ * 256 + half * 128 + 128] = m
            rows = slice(half * 64, half * 64 + 64)
            grow[rows, p_ * 128:(p_ + 1) * 128] = (g[h] ** (i + 1.0))[None, :]
            gkt[:, p_ * 128 + half * 64:p_ * 128 + half * 64 + 64] = \
                (g[h] ** (127.0 - i))[:, None]
            g128[rows, p_] = g[h] ** 128.0
    return (dt.astype(np.float32), grow.astype(np.float32),
            gkt.astype(np.float32), g128.astype(np.float32))


class Prog:
    ENGS = ("sync", "gpsimd", "tensor", "vector", "scalar")

    def __init__(self):
        self.q = {e: [] for e in self.ENGS}
        self.cnt = {"ds": 0, "dg": 0, "pe": 0, "dv": 0, "ac": 0, "gp": 0}

    def op(self, eng, fn, waits=(), inc=None):
        n = 16 if inc in ("ds", "dg") else 1
        w = [(s, v) for (s, v) in waits if s is not None and v > 0]
        self.seq = getattr(self, "seq", 0) + 1
        self.q[eng].append((w, fn, inc, n, self.seq))
        if inc is not None:
            self.cnt[inc] += n
            return (inc, self.cnt[inc])
        return None

    def ev(self, sem):
        return (sem, self.cnt[sem])


def build_program(cutoff=None):
    nc = bass.Bass("TRN2", target_bir_lowering=False)

    xb = nc.dram_tensor("xb", [T, D], F32, kind="ExternalInput").ap()
    xsl = nc.dram_tensor("xsl", [SL, D], F32, kind="ExternalInput").ap()
    sloff = nc.dram_tensor("sloff", [1, 4], I32, kind="ExternalInput").ap()
    wq_d = nc.dram_tensor("wq", [D, D], F32, kind="ExternalInput").ap()
    wk_d = nc.dram_tensor("wk", [D, D], F32, kind="ExternalInput").ap()
    wv_d = nc.dram_tensor("wv", [D, D], F32, kind="ExternalInput").ap()
    wo_d = nc.dram_tensor("wo", [D, D], F32, kind="ExternalInput").ap()
    gw_d = nc.dram_tensor("gw", [D, NE], F32, kind="ExternalInput").ap()
    pc_d = nc.dram_tensor("pconsts", [128, 24], F32, kind="ExternalInput").ap()
    w1_d = nc.dram_tensor("w1", [NE, D, HIDDEN], F32, kind="ExternalInput").ap()
    w2_d = nc.dram_tensor("w2", [NE, HIDDEN, D], F32, kind="ExternalInput").ap()
    w3_d = nc.dram_tensor("w3", [NE, D, HIDDEN], F32, kind="ExternalInput").ap()
    out_d = nc.dram_tensor("out", [SL, D], F32, kind="ExternalOutput").ap()
    yhbm = nc.dram_tensor("yhbm", [128, 8 * T], BF16).ap()

    dt_np, grow_np, gkt_np, g128_np = _tables()
    c_dt = nc.inline_tensor(dt_np, name="c_dt").ap()
    c_grow = nc.inline_tensor(grow_np, name="c_grow").ap()
    c_gkt = nc.inline_tensor(gkt_np, name="c_gkt").ap()
    c_g128 = nc.inline_tensor(g128_np, name="c_g128").ap()
    c_idnb = nc.inline_tensor(np.eye(128).astype(_BF), name="c_idnb").ap()
    c_idnf = nc.inline_tensor(np.eye(128, dtype=np.float32), name="c_idnf").ap()
    half_np = np.zeros((128, 2), np.float32)
    half_np[0:64, 0] = 1.0
    half_np[64:128, 1] = 1.0
    c_half = nc.inline_tensor(half_np, name="c_half").ap()

    p = Prog()
    op, ev = p.op, p.ev

    with ExitStack() as ctx:
        arena = nc.alloc_sbuf_tensor("arena", [128, ARENA], mybir.dt.uint8).ap()
        gw_sb = nc.alloc_sbuf_tensor("gwsbuf", [128, 8 * NE], F32).ap() \
            .rearrange("p (a f) -> p a f", a=8)

        def v_(off, n, dtype):
            sz = mybir.dt.size(dtype)
            return arena[:, off:off + n * sz].bitcast(dtype)

        def v3(off, n, dtype, a):
            return v_(off, n, dtype).rearrange("p (a f) -> p a f", a=a)

        idnb = v_(O_IDNB, 128, BF16)
        idnf = v_(O_IDNF, 128, F32)
        gnw = v_(O_GNW, 8, F32)
        gnb = v_(O_GNB, 8, F32)
        ebias = v_(O_EB, 8, F32)
        ones1 = v_(O_ONES1, 128, F32)[0:1, :]
        g128c = v_(O_G128, 8, F32)
        halfo = v_(O_HALF, 2, F32)
        epsc = v_(O_EPSC, 1, F32)
        spd = v_(O_SPD, 8, F32)
        bpd = v_(O_BPD, 8, F32)
        rexp = v_(O_REXP, 8, F32)
        mexp = v_(O_MEXP, 8, F32)
        sloff_sb = arena[:, O_SLOFF:O_SLOFF + 16].bitcast(I32)[0:1, 0:4]
        srow = v_(O_SROW, 32, F32)[0:1, :]
        ssum = v_(O_SSUM, 2, F32)[0:16, :]
        stk = v_(O_STK, 16, F32)
        wsel = v_(O_WSEL, 128, F32)
        rbuf = [v_(O_RBUF + i * 4096, 1024, F32) for i in range(2)]
        wt8 = v_(O_RBUF, 2048, F32)[0:8, :]
        wrow = v_(O_WROW, 2048, F32)[0:1, :]
        xslt = [v_(O_WROW + i * 4096, 1024, F32) for i in range(2)]
        glog = v_(O_GLOG, 128, F32)
        gsc = v_(O_GSC, 384, F32)
        bn6 = gsc[:, 288:300]
        bn2 = gsc[:, 304:306]
        lncol = gsc[:, 308:312]
        x1s = v3(O_X1S, 1024, F32, 8)
        dtt = v3(O_DT, 2048, F32, 8)
        growt = v3(O_GROW, 1024, F32, 8)
        gktt = v3(O_GKT, 1024, F32, 8)
        st = v3(O_S, 512, F32, 8)
        sbf = v3(O_SBF, 512, BF16, 8)
        stats = v_(O_STATS, 2048, F32)
        kg = [v_(O_KG + i * 256, 128, BF16) for i in range(2)]
        qg = [v_(O_QG + i * 256, 128, BF16) for i in range(2)]
        ah = [v_(O_AH + i * 512, 256, BF16) for i in range(2)]
        trash = [v_(O_TRASH, 128, BF16), v_(O_STATS + 4096, 128, BF16)]
        xblk = v3(O_XBLK, 8192, BF16, 8)
        xt = v3(O_XT, 8192, BF16, 8)
        qt = v3(O_QT, 8192, BF16, 8)
        kt = v3(O_KT, 8192, BF16, 8)
        vt = v3(O_VT, 8192, BF16, 8)
        yb = v3(O_YB, 8192, BF16, 8)
        wqs = v3(O_WQ, 8192, BF16, 8)
        wks = v3(O_WK, 8192, BF16, 8)
        wvs = v3(O_WV, 8192, BF16, 8)
        acc = v3(O_ACC, 16384, F32, 16)
        x1tb = v3(O_X1TB, 16384, BF16, 8)
        ysl = v3(O_YSL, 16384, BF16, 8)
        x1bscr = v_(O_YSL, 1024, BF16)
        yn = v3(O_YN, 16384, BF16, 8)
        wos = v3(O_WO, 8192, BF16, 8)
        smoe = v3(O_SMOE, 12288, BF16, 6)
        wbc = v_(O_WBC, 2048, F32)
        h1w = [v_(O_H1W + i * 4096, 1024, F32) for i in range(2)]
        t1 = [v_(O_T1 + i * 4096, 1024, F32) for i in range(2)]
        outs = [v_(O_OUTS + i * 4096, 1024, F32) for i in range(2)]

        def slA(q):
            return v3(O_SLA, 8 * HQ_TILES[q] * 128, BF16, 8)

        def slB(q):
            return v3(O_SLB, 8 * HQ_TILES[q] * 128, BF16, 8)

        def slC(q):
            return v3(O_SLC, HQ_TILES[q] * 1024, BF16, HQ_TILES[q])

        pb_bf = [ctx.enter_context(nc.psum_tensor(f"pbb{i}", [128, 1024], BF16))
                 for i in range(2)]
        pb = [ctx.enter_context(nc.psum_tensor(f"pb{i}", [128, 512], F32))
              for i in range(6)]

        sems = {s: ctx.enter_context(nc.semaphore(s))
                for s in ("ds", "dg", "pe", "dv", "ac", "gp")}
        block = ctx.enter_context(nc.Block())

        def dma(eng, out, in_, waits=()):
            return op(eng, lambda e, o=out, i=in_: e.dma_start(out=o, in_=i),
                      waits, inc="ds" if eng == "sync" else "dg")

        dma("sync", v_(O_DT, 2048, F32), c_dt)
        dma("sync", v_(O_GROW, 1024, F32), c_grow)
        dma("sync", v_(O_GKT, 1024, F32), c_gkt)
        dma("sync", g128c, c_g128)
        dma("sync", idnb, c_idnb)
        dma("sync", idnf, c_idnf)
        dma("sync", halfo, c_half)
        dma("sync", arena[:, O_GNW:O_GNW + 96].bitcast(F32), pc_d)
        dma("sync", sloff_sb, sloff)
        ev_consts = dma("sync", gw_sb, gw_d.rearrange("(a p) f -> p a f", p=128))
        for wsb, wdr in ((wqs, wq_d), (wks, wk_d), (wvs, wv_d)):
            dma("gpsimd", wsb, wdr.rearrange("(a p) f -> p a f", p=128))
        ev_wqkv = ev("dg")

        op("vector", lambda v: v.memset(v_(O_S, 512, F32), 0.0))
        op("vector", lambda v: v.memset(epsc, EPS))
        op("vector", lambda v: v.memset(v_(O_SBF, 512, BF16), 0.0))
        ev_sbzero = op("vector", lambda v: v.memset(ones1, 1.0), inc="dv")

        xb3 = xb.rearrange("(a p) d -> p a d", p=128)
        yhbm3 = yhbm.rearrange("p (a t) -> p a t", a=8)

        ev_par_post = [ev_sbzero, ev_sbzero]
        ev_par_sq = [None, None]
        sbf_ev = [None] * NPAIR
        ev_kg_last = [ev_sbzero, ev_sbzero]
        ev_yb_dma = None
        ev_projbank = {0: None, 1: None}
        ev_xtrans_done = None
        unit_list = [(c_, j_) for c_ in range(8) for j_ in range(NPAIR)]
        projs = [(wqs, qt, 0.125, False), (wks, kt, None, False),
                 (wvs, vt, None, True)]

        for k in range(NBLK):
            w_ = [ev_consts]
            if ev_xtrans_done is not None:
                w_.append(ev_xtrans_done)
            evx = dma("gpsimd", xblk, xb3[:, 8 * k:8 * k + 8, :], waits=w_)

            ev_xt_cp = [None, None]
            for tt in range(8):
                par = tt % 2
                w_t = []
                if tt == 0:
                    w_t = [evx, ev_consts]
                if tt < 2:
                    w_t.append(ev_kg_last[par])
                else:
                    w_t.append(ev_xt_cp[par])
                for dt_i in range(8):
                    op("tensor",
                       lambda t, tt_=tt, d_=dt_i, par_=par:
                       t.transpose(pb_bf[par_][:, d_ * 128:(d_ + 1) * 128],
                                   xblk[:, tt_, d_ * 128:(d_ + 1) * 128],
                                   idnb),
                       w_t if dt_i == 0 else [],
                       inc="pe" if dt_i == 7 else None)
                ev_tt = ev("pe")
                ev_xt_cp[par] = op(
                    "vector",
                    lambda v, tt_=tt, par_=par:
                    v.tensor_copy(xt[:, :, tt_ * 128:(tt_ + 1) * 128],
                                  pb_bf[par_].rearrange("p (a f) -> p a f",
                                                        a=8)),
                    [ev_tt], inc="dv")
            ev_xtrans_done = ev("pe")
            ev_xt_all = ev_xt_cp[1]

            for which, (wsb, dst, scale, tok_major) in enumerate(projs):
                for m in range(8):
                    mp = m % 2
                    bank0, bank1 = (pb[0], pb[1]) if mp == 0 else (pb[2], pb[3])
                    w_m = []
                    if which == 0 and m == 0:
                        w_m = [ev_xt_all, ev_wqkv]
                        if ev_par_sq[0] is not None:
                            w_m += [ev_par_sq[0], ev_par_sq[1]]
                        w_m.append(ev("dv"))
                    if ev_projbank[mp] is not None:
                        w_m.append(ev_projbank[mp])
                    for kk in range(8):
                        st_, sp_ = kk == 0, kk == 7
                        if tok_major:
                            l0 = lambda kk_, m_: xt[:, kk_,
                                                    m_ * 128:(m_ + 1) * 128]
                            r0 = lambda kk_, n_, w__=wsb: \
                                w__[:, kk_, n_ * 512:(n_ + 1) * 512]
                        else:
                            l0 = lambda kk_, m_, w__=wsb: \
                                w__[:, kk_, m_ * 128:(m_ + 1) * 128]
                            r0 = lambda kk_, n_: xt[:, kk_,
                                                    n_ * 512:(n_ + 1) * 512]
                        op("tensor",
                           lambda t, kk_=kk, m_=m, b_=bank0, l_=l0, r_=r0,
                                  st__=st_, sp__=sp_:
                           t.matmul(b_[:, :], l_(kk_, m_), r_(kk_, 0),
                                    start=st__, stop=sp__),
                           w_m if kk == 0 else [])
                        op("tensor",
                           lambda t, kk_=kk, m_=m, b_=bank1, l_=l0, r_=r0,
                                  st__=st_, sp__=sp_:
                           t.matmul(b_[:, :], l_(kk_, m_), r_(kk_, 1),
                                    start=st__, stop=sp__),
                           inc="pe" if kk == 7 else None)
                    ev_mm = ev("pe")

                    def evac(v, b_, dst_, m_, n_, sc_):
                        o_ = dst_[:, m_, n_ * 512:(n_ + 1) * 512]
                        if sc_ is None:
                            return v.tensor_copy(o_, b_[:, :])
                        return v.tensor_scalar(o_, b_[:, :], sc_, None,
                                               ALU.mult)
                    op("vector",
                       lambda v, b_=bank0, d_=dst, m_=m, sc_=scale:
                       evac(v, b_, d_, m_, 0, sc_), [ev_mm])
                    ev_projbank[mp] = op(
                        "vector",
                        lambda v, b_=bank1, d_=dst, m_=m, sc_=scale:
                        evac(v, b_, d_, m_, 1, sc_), inc="dv")
            ev_projevac = ev("dv")

            NU = len(unit_list)

            def emit_TA(u):
                c_, j_ = unit_list[u]
                par = u % 2
                w_ta = [ev_par_post[par], ev_kg_last[par]]
                if ev_par_sq[par] is not None:
                    w_ta.append(ev_par_sq[par])
                if u == 0:
                    w_ta.append(ev_projevac)
                op("tensor",
                   lambda t, c__=c_, j__=j_, par_=par:
                   t.transpose(pb_bf[par_][:, 0:128],
                               kt[:, j__, c__ * 128:(c__ + 1) * 128], idnb),
                   w_ta)
                op("tensor",
                   lambda t, c__=c_, j__=j_, par_=par:
                   t.matmul(pb[4 + par_][0:128, 0:128],
                            kt[0:64, j__, c__ * 128:(c__ + 1) * 128],
                            qt[0:64, j__, c__ * 128:(c__ + 1) * 128],
                            start=True, stop=True))
                return op("tensor",
                          lambda t, c__=c_, j__=j_, par_=par:
                          t.matmul(pb[2 + par_][0:128, 0:128],
                                   kt[64:128, j__, c__ * 128:(c__ + 1) * 128],
                                   qt[64:128, j__, c__ * 128:(c__ + 1) * 128],
                                   start=True, stop=True,
                                   tile_position=(64, 0)),
                          inc="pe")

            def emit_pre(u, ev_ta):
                c_, j_ = unit_list[u]
                par = u % 2
                ev_kg_last[par] = op(
                    "vector",
                    lambda v, j__=j_, par_=par:
                    v.tensor_tensor(kg[par_], pb_bf[par_][:, 0:128],
                                    gktt[:, j__, :], ALU.mult),
                    [ev_ta], inc="dv")
                op("vector",
                   lambda v, j__=j_, par_=par:
                   v.tensor_tensor(ah[par_][:, 0:128], pb[4 + par_][:, 0:128],
                                   dtt[:, j__, 0:128], ALU.mult))
                ev_pre = op("vector",
                            lambda v, j__=j_, par_=par:
                            v.tensor_tensor(ah[par_][:, 128:256],
                                            pb[2 + par_][:, 0:128],
                                            dtt[:, j__, 128:256], ALU.mult),
                            inc="dv")
                ev_qg_l[u] = op("gpsimd",
                                lambda g, c__=c_, j__=j_, par_=par:
                                g.tensor_tensor(qg[par_],
                                                qt[:, j__,
                                                   c__ * 128:(c__ + 1) * 128],
                                                growt[:, j__, :], ALU.mult),
                                [ev_ta], inc="gp")
                return ev_pre

            def emit_YU(u, ev_pre):
                c_, j_ = unit_list[u]
                par = u % 2
                y_ = pb[par]
                w_yu = [ev_pre, ev_qg_l[u]]
                if sbf_ev[j_] is not None:
                    w_yu.append(sbf_ev[j_])
                op("tensor",
                   lambda t, c__=c_, j__=j_, par_=par, yy=y_:
                   t.matmul(yy[0:64, 0:128],
                            vt[:, c__, j__ * 128:j__ * 128 + 64],
                            ah[par_][:, 0:128], start=True, stop=False),
                   w_yu)
                op("tensor",
                   lambda t, j__=j_, par_=par, yy=y_:
                   t.matmul(yy[0:64, 0:128], sbf[0:64, j__, :],
                            qg[par_][0:64, :], start=False, stop=True))
                op("tensor",
                   lambda t, c__=c_, j__=j_, par_=par, yy=y_:
                   t.matmul(yy[64:128, 0:128],
                            vt[:, c__, j__ * 128 + 64:(j__ + 1) * 128],
                            ah[par_][:, 128:256], start=True, stop=False,
                            tile_position=(0, 64)))
                op("tensor",
                   lambda t, j__=j_, par_=par, yy=y_:
                   t.matmul(yy[64:128, 0:128], sbf[64:128, j__, :],
                            qg[par_][64:128, :], start=False, stop=True,
                            tile_position=(64, 64)))
                op("tensor",
                   lambda t, c__=c_, j__=j_, par_=par:
                   t.matmul(pb[4 + par_][0:64, 256:320], kg[par_][:, 0:64],
                            vt[:, c__, j__ * 128:j__ * 128 + 64],
                            start=True, stop=True))
                return op("tensor",
                          lambda t, c__=c_, j__=j_, par_=par:
                          t.matmul(pb[4 + par_][64:128, 256:320],
                                   kg[par_][:, 64:128],
                                   vt[:, c__, j__ * 128 + 64:(j__ + 1) * 128],
                                   start=True, stop=True,
                                   tile_position=(0, 64)),
                          inc="pe")

            def emit_post(u, ev_yu, mglob):
                c_, j_ = unit_list[u]
                par = u % 2
                y_ = pb[par]
                w_p = [ev_yu]
                if u == 0 and ev_yb_dma is not None:
                    w_p.append(ev_yb_dma)
                op("scalar",
                   lambda s, c__=c_, j__=j_, yy=y_, m_=mglob:
                   s.activation(yb[:, j__, c__ * 128:(c__ + 1) * 128],
                                yy[:, 0:128], AF.Identity,
                                accum_out=stats[:, j__ * 64 + m_:
                                                j__ * 64 + m_ + 1]),
                   w_p, inc="ac")
                ev_su = op("vector",
                           lambda v, j__=j_, par_=par:
                           v.scalar_tensor_tensor(st[:, j__, :], st[:, j__, :],
                                                  g128c[:, j__:j__ + 1],
                                                  pb[4 + par_][:, 256:320],
                                                  ALU.mult, ALU.add),
                           [ev_yu], inc="dv")
                ev_par_post[par] = ev_su
                op("scalar",
                   lambda s, j__=j_, yy=y_, m_=mglob, par_=par:
                   s.activation(trash[par_], yy[:, 0:128], AF.Square,
                                accum_out=stats[:, 512 + j__ * 64 + m_:
                                                512 + j__ * 64 + m_ + 1]))
                sbf_ev[j_] = op(
                    "scalar",
                    lambda s, j__=j_: s.activation(sbf[:, j__, :],
                                                   st[:, j__, :], AF.Identity),
                    [ev_su], inc="ac")
                ev_par_sq[par] = sbf_ev[j_]

            ev_ta_l = [None] * (NU + 1)
            ev_pre_l = [None] * (NU + 1)
            ev_qg_l = [None] * (NU + 1)
            ev_ta_l[0] = emit_TA(0)
            ev_pre_l[0] = emit_pre(0, ev_ta_l[0])
            for u in range(NU):
                if u + 1 < NU:
                    ev_ta_l[u + 1] = emit_TA(u + 1)
                ev_yu = emit_YU(u, ev_pre_l[u])
                if u + 1 < NU:
                    ev_pre_l[u + 1] = emit_pre(u + 1, ev_ta_l[u + 1])
                emit_post(u, ev_yu, 8 * k + unit_list[u][0])

            ev_yb_dma = dma("sync", yhbm3[:, :, k * 1024:(k + 1) * 1024], yb,
                            waits=[ev("dv"), ev("ac")])

        ev_wo = dma("gpsimd", wos, wo_d.rearrange("(a p) f -> p a f", p=128),
                    waits=[ev("pe")])

        op("vector",
           lambda v: v.tensor_reduce(
               stk[:, 0:8],
               stats[:, 0:512].rearrange("p (a f) -> p a f", a=8),
               AX.X, ALU.add),
           [ev("dv"), ev("ac")])
        ev_red = op("vector",
                    lambda v: v.tensor_reduce(
                        stk[:, 8:16],
                        stats[:, 512:1024].rearrange("p (a f) -> p a f", a=8),
                        AX.X, ALU.add),
                    inc="dv")
        ev_gs = op("tensor",
                   lambda t: t.matmul(pb[4][0:16, 0:2], stk, halfo,
                                      start=True, stop=True),
                   [ev_red], inc="pe")
        ev_sc = op("vector", lambda v: v.tensor_copy(ssum, pb[4][0:16, 0:2]),
                   [ev_gs], inc="dv")
        ev_sr = dma("sync", srow.rearrange("q (a f) -> q a f", a=16), ssum,
                    waits=[ev_sc])
        ev_g1 = op("vector",
           lambda v: v.tensor_scalar(srow, srow, 1.0 / GN_N, None, ALU.mult),
           [ev_sr], inc="dv")
        ev_g2 = op("vector",
           lambda v: v.tensor_tensor(gsc[0:1, 0:16], srow[0:1, 0:16],
                                     srow[0:1, 0:16], ALU.mult),
           [ev_g1], inc="dv")
        ev_var = op("vector",
                    lambda v: v.tensor_tensor(gsc[0:1, 16:32],
                                              srow[0:1, 16:32],
                                              gsc[0:1, 0:16], ALU.subtract),
                    [ev_g2], inc="dv")
        ev_gln = op("scalar",
           lambda s: s.activation(gsc[0:1, 32:48], gsc[0:1, 16:32], AF.Ln,
                                  bias=epsc[0:1, :]),
           [ev_var], inc="ac")
        ev_rstd = op("scalar",
                     lambda s: s.activation(gsc[0:1, 48:64], gsc[0:1, 32:48],
                                            AF.Exp, scale=-0.5),
                     [ev_gln], inc="ac")
        exp_srcs = [gsc[0:1, 48:64][:, 0:16:2], gsc[0:1, 48:64][:, 1:16:2],
                    srow[0:1, 0:16][:, 0:16:2], srow[0:1, 0:16][:, 1:16:2]]
        for i_, src_ in enumerate(exp_srcs):
            op("tensor",
               lambda t, i__=i_, s_=src_:
               t.matmul(pb[5][:, 328 + i__ * 8:336 + i__ * 8], ones1, s_,
                        start=True, stop=True),
               [ev_rstd] if i_ == 0 else [],
               inc="pe" if i_ == 3 else None)
        ev_expmm = ev("pe")
        op("vector",
           lambda v: v.tensor_copy(rexp[0:64, :], pb[5][0:64, 328:336]),
           [ev_expmm])
        op("vector",
           lambda v: v.tensor_copy(rexp[64:128, :], pb[5][64:128, 336:344]))
        op("vector",
           lambda v: v.tensor_copy(mexp[0:64, :], pb[5][0:64, 344:352]))
        ev_e4 = op("vector",
                   lambda v: v.tensor_copy(mexp[64:128, :],
                                           pb[5][64:128, 352:360]),
                   inc="dv")
        ev_a1 = op("vector", lambda v: v.tensor_tensor(spd, gnw, rexp,
                                                       ALU.mult),
                   [ev_e4], inc="dv")
        ev_a2 = op("vector", lambda v: v.tensor_tensor(gsc[:, 64:72], mexp,
                                                       spd, ALU.mult),
                   [ev_a1], inc="dv")
        ev_aff = op("vector",
                    lambda v: v.tensor_tensor(bpd, gnb, gsc[:, 64:72],
                                              ALU.subtract),
                    [ev_a2], inc="dv")

        def dyn_read(g):
            r = g.alloc_register("roff")
            g.reg_load(r, sloff_sb[0:1, 0:1])
            src = dataclasses.replace(yhbm3[:, :, 0:SL], offset=r)
            return g.dma_start(out=ysl, in_=src)

        ev_ysl = op("gpsimd", dyn_read, [ev_consts, ev_yb_dma], inc="dg")

        for dt_i in range(8):
            op("scalar",
               lambda s, d_=dt_i:
               s.activation(yn[:, d_, :], ysl[:, d_, :], AF.Identity,
                            bias=bpd[:, d_:d_ + 1], scale=spd[:, d_:d_ + 1]),
               [ev_ysl, ev_aff] if dt_i == 0 else [],
               inc="ac" if dt_i == 7 else None)
        ev_yn_all = ev("ac")

        xsl3 = xsl.rearrange("(a p) d -> p a d", p=128)
        out3 = out_d.rearrange("(a p) d -> p a d", p=128)
        ev_xslt = [None, None]
        ev_rb = [None, None]
        ev_x1_prev = [None, None]
        ev_x1tb_cp = None
        ev_tb_prev = None
        ev_strip_cp = None
        ev_glog = None
        for m in range(16):
            mp = m % 2
            w_x = [ev_rb[mp]] if ev_rb[mp] else []
            ev_xslt[mp] = dma("sync", xslt[mp], xsl3[:, m, :], waits=w_x)
            bank0, bank1 = (pb[0], pb[1]) if mp == 0 else (pb[2], pb[3])
            w_m = []
            if m == 0:
                w_m = [ev_yn_all, ev_wo, ev("dv")]
            if ev_rb[mp]:
                w_m.append(ev_rb[mp])
            for kk in range(8):
                st_, sp_ = kk == 0, kk == 7
                op("tensor",
                   lambda t, kk_=kk, m_=m, b_=bank0, st__=st_, sp__=sp_:
                   t.matmul(b_[:, :], yn[:, kk_, m_ * 128:(m_ + 1) * 128],
                            wos[:, kk_, 0:512], start=st__, stop=sp__),
                   w_m if kk == 0 else [])
                op("tensor",
                   lambda t, kk_=kk, m_=m, b_=bank1, st__=st_, sp__=sp_:
                   t.matmul(b_[:, :], yn[:, kk_, m_ * 128:(m_ + 1) * 128],
                            wos[:, kk_, 512:1024], start=st__, stop=sp__),
                   inc="pe" if kk == 7 else None)
            ev_oproj = ev("pe")
            w_r = [ev_oproj, ev_xslt[mp]]
            if ev_x1_prev[mp]:
                w_r.append(ev_x1_prev[mp])
            op("vector",
               lambda v, b_=bank0, mp_=mp:
               v.tensor_tensor(rbuf[mp_][:, 0:512], b_[:, :],
                               xslt[mp_][:, 0:512], ALU.add),
               w_r)
            ev_ra = op("vector",
                       lambda v, b_=bank1, mp_=mp:
                       v.tensor_tensor(rbuf[mp_][:, 512:1024], b_[:, :],
                                       xslt[mp_][:, 512:1024], ALU.add),
                       inc="dv")
            op("vector",
               lambda v, mp_=mp: v.bn_stats(bn6[:, 0:6], rbuf[mp_][:, 0:512]),
               [ev_ra])
            ev_bs = op("vector",
                       lambda v, mp_=mp: v.bn_stats(bn6[:, 6:12],
                                                    rbuf[mp_][:, 512:1024]),
                       inc="dv")
            ev_bn = op("vector", lambda v: v.bn_aggr(bn2, bn6), [ev_bs],
                       inc="dv")
            ev_rb[mp] = ev_bn
            ev_l1 = op("scalar",
                       lambda s: s.activation(lncol[:, 0:1], bn2[:, 1:2],
                                              AF.Ln, bias=epsc),
                       [ev_bn], inc="ac")
            ev_ln = op("scalar",
                       lambda s: s.activation(lncol[:, 1:2], lncol[:, 0:1],
                                              AF.Exp, scale=-0.5),
                       [ev_l1], inc="ac")
            ev_nmr = op("vector",
                        lambda v:
                        v.scalar_tensor_tensor(lncol[:, 2:3], bn2[:, 0:1],
                                               -1.0, lncol[:, 1:2],
                                               ALU.mult, ALU.mult),
                        [ev_ln], inc="dv")
            ev_x1 = op("scalar",
                       lambda s, m_=m, mp_=mp:
                       s.activation(acc[:, m_, :], rbuf[mp_], AF.Identity,
                                    bias=lncol[:, 2:3], scale=lncol[:, 1:2]),
                       [ev_nmr], inc="ac")
            ev_x1_prev[mp] = ev_x1
            w_c = [ev_x1]
            if ev_tb_prev:
                w_c.append(ev_tb_prev)
            ev_c = op("vector",
                      lambda v, m_=m: v.tensor_copy(x1bscr, acc[:, m_, :]),
                      w_c, inc="dv")
            for dt_i in range(8):
                w_t = []
                if dt_i == 0:
                    w_t = [ev_c]
                    if ev_x1tb_cp:
                        w_t.append(ev_x1tb_cp)
                op("tensor",
                   lambda t, d_=dt_i:
                   t.transpose(pb_bf[0][:, d_ * 128:(d_ + 1) * 128],
                               x1bscr[:, d_ * 128:(d_ + 1) * 128], idnb),
                   w_t, inc="pe" if dt_i == 7 else None)
            ev_tb_prev = ev("pe")
            ev_x1tb_cp = op(
                "vector",
                lambda v, m_=m:
                v.tensor_copy(x1tb[:, :, m_ * 128:(m_ + 1) * 128],
                              pb_bf[0].rearrange("p (a f) -> p a f", a=8)),
                [ev_tb_prev], inc="dv")
            for dt_i in range(8):
                bsel = pb[4] if dt_i < 4 else pb[5]
                w_t = []
                if dt_i == 0:
                    w_t = [ev_x1]
                    if ev_strip_cp:
                        w_t.append(ev_strip_cp)
                    if ev_glog:
                        w_t.append(ev_glog)
                op("tensor",
                   lambda t, m_=m, d_=dt_i, b_=bsel:
                   t.transpose(b_[:, (d_ % 4) * 128:(d_ % 4 + 1) * 128],
                               acc[:, m_, d_ * 128:(d_ + 1) * 128], idnf),
                   w_t, inc="pe" if dt_i == 7 else None)
            ev_tf = ev("pe")
            op("vector",
               lambda v: v.tensor_copy(
                   x1s[:, 0:4, :].rearrange("p a f -> p (a f)"), pb[4][:, :]),
               [ev_tf])
            ev_strip_cp = op(
                "vector",
                lambda v: v.tensor_copy(
                    x1s[:, 4:8, :].rearrange("p a f -> p (a f)"), pb[5][:, :]),
                inc="dv")
            for kk in range(8):
                op("tensor",
                   lambda t, kk_=kk, st__=(kk == 0), sp__=(kk == 7):
                   t.matmul(pb[4][0:128, 320:328], x1s[:, kk_, :],
                            gw_sb[:, kk_, :], start=st__, stop=sp__),
                   [ev_strip_cp] if kk == 0 else [],
                   inc="pe" if kk == 7 else None)
            ev_gmm = ev("pe")
            ev_glog = op("vector",
                         lambda v, m_=m:
                         v.tensor_copy(glog[:, m_ * 8:(m_ + 1) * 8],
                                       pb[4][:, 320:328]),
                         [ev_gmm], inc="dv")

        g3 = glog.rearrange("p (a f) -> p a f", a=16)

        def bc16(ap_):
            return ap_.rearrange("p (a f) -> p a f", a=16) \
                .broadcast_to([128, 16, 8])

        def t3(ap_):
            return ap_.rearrange("p (a f) -> p a f", a=16)

        def chain_v(fn, waits=()):
            return op("vector", fn, waits, inc="dv")

        e1 = chain_v(lambda v: v.tensor_reduce(gsc[:, 0:16], g3, AX.X,
                                               ALU.max), [ev_glog])
        ev_sub = chain_v(lambda v: v.tensor_tensor(g3, g3, bc16(gsc[:, 0:16]),
                                                   ALU.subtract), [e1])
        ev_exp = op("scalar", lambda s: s.activation(glog, glog, AF.Exp),
                    [ev_sub], inc="ac")
        e2 = chain_v(lambda v: v.tensor_reduce(gsc[:, 0:16], g3, AX.X,
                                               ALU.add), [ev_exp])
        e3 = chain_v(lambda v: v.reciprocal(gsc[:, 0:16], gsc[:, 0:16]), [e2])
        e4 = chain_v(lambda v: v.tensor_tensor(g3, g3, bc16(gsc[:, 0:16]),
                                               ALU.mult), [e3])
        e5 = chain_v(lambda v: v.tensor_tensor(
            t3(gsc[:, 16:144]), g3,
            ebias.rearrange("p (a f) -> p a f", a=1)
            .broadcast_to([128, 16, 8]), ALU.add), [e4])
        sb3 = gsc[:, 16:144].rearrange("p (a f) -> p a f", a=16)
        e6 = chain_v(lambda v: v.tensor_reduce(gsc[:, 0:16], sb3, AX.X,
                                               ALU.max), [e5])
        e7 = chain_v(lambda v: v.tensor_tensor(t3(gsc[:, 144:272]),
                                               t3(gsc[:, 16:144]),
                                               bc16(gsc[:, 0:16]), ALU.is_ge),
                     [e6])
        e8 = chain_v(lambda v: v.scalar_tensor_tensor(
            gsc[:, 144:272], gsc[:, 144:272], 1e30, gsc[:, 16:144],
            ALU.mult, ALU.subtract), [e7])
        e9 = chain_v(lambda v: v.tensor_reduce(
            gsc[:, 0:16],
            gsc[:, 144:272].rearrange("p (a f) -> p a f", a=16),
            AX.X, ALU.min), [e8])
        e10 = chain_v(lambda v: v.tensor_scalar(gsc[:, 0:16], gsc[:, 0:16],
                                                -1.0, None, ALU.mult), [e9])
        e11 = chain_v(lambda v: v.tensor_tensor(t3(gsc[:, 144:272]),
                                                t3(gsc[:, 16:144]),
                                                bc16(gsc[:, 0:16]),
                                                ALU.is_ge), [e10])
        ev_wsel = chain_v(lambda v: v.tensor_tensor(wsel, glog,
                                                    gsc[:, 144:272],
                                                    ALU.mult), [e11])
        ev_wt8 = None
        for m in range(16):
            op("tensor",
               lambda t, m_=m:
               t.transpose(pb[4][0:8, 328:456],
                           wsel[:, m_ * 8:(m_ + 1) * 8], idnf),
               [ev_wsel] if m == 0 else [ev_wt8],
               inc="pe")
            ev_tw = ev("pe")
            ev_wt8 = op("vector",
                        lambda v, m_=m:
                        v.tensor_copy(wt8[:, m_ * 128:(m_ + 1) * 128],
                                      pb[4][0:8, 328:456]),
                        [ev_tw], inc="dv")

        pairs3 = [(pb[0], pb[1]), (pb[2], pb[3]), (pb[4], pb[5])]

        def w13_src(wd, e, q):
            return wd[e:e + 1, :, HQ_OFF[q]:HQ_OFF[q] + HQ_TILES[q] * 128] \
                .rearrange("e (a p) h -> p (e a) h", p=128)

        def w2_src(e, q):
            return w2_d[e:e + 1, HQ_OFF[q]:HQ_OFF[q] + HQ_TILES[q] * 128, :] \
                .rearrange("e (a p) d -> p (e a) d", p=128)

        fetch_q = [(e, q) for e in range(NE) for q in range(4)]
        ev_fA, ev_fB, ev_fC = {}, {}, {}
        e0, q0 = fetch_q[0]
        ev_fA[(e0, q0)] = dma("gpsimd", slA(q0), w13_src(w1_d, e0, q0),
                              waits=[ev("pe")])
        ev_fB[(e0, q0)] = dma("gpsimd", slB(q0), w13_src(w3_d, e0, q0))
        ev_fC[(e0, q0)] = dma("gpsimd", slC(q0), w2_src(e0, q0))
        fetch_idx = 1

        ev_wbmm_prev = None
        ev_acc_upd = [None, None]
        last_h1_mm = None
        last_h3_mm = None
        for e in range(NE):
            w_wr = [ev_wt8] if e == 0 else []
            if ev_wbmm_prev:
                w_wr.append(ev_wbmm_prev)
            ev_wr = dma("sync", wrow, wt8[e:e + 1, :], waits=w_wr)
            for nchunk in range(4):
                op("tensor",
                   lambda t, n_=nchunk:
                   t.matmul(pb[n_][:, :], ones1,
                            wrow[0:1, n_ * 512:(n_ + 1) * 512],
                            start=True, stop=True),
                   [ev_wr, ev("dv")] if nchunk == 0 else [],
                   inc="pe" if nchunk == 3 else None)
            ev_wbmm_prev = ev("pe")
            ev_wbc = None
            for nchunk in range(4):
                ev_wbc = op("vector",
                            lambda v, n_=nchunk:
                            v.tensor_copy(wbc[:, n_ * 512:(n_ + 1) * 512],
                                          pb[n_][:, :]),
                            [ev_wbmm_prev] if nchunk == 0 else [], inc="dv")
            job = 0
            job_ev = [ev_wbc, ev_wbc, ev_wbc]
            for q in range(4):
                nt = HQ_TILES[q]
                ev_s = None
                h1_pair = None
                for tcp in range(2):
                    for hp in range(nt):
                        for mat in range(2):
                            pr = pairs3[job % 3]
                            w_j = []
                            if job_ev[job % 3] is not None:
                                w_j.append(job_ev[job % 3])
                            if hp == 0 and tcp == 0 and mat == 0:
                                w_j += [ev_fA[(e, q)], ev_fB[(e, q)], ev_wbc]
                            src = (slA(q) if mat == 0 else slB(q))
                            for kk in range(8):
                                st_, sp_ = kk == 0, kk == 7
                                op("tensor",
                                   lambda t, kk_=kk, h_=hp, tc_=tcp, s_=src,
                                          b_=pr[0], st__=st_, sp__=sp_:
                                   t.matmul(b_[:, :],
                                            s_[:, kk_, h_ * 128:(h_ + 1) * 128],
                                            x1tb[:, kk_,
                                                 tc_ * 1024:tc_ * 1024 + 512],
                                            start=st__, stop=sp__),
                                   w_j if kk == 0 else [])
                                op("tensor",
                                   lambda t, kk_=kk, h_=hp, tc_=tcp, s_=src,
                                          b_=pr[1], st__=st_, sp__=sp_:
                                   t.matmul(b_[:, :],
                                            s_[:, kk_, h_ * 128:(h_ + 1) * 128],
                                            x1tb[:, kk_, tc_ * 1024 + 512:
                                                 (tc_ + 1) * 1024],
                                            start=st__, stop=sp__),
                                   inc="pe" if kk == 7 else None)
                            if mat == 0:
                                h1_pair = pr
                                h1_job = job
                                last_h1_mm = ev("pe")
                            else:
                                last_h3_mm = ev("pe")
                                hb = tcp
                                w_h1 = [last_h3_mm]
                                if job <= 2:
                                    w_h1.append(ev_wbc)
                                op("vector",
                                   lambda v, b_=h1_pair[0], hb_=hb, tc_=tcp:
                                   v.tensor_tensor(
                                       h1w[hb_][:, 0:512], b_[:, :],
                                       wbc[:, tc_ * 1024:tc_ * 1024 + 512],
                                       ALU.mult),
                                   w_h1)
                                ev_h1w = op(
                                    "vector",
                                    lambda v, b_=h1_pair[1], hb_=hb, tc_=tcp:
                                    v.tensor_tensor(
                                        h1w[hb_][:, 512:1024], b_[:, :],
                                        wbc[:, tc_ * 1024 + 512:
                                            (tc_ + 1) * 1024], ALU.mult),
                                    inc="dv")
                                ev_t1 = op(
                                    "scalar",
                                    lambda s, hb_=hb:
                                    s.activation(t1[hb_], h1w[hb_], AF.Silu),
                                    [ev_h1w], inc="ac")
                                op("vector",
                                   lambda v, b_=pr[0], hb_=hb, h_=hp, tc_=tcp:
                                   v.tensor_tensor(
                                       smoe[:, h_,
                                            tc_ * 1024:tc_ * 1024 + 512],
                                       t1[hb_][:, 0:512], b_[:, :], ALU.mult),
                                   [ev_t1])
                                ev_s = op(
                                    "vector",
                                    lambda v, b_=pr[1], hb_=hb, h_=hp, tc_=tcp:
                                    v.tensor_tensor(
                                        smoe[:, h_, tc_ * 1024 + 512:
                                             (tc_ + 1) * 1024],
                                        t1[hb_][:, 512:1024], b_[:, :],
                                        ALU.mult),
                                    inc="dv")
                                job_ev[h1_job % 3] = ev_h1w
                                job_ev[job % 3] = ev_s
                            job += 1
                for m in range(16):
                    mp = m % 2
                    pr2 = (pb[0], pb[1]) if mp == 0 else (pb[2], pb[3])
                    w_m = []
                    if m == 0:
                        w_m = [ev_s, ev_fC[(e, q)], ev("dv")]
                    if ev_acc_upd[mp] is not None:
                        w_m.append(ev_acc_upd[mp])
                    for kk in range(nt):
                        st_, sp_ = kk == 0, kk == nt - 1
                        op("tensor",
                           lambda t, kk_=kk, m_=m, b_=pr2[0], q_=q,
                                  st__=st_, sp__=sp_:
                           t.matmul(b_[:, :],
                                    smoe[:, kk_, m_ * 128:(m_ + 1) * 128],
                                    slC(q_)[:, kk_, 0:512],
                                    start=st__, stop=sp__),
                           w_m if kk == 0 else [])
                        op("tensor",
                           lambda t, kk_=kk, m_=m, b_=pr2[1], q_=q,
                                  st__=st_, sp__=sp_:
                           t.matmul(b_[:, :],
                                    smoe[:, kk_, m_ * 128:(m_ + 1) * 128],
                                    slC(q_)[:, kk_, 512:1024],
                                    start=st__, stop=sp__),
                           inc="pe" if kk == nt - 1 else None)
                    ev_o2 = ev("pe")
                    w_st = [ev_o2]
                    if ev_acc_upd[mp] is not None:
                        w_st.append(ev_acc_upd[mp])
                    op("vector",
                       lambda v, b_=pr2[0], m_=m, e_=e:
                       v.scalar_tensor_tensor(
                           acc[:, m_, 0:512], b_[:, :],
                           wsel[:, m_ * 8 + e_:m_ * 8 + e_ + 1],
                           acc[:, m_, 0:512], ALU.mult, ALU.add),
                       w_st)
                    ev_acc_upd[mp] = op(
                        "vector",
                        lambda v, b_=pr2[1], m_=m, e_=e:
                        v.scalar_tensor_tensor(
                            acc[:, m_, 512:1024], b_[:, :],
                            wsel[:, m_ * 8 + e_:m_ * 8 + e_ + 1],
                            acc[:, m_, 512:1024], ALU.mult, ALU.add),
                        inc="dv")
                if fetch_idx < len(fetch_q):
                    en, qn = fetch_q[fetch_idx]
                    ev_fA[(en, qn)] = dma("gpsimd", slA(qn),
                                          w13_src(w1_d, en, qn),
                                          waits=[last_h1_mm])
                    ev_fB[(en, qn)] = dma("gpsimd", slB(qn),
                                          w13_src(w3_d, en, qn),
                                          waits=[last_h3_mm])
                    ev_fC[(en, qn)] = dma("gpsimd", slC(qn), w2_src(en, qn),
                                          waits=[ev("pe")])
                    fetch_idx += 1
                job_ev = [ev("dv")] * 3

        ev_out_dma = [None, None]
        for m in range(16):
            mp = m % 2
            w_m = [ev_acc_upd[0], ev_acc_upd[1]] if m == 0 else []
            op("vector",
               lambda v, m_=m: v.bn_stats(bn6[:, 0:6], acc[:, m_, 0:512]),
               w_m)
            ev_bs = op("vector",
                       lambda v, m_=m: v.bn_stats(bn6[:, 6:12],
                                                  acc[:, m_, 512:1024]),
                       inc="dv")
            ev_bn = op("vector", lambda v: v.bn_aggr(bn2, bn6), [ev_bs],
                       inc="dv")
            ev_l1 = op("scalar",
                       lambda s: s.activation(lncol[:, 0:1], bn2[:, 1:2],
                                              AF.Ln, bias=epsc),
                       [ev_bn], inc="ac")
            ev_ln = op("scalar",
                       lambda s: s.activation(lncol[:, 1:2], lncol[:, 0:1],
                                              AF.Exp, scale=-0.5),
                       [ev_l1], inc="ac")
            ev_nmr = op("vector",
                        lambda v:
                        v.scalar_tensor_tensor(lncol[:, 2:3], bn2[:, 0:1],
                                               -1.0, lncol[:, 1:2],
                                               ALU.mult, ALU.mult),
                        [ev_ln], inc="dv")
            w_o = [ev_nmr]
            if ev_out_dma[mp]:
                w_o.append(ev_out_dma[mp])
            ev_o = op("scalar",
                      lambda s, m_=m, mp_=mp:
                      s.activation(outs[mp_], acc[:, m_, :], AF.Identity,
                                   bias=lncol[:, 2:3], scale=lncol[:, 1:2]),
                      w_o, inc="ac")
            ev_out_dma[mp] = dma("sync", out3[:, m, :], outs[mp],
                                 waits=[ev_o])

        if cutoff is not None:
            for e_ in p.q:
                p.q[e_] = [o_ for o_ in p.q[e_] if o_[4] <= cutoff]
        final_ds = sum(o_[3] for o_ in p.q["sync"] if o_[2] == "ds")

        dma_waited = {"ds": set(), "dg": set()}
        for q_ in p.q.values():
            for (waits, _f, _i, _n, _sq) in q_:
                for (s_n, val) in waits:
                    if s_n in dma_waited:
                        dma_waited[s_n].add(val)
        dma_waited = {k: sorted(v) for k, v in dma_waited.items()}

        def body(name):
            own = {"sync": "ds", "gpsimd": "dg"}.get(name)
            def _(eng):
                seen = {}
                cnt = 0
                for (waits, fn, inc, n, _sq) in p.q[name]:
                    if own is not None and inc == own:
                        need = 0
                        for v_ in dma_waited[own]:
                            if v_ <= cnt:
                                need = v_
                            else:
                                break
                        if need > seen.get(own, -1):
                            eng.wait_ge(sems[own], need)
                            seen[own] = need
                        cnt += n
                    for (s_n, val) in waits:
                        if seen.get(s_n, -1) >= val:
                            continue
                        eng.wait_ge(sems[s_n], val)
                        seen[s_n] = val
                    instr = fn(eng)
                    if inc is not None:
                        instr.then_inc(sems[inc], n)
                if name == "sync":
                    eng.wait_ge(sems["ds"], final_ds)
            return _

        block.sync(body("sync"))
        block.gpsimd(body("gpsimd"))
        block.tensor(body("tensor"))
        block.vector(body("vector"))
        block.scalar(body("scalar"))
    return nc


_NC_CACHE = None


def kernel(L0_attn_x, L0_attn_W_q, L0_attn_W_k, L0_attn_W_v, L0_attn_W_o,
           L0_attn_gn_weight, L0_attn_gn_bias, L0_ffn_gate_w,
           L0_ffn_expert_bias, L0_ffn_w1, L0_ffn_w2, L0_ffn_w3):
    global _NC_CACHE
    if _NC_CACHE is None:
        _NC_CACHE = build_program()
    nc = _NC_CACHE

    x = np.asarray(L0_attn_x, np.float32)
    wq = np.ascontiguousarray(np.asarray(L0_attn_W_q, np.float32).T)
    wk = np.ascontiguousarray(np.asarray(L0_attn_W_k, np.float32).T)
    wv = np.ascontiguousarray(np.asarray(L0_attn_W_v, np.float32).T)
    wo = np.ascontiguousarray(np.asarray(L0_attn_W_o, np.float32).T)
    gw = np.ascontiguousarray(np.asarray(L0_ffn_gate_w, np.float32).T)
    pconsts = np.zeros((128, 24), np.float32)
    pconsts[:, 0:8] = np.asarray(L0_attn_gn_weight, np.float32).reshape(8, 128).T
    pconsts[:, 8:16] = np.asarray(L0_attn_gn_bias, np.float32).reshape(8, 128).T
    pconsts[:, 16:24] = np.broadcast_to(
        np.asarray(L0_ffn_expert_bias, np.float32), (128, 8))
    w1 = np.ascontiguousarray(np.asarray(L0_ffn_w1, np.float32))
    w2 = np.ascontiguousarray(np.asarray(L0_ffn_w2, np.float32))
    w3 = np.ascontiguousarray(np.asarray(L0_ffn_w3, np.float32))

    in_maps = []
    for c in range(NCORES):
        b, s = c // 4, c % 4
        in_maps.append({
            "xb": np.ascontiguousarray(x[b]),
            "xsl": np.ascontiguousarray(x[b, s * SL:(s + 1) * SL]),
            "sloff": np.array([[s * SL, 0, 0, 0]], np.int32),
            "wq": wq, "wk": wk, "wv": wv, "wo": wo, "gw": gw,
            "pconsts": pconsts, "w1": w1, "w2": w2, "w3": w3,
        })
    res = run_bass_kernel_spmd(nc, in_maps, core_ids=list(range(NCORES)))
    out = np.empty((B, T, D), np.float32)
    for c in range(NCORES):
        b, s = c // 4, c % 4
        out[b, s * SL:(s + 1) * SL] = res.results[c]["out"]
    return out


# version 07
# speedup vs baseline: 1.0375x; 1.0211x over previous
import dataclasses
from contextlib import ExitStack

import numpy as np
import ml_dtypes

import concourse.bass as bass
import concourse.mybir as mybir
from concourse.bass_utils import run_bass_kernel_spmd

F32 = mybir.dt.float32
BF16 = mybir.dt.bfloat16
I32 = mybir.dt.int32
AF = mybir.ActivationFunctionType
ALU = mybir.AluOpType
AX = mybir.AxisListType

B, T, D = 2, 8192, 1024
H, DH, NPAIR = 16, 64, 8
NBLK, CH = 8, 128
NCORES, SL = 8, 2048
HIDDEN, NE = 2816, 8
HQ_TILES = [6, 5, 6, 5]
HQ_OFF = [0, 768, 1408, 2176]
GN_N = float(DH * T)
EPS = 1e-5

O_IDNB, O_IDNF = 0, 256
O_GNW, O_GNB, O_EB = 768, 800, 832
O_ONES1 = 864
O_G128, O_HALF = 1376, 1408
O_EPSC = 1424
O_SPD, O_BPD, O_REXP, O_MEXP = 1440, 1472, 1504, 1536
O_SLOFF, O_SROW, O_SSUM, O_STK = 1568, 1600, 1728, 1760
O_WSEL = 2048
O_RBUF = 2560
O_WROW = 10752
O_GLOG, O_GSC, O_X1S = 18944, 19456, 20992
O_DT, O_GROW, O_GKT = 25088, 33280, 37376
O_S, O_SBF, O_STATS = 41472, 43520, 44544
O_KG, O_QG, O_AH, O_TRASH = 52736, 53248, 53760, 54784
O_XBLK, O_XT, O_QT, O_KT = 55040, 71424, 87808, 104192
O_VT, O_YB, O_WQ, O_WK, O_WV = 120576, 136960, 153344, 169728, 186112
O_ACC, O_X1TB, O_YSL, O_YN, O_WO = 25088, 90624, 123392, 156160, 188928
O_SMOE, O_WBC = 123392, 147968
O_SLA, O_SLB, O_SLC = 156160, 168448, 180736
O_H1W, O_T1 = 193024, 201216
O_OUTS = 123392
ARENA = 209408

_BF = ml_dtypes.bfloat16


def _tables():
    g = 1.0 - 2.0 ** (-5.0 - np.arange(H, dtype=np.float64))
    i = np.arange(CH, dtype=np.float64)
    diff = i[None, :] - i[:, None]
    dt = np.zeros((CH, NPAIR * 2 * CH), np.float64)
    grow = np.zeros((CH, NPAIR * CH), np.float64)
    gkt = np.zeros((CH, NPAIR * CH), np.float64)
    g128 = np.zeros((CH, NPAIR), np.float64)
    for p_ in range(NPAIR):
        for half in range(2):
            h = 2 * p_ + half
            m = np.where(diff >= 0, g[h] ** np.maximum(diff, 0.0), 0.0)
            dt[:, p_ * 256 + half * 128:p_ * 256 + half * 128 + 128] = m
            rows = slice(half * 64, half * 64 + 64)
            grow[rows, p_ * 128:(p_ + 1) * 128] = (g[h] ** (i + 1.0))[None, :]
            gkt[:, p_ * 128 + half * 64:p_ * 128 + half * 64 + 64] = \
                (g[h] ** (127.0 - i))[:, None]
            g128[rows, p_] = g[h] ** 128.0
    return (dt.astype(np.float32), grow.astype(np.float32),
            gkt.astype(np.float32), g128.astype(np.float32))


class Prog:
    ENGS = ("sync", "gpsimd", "tensor", "vector", "scalar")

    def __init__(self):
        self.q = {e: [] for e in self.ENGS}
        self.cnt = {"ds": 0, "dg": 0, "pe": 0, "dv": 0, "ac": 0, "gp": 0}

    def op(self, eng, fn, waits=(), inc=None):
        n = 16 if inc in ("ds", "dg") else 1
        w = [(s, v) for (s, v) in waits if s is not None and v > 0]
        self.seq = getattr(self, "seq", 0) + 1
        self.q[eng].append((w, fn, inc, n, self.seq))
        if inc is not None:
            self.cnt[inc] += n
            return (inc, self.cnt[inc])
        return None

    def ev(self, sem):
        return (sem, self.cnt[sem])


def build_program(cutoff=None):
    nc = bass.Bass("TRN2", target_bir_lowering=False)

    xb = nc.dram_tensor("xb", [T, D], F32, kind="ExternalInput").ap()
    xsl = nc.dram_tensor("xsl", [SL, D], F32, kind="ExternalInput").ap()
    sloff = nc.dram_tensor("sloff", [1, 4], I32, kind="ExternalInput").ap()
    wq_d = nc.dram_tensor("wq", [D, D], F32, kind="ExternalInput").ap()
    wk_d = nc.dram_tensor("wk", [D, D], F32, kind="ExternalInput").ap()
    wv_d = nc.dram_tensor("wv", [D, D], F32, kind="ExternalInput").ap()
    wo_d = nc.dram_tensor("wo", [D, D], F32, kind="ExternalInput").ap()
    gw_d = nc.dram_tensor("gw", [D, NE], F32, kind="ExternalInput").ap()
    pc_d = nc.dram_tensor("pconsts", [128, 24], F32, kind="ExternalInput").ap()
    w1_d = nc.dram_tensor("w1", [NE, D, HIDDEN], F32, kind="ExternalInput").ap()
    w2_d = nc.dram_tensor("w2", [NE, HIDDEN, D], F32, kind="ExternalInput").ap()
    w3_d = nc.dram_tensor("w3", [NE, D, HIDDEN], F32, kind="ExternalInput").ap()
    out_d = nc.dram_tensor("out", [SL, D], F32, kind="ExternalOutput").ap()
    yhbm = nc.dram_tensor("yhbm", [128, 8 * T], BF16).ap()

    dt_np, grow_np, gkt_np, g128_np = _tables()
    c_dt = nc.inline_tensor(dt_np, name="c_dt").ap()
    c_grow = nc.inline_tensor(grow_np, name="c_grow").ap()
    c_gkt = nc.inline_tensor(gkt_np, name="c_gkt").ap()
    c_g128 = nc.inline_tensor(g128_np, name="c_g128").ap()
    c_idnb = nc.inline_tensor(np.eye(128).astype(_BF), name="c_idnb").ap()
    c_idnf = nc.inline_tensor(np.eye(128, dtype=np.float32), name="c_idnf").ap()
    half_np = np.zeros((128, 2), np.float32)
    half_np[0:64, 0] = 1.0
    half_np[64:128, 1] = 1.0
    c_half = nc.inline_tensor(half_np, name="c_half").ap()

    p = Prog()
    op, ev = p.op, p.ev

    with ExitStack() as ctx:
        arena = nc.alloc_sbuf_tensor("arena", [128, ARENA], mybir.dt.uint8).ap()
        gw_sb = nc.alloc_sbuf_tensor("gwsbuf", [128, 8 * NE], F32).ap() \
            .rearrange("p (a f) -> p a f", a=8)

        def v_(off, n, dtype):
            sz = mybir.dt.size(dtype)
            return arena[:, off:off + n * sz].bitcast(dtype)

        def v3(off, n, dtype, a):
            return v_(off, n, dtype).rearrange("p (a f) -> p a f", a=a)

        idnb = v_(O_IDNB, 128, BF16)
        idnf = v_(O_IDNF, 128, F32)
        gnw = v_(O_GNW, 8, F32)
        gnb = v_(O_GNB, 8, F32)
        ebias = v_(O_EB, 8, F32)
        ones1 = v_(O_ONES1, 128, F32)[0:1, :]
        g128c = v_(O_G128, 8, F32)
        halfo = v_(O_HALF, 2, F32)
        epsc = v_(O_EPSC, 1, F32)
        spd = v_(O_SPD, 8, F32)
        bpd = v_(O_BPD, 8, F32)
        rexp = v_(O_REXP, 8, F32)
        mexp = v_(O_MEXP, 8, F32)
        sloff_sb = arena[:, O_SLOFF:O_SLOFF + 16].bitcast(I32)[0:1, 0:4]
        srow = v_(O_SROW, 32, F32)[0:1, :]
        ssum = v_(O_SSUM, 2, F32)[0:16, :]
        stk = v_(O_STK, 16, F32)
        wsel = v_(O_WSEL, 128, F32)
        rbuf = [v_(O_RBUF + i * 4096, 1024, F32) for i in range(2)]
        wt8 = v_(O_RBUF, 2048, F32)[0:8, :]
        wrow = v_(O_WROW, 2048, F32)[0:1, :]
        xslt = [v_(O_WROW + i * 4096, 1024, F32) for i in range(2)]
        glog = v_(O_GLOG, 128, F32)
        gsc = v_(O_GSC, 384, F32)
        bn6 = gsc[:, 288:300]
        bn2 = gsc[:, 304:306]
        lncol = gsc[:, 308:312]
        x1s = v3(O_X1S, 1024, F32, 8)
        dtt = v3(O_DT, 2048, F32, 8)
        growt = v3(O_GROW, 1024, F32, 8)
        gktt = v3(O_GKT, 1024, F32, 8)
        st = v3(O_S, 512, F32, 8)
        sbf = v3(O_SBF, 512, BF16, 8)
        stats = v_(O_STATS, 2048, F32)
        kg = [v_(O_KG + i * 256, 128, BF16) for i in range(2)]
        qg = [v_(O_QG + i * 256, 128, BF16) for i in range(2)]
        ah = [v_(O_AH + i * 512, 256, BF16) for i in range(2)]
        trash = [v_(O_TRASH, 128, BF16), v_(O_STATS + 4096, 128, BF16)]
        xblk = v3(O_XBLK, 8192, BF16, 8)
        xt = v3(O_XT, 8192, BF16, 8)
        qt = v3(O_QT, 8192, BF16, 8)
        kt = v3(O_KT, 8192, BF16, 8)
        vt = v3(O_VT, 8192, BF16, 8)
        yb = v3(O_YB, 8192, BF16, 8)
        wqs = v3(O_WQ, 8192, BF16, 8)
        wks = v3(O_WK, 8192, BF16, 8)
        wvs = v3(O_WV, 8192, BF16, 8)
        acc = v3(O_ACC, 16384, F32, 16)
        x1tb = v3(O_X1TB, 16384, BF16, 8)
        ysl = v3(O_YSL, 16384, BF16, 8)
        x1bscr = v_(O_YSL, 1024, BF16)
        yn = v3(O_YN, 16384, BF16, 8)
        wos = v3(O_WO, 8192, BF16, 8)
        smoe = v3(O_SMOE, 12288, BF16, 6)
        wbc = v_(O_WBC, 2048, F32)
        h1w = [v_(O_H1W + i * 4096, 1024, F32) for i in range(2)]
        t1 = [v_(O_T1 + i * 4096, 1024, F32) for i in range(2)]
        outs = [v_(O_OUTS + i * 4096, 1024, F32) for i in range(2)]

        def slA(q):
            return v3(O_SLA, 8 * HQ_TILES[q] * 128, BF16, 8)

        def slB(q):
            return v3(O_SLB, 8 * HQ_TILES[q] * 128, BF16, 8)

        def slC(q):
            return v3(O_SLC, HQ_TILES[q] * 1024, BF16, HQ_TILES[q])

        pb_bf = [ctx.enter_context(nc.psum_tensor(f"pbb{i}", [128, 1024], BF16))
                 for i in range(2)]
        pb = [ctx.enter_context(nc.psum_tensor(f"pb{i}", [128, 512], F32))
              for i in range(6)]

        sems = {s: ctx.enter_context(nc.semaphore(s))
                for s in ("ds", "dg", "pe", "dv", "ac", "gp")}
        block = ctx.enter_context(nc.Block())

        def dma(eng, out, in_, waits=()):
            return op(eng, lambda e, o=out, i=in_: e.dma_start(out=o, in_=i),
                      waits, inc="ds" if eng == "sync" else "dg")

        dma("sync", v_(O_DT, 2048, F32), c_dt)
        dma("sync", v_(O_GROW, 1024, F32), c_grow)
        dma("sync", v_(O_GKT, 1024, F32), c_gkt)
        dma("sync", g128c, c_g128)
        dma("sync", idnb, c_idnb)
        dma("sync", idnf, c_idnf)
        dma("sync", halfo, c_half)
        dma("sync", arena[:, O_GNW:O_GNW + 96].bitcast(F32), pc_d)
        dma("sync", sloff_sb, sloff)
        ev_consts = dma("sync", gw_sb, gw_d.rearrange("(a p) f -> p a f", p=128))
        for wsb, wdr in ((wqs, wq_d), (wks, wk_d), (wvs, wv_d)):
            dma("gpsimd", wsb, wdr.rearrange("(a p) f -> p a f", p=128))
        ev_wqkv = ev("dg")

        op("vector", lambda v: v.memset(v_(O_S, 512, F32), 0.0))
        op("vector", lambda v: v.memset(epsc, EPS))
        op("vector", lambda v: v.memset(v_(O_SBF, 512, BF16), 0.0))
        ev_sbzero = op("vector", lambda v: v.memset(ones1, 1.0), inc="dv")

        xb3 = xb.rearrange("(a p) d -> p a d", p=128)
        yhbm3 = yhbm.rearrange("p (a t) -> p a t", a=8)

        ev_par_post = [ev_sbzero, ev_sbzero]
        ev_par_sq = [None, None]
        sbf_ev = [None] * NPAIR
        ev_kg_last = [ev_sbzero, ev_sbzero]
        ev_yb_dma = None
        ev_projbank = {0: None, 1: None}
        ev_xtrans_done = None
        unit_list = [(c_, j_) for c_ in range(8) for j_ in range(NPAIR)]
        projs = [(wqs, qt, 0.125, False), (wks, kt, None, False),
                 (wvs, vt, None, True)]

        for k in range(NBLK):
            w_ = [ev_consts]
            if ev_xtrans_done is not None:
                w_.append(ev_xtrans_done)
            evx = dma("gpsimd", xblk, xb3[:, 8 * k:8 * k + 8, :], waits=w_)

            ev_xt_cp = [None, None]
            for tt in range(8):
                par = tt % 2
                w_t = []
                if tt == 0:
                    w_t = [evx, ev_consts]
                if tt < 2:
                    w_t.append(ev_kg_last[par])
                else:
                    w_t.append(ev_xt_cp[par])
                for dt_i in range(8):
                    op("tensor",
                       lambda t, tt_=tt, d_=dt_i, par_=par:
                       t.transpose(pb_bf[par_][:, d_ * 128:(d_ + 1) * 128],
                                   xblk[:, tt_, d_ * 128:(d_ + 1) * 128],
                                   idnb),
                       w_t if dt_i == 0 else [],
                       inc="pe" if dt_i == 7 else None)
                ev_tt = ev("pe")
                ev_xt_cp[par] = op(
                    "vector",
                    lambda v, tt_=tt, par_=par:
                    v.tensor_copy(xt[:, :, tt_ * 128:(tt_ + 1) * 128],
                                  pb_bf[par_].rearrange("p (a f) -> p a f",
                                                        a=8)),
                    [ev_tt], inc="dv")
            ev_xtrans_done = ev("pe")
            ev_xt_all = ev_xt_cp[1]

            for which, (wsb, dst, scale, tok_major) in enumerate(projs):
                for m in range(8):
                    mp = m % 2
                    bank0, bank1 = (pb[0], pb[1]) if mp == 0 else (pb[2], pb[3])
                    w_m = []
                    if which == 0 and m == 0:
                        w_m = [ev_xt_all, ev_wqkv]
                        if ev_par_sq[0] is not None:
                            w_m += [ev_par_sq[0], ev_par_sq[1]]
                        w_m.append(ev("dv"))
                    if ev_projbank[mp] is not None:
                        w_m.append(ev_projbank[mp])
                    for kk in range(8):
                        st_, sp_ = kk == 0, kk == 7
                        if tok_major:
                            l0 = lambda kk_, m_: xt[:, kk_,
                                                    m_ * 128:(m_ + 1) * 128]
                            r0 = lambda kk_, n_, w__=wsb: \
                                w__[:, kk_, n_ * 512:(n_ + 1) * 512]
                        else:
                            l0 = lambda kk_, m_, w__=wsb: \
                                w__[:, kk_, m_ * 128:(m_ + 1) * 128]
                            r0 = lambda kk_, n_: xt[:, kk_,
                                                    n_ * 512:(n_ + 1) * 512]
                        op("tensor",
                           lambda t, kk_=kk, m_=m, b_=bank0, l_=l0, r_=r0,
                                  st__=st_, sp__=sp_:
                           t.matmul(b_[:, :], l_(kk_, m_), r_(kk_, 0),
                                    start=st__, stop=sp__),
                           w_m if kk == 0 else [])
                        op("tensor",
                           lambda t, kk_=kk, m_=m, b_=bank1, l_=l0, r_=r0,
                                  st__=st_, sp__=sp_:
                           t.matmul(b_[:, :], l_(kk_, m_), r_(kk_, 1),
                                    start=st__, stop=sp__),
                           inc="pe" if kk == 7 else None)
                    ev_mm = ev("pe")

                    def evac(v, b_, dst_, m_, n_, sc_):
                        o_ = dst_[:, m_, n_ * 512:(n_ + 1) * 512]
                        if sc_ is None:
                            return v.tensor_copy(o_, b_[:, :])
                        return v.tensor_scalar(o_, b_[:, :], sc_, None,
                                               ALU.mult)
                    op("vector",
                       lambda v, b_=bank0, d_=dst, m_=m, sc_=scale:
                       evac(v, b_, d_, m_, 0, sc_), [ev_mm])
                    ev_projbank[mp] = op(
                        "vector",
                        lambda v, b_=bank1, d_=dst, m_=m, sc_=scale:
                        evac(v, b_, d_, m_, 1, sc_), inc="dv")
            ev_projevac = ev("dv")

            NU = len(unit_list)

            def emit_TA(u):
                c_, j_ = unit_list[u]
                par = u % 2
                w_ta = [ev_par_post[par], ev_kg_last[par]]
                if ev_par_sq[par] is not None:
                    w_ta.append(ev_par_sq[par])
                if u == 0:
                    w_ta.append(ev_projevac)
                op("tensor",
                   lambda t, c__=c_, j__=j_, par_=par:
                   t.transpose(pb_bf[par_][:, 0:128],
                               kt[:, j__, c__ * 128:(c__ + 1) * 128], idnb),
                   w_ta)
                op("tensor",
                   lambda t, c__=c_, j__=j_, par_=par:
                   t.matmul(pb[4 + par_][0:128, 0:128],
                            kt[0:64, j__, c__ * 128:(c__ + 1) * 128],
                            qt[0:64, j__, c__ * 128:(c__ + 1) * 128],
                            start=True, stop=True))
                return op("tensor",
                          lambda t, c__=c_, j__=j_, par_=par:
                          t.matmul(pb[2 + par_][0:128, 0:128],
                                   kt[64:128, j__, c__ * 128:(c__ + 1) * 128],
                                   qt[64:128, j__, c__ * 128:(c__ + 1) * 128],
                                   start=True, stop=True,
                                   tile_position=(64, 0)),
                          inc="pe")

            def emit_pre(u, ev_ta):
                c_, j_ = unit_list[u]
                par = u % 2
                ev_kg_last[par] = op(
                    "vector",
                    lambda v, j__=j_, par_=par:
                    v.tensor_tensor(kg[par_], pb_bf[par_][:, 0:128],
                                    gktt[:, j__, :], ALU.mult),
                    [ev_ta], inc="dv")
                op("vector",
                   lambda v, j__=j_, par_=par:
                   v.tensor_tensor(ah[par_][:, 0:128], pb[4 + par_][:, 0:128],
                                   dtt[:, j__, 0:128], ALU.mult))
                ev_pre = op("vector",
                            lambda v, j__=j_, par_=par:
                            v.tensor_tensor(ah[par_][:, 128:256],
                                            pb[2 + par_][:, 0:128],
                                            dtt[:, j__, 128:256], ALU.mult),
                            inc="dv")
                ev_qg_l[u] = op("vector",
                                lambda v, c__=c_, j__=j_, par_=par:
                                v.tensor_tensor(qg[par_],
                                                qt[:, j__,
                                                   c__ * 128:(c__ + 1) * 128],
                                                growt[:, j__, :], ALU.mult),
                                inc="dv")
                return ev_pre

            def emit_YU(u, ev_pre):
                c_, j_ = unit_list[u]
                par = u % 2
                y_ = pb[par]
                w_yu = [ev_pre, ev_qg_l[u]]
                if sbf_ev[j_] is not None:
                    w_yu.append(sbf_ev[j_])
                op("tensor",
                   lambda t, c__=c_, j__=j_, par_=par, yy=y_:
                   t.matmul(yy[0:64, 0:128],
                            vt[:, c__, j__ * 128:j__ * 128 + 64],
                            ah[par_][:, 0:128], start=True, stop=False),
                   w_yu)
                op("tensor",
                   lambda t, j__=j_, par_=par, yy=y_:
                   t.matmul(yy[0:64, 0:128], sbf[0:64, j__, :],
                            qg[par_][0:64, :], start=False, stop=True))
                op("tensor",
                   lambda t, c__=c_, j__=j_, par_=par, yy=y_:
                   t.matmul(yy[64:128, 0:128],
                            vt[:, c__, j__ * 128 + 64:(j__ + 1) * 128],
                            ah[par_][:, 128:256], start=True, stop=False,
                            tile_position=(0, 64)))
                op("tensor",
                   lambda t, j__=j_, par_=par, yy=y_:
                   t.matmul(yy[64:128, 0:128], sbf[64:128, j__, :],
                            qg[par_][64:128, :], start=False, stop=True,
                            tile_position=(64, 64)))
                op("tensor",
                   lambda t, c__=c_, j__=j_, par_=par:
                   t.matmul(pb[4 + par_][0:64, 256:320], kg[par_][:, 0:64],
                            vt[:, c__, j__ * 128:j__ * 128 + 64],
                            start=True, stop=True))
                return op("tensor",
                          lambda t, c__=c_, j__=j_, par_=par:
                          t.matmul(pb[4 + par_][64:128, 256:320],
                                   kg[par_][:, 64:128],
                                   vt[:, c__, j__ * 128 + 64:(j__ + 1) * 128],
                                   start=True, stop=True,
                                   tile_position=(0, 64)),
                          inc="pe")

            def emit_post(u, ev_yu, mglob):
                c_, j_ = unit_list[u]
                par = u % 2
                y_ = pb[par]
                w_p = [ev_yu]
                if u == 0 and ev_yb_dma is not None:
                    w_p.append(ev_yb_dma)
                op("scalar",
                   lambda s, c__=c_, j__=j_, yy=y_, m_=mglob:
                   s.activation(yb[:, j__, c__ * 128:(c__ + 1) * 128],
                                yy[:, 0:128], AF.Identity,
                                accum_out=stats[:, j__ * 64 + m_:
                                                j__ * 64 + m_ + 1]),
                   w_p, inc="ac")
                ev_su = op("vector",
                           lambda v, j__=j_, par_=par:
                           v.scalar_tensor_tensor(st[:, j__, :], st[:, j__, :],
                                                  g128c[:, j__:j__ + 1],
                                                  pb[4 + par_][:, 256:320],
                                                  ALU.mult, ALU.add),
                           [ev_yu], inc="dv")
                ev_par_post[par] = ev_su
                op("scalar",
                   lambda s, j__=j_, yy=y_, m_=mglob, par_=par:
                   s.activation(trash[par_], yy[:, 0:128], AF.Square,
                                accum_out=stats[:, 512 + j__ * 64 + m_:
                                                512 + j__ * 64 + m_ + 1]))
                sbf_ev[j_] = op(
                    "scalar",
                    lambda s, j__=j_: s.activation(sbf[:, j__, :],
                                                   st[:, j__, :], AF.Identity),
                    [ev_su], inc="ac")
                ev_par_sq[par] = sbf_ev[j_]

            ev_ta_l = [None] * (NU + 1)
            ev_pre_l = [None] * (NU + 1)
            ev_qg_l = [None] * (NU + 1)
            ev_ta_l[0] = emit_TA(0)
            ev_pre_l[0] = emit_pre(0, ev_ta_l[0])
            for u in range(NU):
                if u + 1 < NU:
                    ev_ta_l[u + 1] = emit_TA(u + 1)
                ev_yu = emit_YU(u, ev_pre_l[u])
                if u + 1 < NU:
                    ev_pre_l[u + 1] = emit_pre(u + 1, ev_ta_l[u + 1])
                emit_post(u, ev_yu, 8 * k + unit_list[u][0])

            ev_yb_dma = dma("sync", yhbm3[:, :, k * 1024:(k + 1) * 1024], yb,
                            waits=[ev("dv"), ev("ac")])

        ev_wo = dma("gpsimd", wos, wo_d.rearrange("(a p) f -> p a f", p=128),
                    waits=[ev("pe")])

        op("vector",
           lambda v: v.tensor_reduce(
               stk[:, 0:8],
               stats[:, 0:512].rearrange("p (a f) -> p a f", a=8),
               AX.X, ALU.add),
           [ev("dv"), ev("ac")])
        ev_red = op("vector",
                    lambda v: v.tensor_reduce(
                        stk[:, 8:16],
                        stats[:, 512:1024].rearrange("p (a f) -> p a f", a=8),
                        AX.X, ALU.add),
                    inc="dv")
        ev_gs = op("tensor",
                   lambda t: t.matmul(pb[4][0:16, 0:2], stk, halfo,
                                      start=True, stop=True),
                   [ev_red], inc="pe")
        ev_sc = op("vector", lambda v: v.tensor_copy(ssum, pb[4][0:16, 0:2]),
                   [ev_gs], inc="dv")
        ev_sr = dma("sync", srow.rearrange("q (a f) -> q a f", a=16), ssum,
                    waits=[ev_sc])
        ev_g1 = op("vector",
           lambda v: v.tensor_scalar(srow, srow, 1.0 / GN_N, None, ALU.mult),
           [ev_sr], inc="dv")
        ev_g2 = op("vector",
           lambda v: v.tensor_tensor(gsc[0:1, 0:16], srow[0:1, 0:16],
                                     srow[0:1, 0:16], ALU.mult),
           [ev_g1], inc="dv")
        ev_var = op("vector",
                    lambda v: v.tensor_tensor(gsc[0:1, 16:32],
                                              srow[0:1, 16:32],
                                              gsc[0:1, 0:16], ALU.subtract),
                    [ev_g2], inc="dv")
        ev_gln = op("scalar",
           lambda s: s.activation(gsc[0:1, 32:48], gsc[0:1, 16:32], AF.Ln,
                                  bias=epsc[0:1, :]),
           [ev_var], inc="ac")
        ev_rstd = op("scalar",
                     lambda s: s.activation(gsc[0:1, 48:64], gsc[0:1, 32:48],
                                            AF.Exp, scale=-0.5),
                     [ev_gln], inc="ac")
        exp_srcs = [gsc[0:1, 48:64][:, 0:16:2], gsc[0:1, 48:64][:, 1:16:2],
                    srow[0:1, 0:16][:, 0:16:2], srow[0:1, 0:16][:, 1:16:2]]
        for i_, src_ in enumerate(exp_srcs):
            op("tensor",
               lambda t, i__=i_, s_=src_:
               t.matmul(pb[5][:, 328 + i__ * 8:336 + i__ * 8], ones1, s_,
                        start=True, stop=True),
               [ev_rstd] if i_ == 0 else [],
               inc="pe" if i_ == 3 else None)
        ev_expmm = ev("pe")
        op("vector",
           lambda v: v.tensor_copy(rexp[0:64, :], pb[5][0:64, 328:336]),
           [ev_expmm])
        op("vector",
           lambda v: v.tensor_copy(rexp[64:128, :], pb[5][64:128, 336:344]))
        op("vector",
           lambda v: v.tensor_copy(mexp[0:64, :], pb[5][0:64, 344:352]))
        ev_e4 = op("vector",
                   lambda v: v.tensor_copy(mexp[64:128, :],
                                           pb[5][64:128, 352:360]),
                   inc="dv")
        ev_a1 = op("vector", lambda v: v.tensor_tensor(spd, gnw, rexp,
                                                       ALU.mult),
                   [ev_e4], inc="dv")
        ev_a2 = op("vector", lambda v: v.tensor_tensor(gsc[:, 64:72], mexp,
                                                       spd, ALU.mult),
                   [ev_a1], inc="dv")
        ev_aff = op("vector",
                    lambda v: v.tensor_tensor(bpd, gnb, gsc[:, 64:72],
                                              ALU.subtract),
                    [ev_a2], inc="dv")

        def dyn_read(g):
            r = g.alloc_register("roff")
            g.reg_load(r, sloff_sb[0:1, 0:1])
            src = dataclasses.replace(yhbm3[:, :, 0:SL], offset=r)
            return g.dma_start(out=ysl, in_=src)

        ev_ysl = op("gpsimd", dyn_read, [ev_consts, ev_yb_dma], inc="dg")

        for dt_i in range(8):
            op("scalar",
               lambda s, d_=dt_i:
               s.activation(yn[:, d_, :], ysl[:, d_, :], AF.Identity,
                            bias=bpd[:, d_:d_ + 1], scale=spd[:, d_:d_ + 1]),
               [ev_ysl, ev_aff] if dt_i == 0 else [],
               inc="ac" if dt_i == 7 else None)
        ev_yn_all = ev("ac")

        xsl3 = xsl.rearrange("(a p) d -> p a d", p=128)
        out3 = out_d.rearrange("(a p) d -> p a d", p=128)
        ev_xslt = [None, None]
        ev_rb = [None, None]
        ev_x1_prev = [None, None]
        ev_x1tb_cp = None
        ev_tb_prev = None
        ev_strip_cp = None
        ev_glog = None
        for m in range(16):
            mp = m % 2
            w_x = [ev_rb[mp]] if ev_rb[mp] else []
            ev_xslt[mp] = dma("sync", xslt[mp], xsl3[:, m, :], waits=w_x)
            bank0, bank1 = (pb[0], pb[1]) if mp == 0 else (pb[2], pb[3])
            w_m = []
            if m == 0:
                w_m = [ev_yn_all, ev_wo, ev("dv")]
            if ev_rb[mp]:
                w_m.append(ev_rb[mp])
            for kk in range(8):
                st_, sp_ = kk == 0, kk == 7
                op("tensor",
                   lambda t, kk_=kk, m_=m, b_=bank0, st__=st_, sp__=sp_:
                   t.matmul(b_[:, :], yn[:, kk_, m_ * 128:(m_ + 1) * 128],
                            wos[:, kk_, 0:512], start=st__, stop=sp__),
                   w_m if kk == 0 else [])
                op("tensor",
                   lambda t, kk_=kk, m_=m, b_=bank1, st__=st_, sp__=sp_:
                   t.matmul(b_[:, :], yn[:, kk_, m_ * 128:(m_ + 1) * 128],
                            wos[:, kk_, 512:1024], start=st__, stop=sp__),
                   inc="pe" if kk == 7 else None)
            ev_oproj = ev("pe")
            w_r = [ev_oproj, ev_xslt[mp]]
            if ev_x1_prev[mp]:
                w_r.append(ev_x1_prev[mp])
            op("vector",
               lambda v, b_=bank0, mp_=mp:
               v.tensor_tensor(rbuf[mp_][:, 0:512], b_[:, :],
                               xslt[mp_][:, 0:512], ALU.add),
               w_r)
            ev_ra = op("vector",
                       lambda v, b_=bank1, mp_=mp:
                       v.tensor_tensor(rbuf[mp_][:, 512:1024], b_[:, :],
                                       xslt[mp_][:, 512:1024], ALU.add),
                       inc="dv")
            op("vector",
               lambda v, mp_=mp: v.bn_stats(bn6[:, 0:6], rbuf[mp_][:, 0:512]),
               [ev_ra])
            ev_bs = op("vector",
                       lambda v, mp_=mp: v.bn_stats(bn6[:, 6:12],
                                                    rbuf[mp_][:, 512:1024]),
                       inc="dv")
            ev_bn = op("vector", lambda v: v.bn_aggr(bn2, bn6), [ev_bs],
                       inc="dv")
            ev_rb[mp] = ev_bn
            ev_l1 = op("scalar",
                       lambda s: s.activation(lncol[:, 0:1], bn2[:, 1:2],
                                              AF.Ln, bias=epsc),
                       [ev_bn], inc="ac")
            ev_ln = op("scalar",
                       lambda s: s.activation(lncol[:, 1:2], lncol[:, 0:1],
                                              AF.Exp, scale=-0.5),
                       [ev_l1], inc="ac")
            ev_nmr = op("vector",
                        lambda v:
                        v.scalar_tensor_tensor(lncol[:, 2:3], bn2[:, 0:1],
                                               -1.0, lncol[:, 1:2],
                                               ALU.mult, ALU.mult),
                        [ev_ln], inc="dv")
            ev_x1 = op("scalar",
                       lambda s, m_=m, mp_=mp:
                       s.activation(acc[:, m_, :], rbuf[mp_], AF.Identity,
                                    bias=lncol[:, 2:3], scale=lncol[:, 1:2]),
                       [ev_nmr], inc="ac")
            ev_x1_prev[mp] = ev_x1
            w_c = [ev_x1]
            if ev_tb_prev:
                w_c.append(ev_tb_prev)
            ev_c = op("vector",
                      lambda v, m_=m: v.tensor_copy(x1bscr, acc[:, m_, :]),
                      w_c, inc="dv")
            for dt_i in range(8):
                w_t = []
                if dt_i == 0:
                    w_t = [ev_c]
                    if ev_x1tb_cp:
                        w_t.append(ev_x1tb_cp)
                op("tensor",
                   lambda t, d_=dt_i:
                   t.transpose(pb_bf[0][:, d_ * 128:(d_ + 1) * 128],
                               x1bscr[:, d_ * 128:(d_ + 1) * 128], idnb),
                   w_t, inc="pe" if dt_i == 7 else None)
            ev_tb_prev = ev("pe")
            ev_x1tb_cp = op(
                "vector",
                lambda v, m_=m:
                v.tensor_copy(x1tb[:, :, m_ * 128:(m_ + 1) * 128],
                              pb_bf[0].rearrange("p (a f) -> p a f", a=8)),
                [ev_tb_prev], inc="dv")
            for dt_i in range(8):
                bsel = pb[4] if dt_i < 4 else pb[5]
                w_t = []
                if dt_i == 0:
                    w_t = [ev_x1]
                    if ev_strip_cp:
                        w_t.append(ev_strip_cp)
                    if ev_glog:
                        w_t.append(ev_glog)
                op("tensor",
                   lambda t, m_=m, d_=dt_i, b_=bsel:
                   t.transpose(b_[:, (d_ % 4) * 128:(d_ % 4 + 1) * 128],
                               acc[:, m_, d_ * 128:(d_ + 1) * 128], idnf),
                   w_t, inc="pe" if dt_i == 7 else None)
            ev_tf = ev("pe")
            op("vector",
               lambda v: v.tensor_copy(
                   x1s[:, 0:4, :].rearrange("p a f -> p (a f)"), pb[4][:, :]),
               [ev_tf])
            ev_strip_cp = op(
                "vector",
                lambda v: v.tensor_copy(
                    x1s[:, 4:8, :].rearrange("p a f -> p (a f)"), pb[5][:, :]),
                inc="dv")
            for kk in range(8):
                op("tensor",
                   lambda t, kk_=kk, st__=(kk == 0), sp__=(kk == 7):
                   t.matmul(pb[4][0:128, 320:328], x1s[:, kk_, :],
                            gw_sb[:, kk_, :], start=st__, stop=sp__),
                   [ev_strip_cp] if kk == 0 else [],
                   inc="pe" if kk == 7 else None)
            ev_gmm = ev("pe")
            ev_glog = op("vector",
                         lambda v, m_=m:
                         v.tensor_copy(glog[:, m_ * 8:(m_ + 1) * 8],
                                       pb[4][:, 320:328]),
                         [ev_gmm], inc="dv")

        g3 = glog.rearrange("p (a f) -> p a f", a=16)

        def bc16(ap_):
            return ap_.rearrange("p (a f) -> p a f", a=16) \
                .broadcast_to([128, 16, 8])

        def t3(ap_):
            return ap_.rearrange("p (a f) -> p a f", a=16)

        def chain_v(fn, waits=()):
            return op("vector", fn, waits, inc="dv")

        e1 = chain_v(lambda v: v.tensor_reduce(gsc[:, 0:16], g3, AX.X,
                                               ALU.max), [ev_glog])
        ev_sub = chain_v(lambda v: v.tensor_tensor(g3, g3, bc16(gsc[:, 0:16]),
                                                   ALU.subtract), [e1])
        ev_exp = op("scalar", lambda s: s.activation(glog, glog, AF.Exp),
                    [ev_sub], inc="ac")
        e2 = chain_v(lambda v: v.tensor_reduce(gsc[:, 0:16], g3, AX.X,
                                               ALU.add), [ev_exp])
        e3 = chain_v(lambda v: v.reciprocal(gsc[:, 0:16], gsc[:, 0:16]), [e2])
        e4 = chain_v(lambda v: v.tensor_tensor(g3, g3, bc16(gsc[:, 0:16]),
                                               ALU.mult), [e3])
        e5 = chain_v(lambda v: v.tensor_tensor(
            t3(gsc[:, 16:144]), g3,
            ebias.rearrange("p (a f) -> p a f", a=1)
            .broadcast_to([128, 16, 8]), ALU.add), [e4])
        sb3 = gsc[:, 16:144].rearrange("p (a f) -> p a f", a=16)
        e6 = chain_v(lambda v: v.tensor_reduce(gsc[:, 0:16], sb3, AX.X,
                                               ALU.max), [e5])
        e7 = chain_v(lambda v: v.tensor_tensor(t3(gsc[:, 144:272]),
                                               t3(gsc[:, 16:144]),
                                               bc16(gsc[:, 0:16]), ALU.is_ge),
                     [e6])
        e8 = chain_v(lambda v: v.scalar_tensor_tensor(
            gsc[:, 144:272], gsc[:, 144:272], 1e30, gsc[:, 16:144],
            ALU.mult, ALU.subtract), [e7])
        e9 = chain_v(lambda v: v.tensor_reduce(
            gsc[:, 0:16],
            gsc[:, 144:272].rearrange("p (a f) -> p a f", a=16),
            AX.X, ALU.min), [e8])
        e10 = chain_v(lambda v: v.tensor_scalar(gsc[:, 0:16], gsc[:, 0:16],
                                                -1.0, None, ALU.mult), [e9])
        e11 = chain_v(lambda v: v.tensor_tensor(t3(gsc[:, 144:272]),
                                                t3(gsc[:, 16:144]),
                                                bc16(gsc[:, 0:16]),
                                                ALU.is_ge), [e10])
        ev_wsel = chain_v(lambda v: v.tensor_tensor(wsel, glog,
                                                    gsc[:, 144:272],
                                                    ALU.mult), [e11])
        ev_wt8 = None
        for m in range(16):
            op("tensor",
               lambda t, m_=m:
               t.transpose(pb[4][0:8, 328:456],
                           wsel[:, m_ * 8:(m_ + 1) * 8], idnf),
               [ev_wsel] if m == 0 else [ev_wt8],
               inc="pe")
            ev_tw = ev("pe")
            ev_wt8 = op("vector",
                        lambda v, m_=m:
                        v.tensor_copy(wt8[:, m_ * 128:(m_ + 1) * 128],
                                      pb[4][0:8, 328:456]),
                        [ev_tw], inc="dv")

        pairs3 = [(pb[0], pb[1]), (pb[2], pb[3]), (pb[4], pb[5])]

        def w13_src(wd, e, q):
            return wd[e:e + 1, :, HQ_OFF[q]:HQ_OFF[q] + HQ_TILES[q] * 128] \
                .rearrange("e (a p) h -> p (e a) h", p=128)

        def w2_src(e, q):
            return w2_d[e:e + 1, HQ_OFF[q]:HQ_OFF[q] + HQ_TILES[q] * 128, :] \
                .rearrange("e (a p) d -> p (e a) d", p=128)

        fetch_q = [(e, q) for e in range(NE) for q in range(4)]
        ev_fA, ev_fB, ev_fC = {}, {}, {}
        e0, q0 = fetch_q[0]
        ev_fA[(e0, q0)] = dma("gpsimd", slA(q0), w13_src(w1_d, e0, q0),
                              waits=[ev("pe")])
        ev_fB[(e0, q0)] = dma("gpsimd", slB(q0), w13_src(w3_d, e0, q0))
        ev_fC[(e0, q0)] = dma("gpsimd", slC(q0), w2_src(e0, q0))
        fetch_idx = 1

        ev_wbmm_prev = None
        ev_acc_upd = [None, None]
        last_h1_mm = None
        last_h3_mm = None
        for e in range(NE):
            w_wr = [ev_wt8] if e == 0 else []
            if ev_wbmm_prev:
                w_wr.append(ev_wbmm_prev)
            ev_wr = dma("sync", wrow, wt8[e:e + 1, :], waits=w_wr)
            for nchunk in range(4):
                op("tensor",
                   lambda t, n_=nchunk:
                   t.matmul(pb[n_][:, :], ones1,
                            wrow[0:1, n_ * 512:(n_ + 1) * 512],
                            start=True, stop=True),
                   [ev_wr, ev("dv")] if nchunk == 0 else [],
                   inc="pe" if nchunk == 3 else None)
            ev_wbmm_prev = ev("pe")
            ev_wbc = None
            for nchunk in range(4):
                ev_wbc = op("vector",
                            lambda v, n_=nchunk:
                            v.tensor_copy(wbc[:, n_ * 512:(n_ + 1) * 512],
                                          pb[n_][:, :]),
                            [ev_wbmm_prev] if nchunk == 0 else [], inc="dv")
            job = 0
            job_ev = [ev_wbc, ev_wbc, ev_wbc]
            for q in range(4):
                nt = HQ_TILES[q]
                ev_s = None
                h1_pair = None
                for tcp in range(2):
                    for hp in range(nt):
                        for mat in range(2):
                            pr = pairs3[job % 3]
                            w_j = []
                            if job_ev[job % 3] is not None:
                                w_j.append(job_ev[job % 3])
                            if hp == 0 and tcp == 0 and mat == 0:
                                w_j += [ev_fA[(e, q)], ev_fB[(e, q)], ev_wbc]
                            src = (slA(q) if mat == 0 else slB(q))
                            for kk in range(8):
                                st_, sp_ = kk == 0, kk == 7
                                op("tensor",
                                   lambda t, kk_=kk, h_=hp, tc_=tcp, s_=src,
                                          b_=pr[0], st__=st_, sp__=sp_:
                                   t.matmul(b_[:, :],
                                            s_[:, kk_, h_ * 128:(h_ + 1) * 128],
                                            x1tb[:, kk_,
                                                 tc_ * 1024:tc_ * 1024 + 512],
                                            start=st__, stop=sp__),
                                   w_j if kk == 0 else [])
                                op("tensor",
                                   lambda t, kk_=kk, h_=hp, tc_=tcp, s_=src,
                                          b_=pr[1], st__=st_, sp__=sp_:
                                   t.matmul(b_[:, :],
                                            s_[:, kk_, h_ * 128:(h_ + 1) * 128],
                                            x1tb[:, kk_, tc_ * 1024 + 512:
                                                 (tc_ + 1) * 1024],
                                            start=st__, stop=sp__),
                                   inc="pe" if kk == 7 else None)
                            if mat == 0:
                                h1_pair = pr
                                h1_job = job
                                last_h1_mm = ev("pe")
                            else:
                                last_h3_mm = ev("pe")
                                hb = tcp
                                w_h1 = [last_h3_mm]
                                if job <= 2:
                                    w_h1.append(ev_wbc)
                                op("vector",
                                   lambda v, b_=h1_pair[0], hb_=hb, tc_=tcp:
                                   v.tensor_tensor(
                                       h1w[hb_][:, 0:512], b_[:, :],
                                       wbc[:, tc_ * 1024:tc_ * 1024 + 512],
                                       ALU.mult),
                                   w_h1)
                                ev_h1w = op(
                                    "vector",
                                    lambda v, b_=h1_pair[1], hb_=hb, tc_=tcp:
                                    v.tensor_tensor(
                                        h1w[hb_][:, 512:1024], b_[:, :],
                                        wbc[:, tc_ * 1024 + 512:
                                            (tc_ + 1) * 1024], ALU.mult),
                                    inc="dv")
                                ev_t1 = op(
                                    "scalar",
                                    lambda s, hb_=hb:
                                    s.activation(t1[hb_], h1w[hb_], AF.Silu),
                                    [ev_h1w], inc="ac")
                                op("vector",
                                   lambda v, b_=pr[0], hb_=hb, h_=hp, tc_=tcp:
                                   v.tensor_tensor(
                                       smoe[:, h_,
                                            tc_ * 1024:tc_ * 1024 + 512],
                                       t1[hb_][:, 0:512], b_[:, :], ALU.mult),
                                   [ev_t1])
                                ev_s = op(
                                    "vector",
                                    lambda v, b_=pr[1], hb_=hb, h_=hp, tc_=tcp:
                                    v.tensor_tensor(
                                        smoe[:, h_, tc_ * 1024 + 512:
                                             (tc_ + 1) * 1024],
                                        t1[hb_][:, 512:1024], b_[:, :],
                                        ALU.mult),
                                    inc="dv")
                                job_ev[h1_job % 3] = ev_h1w
                                job_ev[job % 3] = ev_s
                            job += 1
                for m in range(16):
                    mp = m % 2
                    pr2 = (pb[0], pb[1]) if mp == 0 else (pb[2], pb[3])
                    w_m = []
                    if m == 0:
                        w_m = [ev_s, ev_fC[(e, q)], ev("dv")]
                    if ev_acc_upd[mp] is not None:
                        w_m.append(ev_acc_upd[mp])
                    for kk in range(nt):
                        st_, sp_ = kk == 0, kk == nt - 1
                        op("tensor",
                           lambda t, kk_=kk, m_=m, b_=pr2[0], q_=q,
                                  st__=st_, sp__=sp_:
                           t.matmul(b_[:, :],
                                    smoe[:, kk_, m_ * 128:(m_ + 1) * 128],
                                    slC(q_)[:, kk_, 0:512],
                                    start=st__, stop=sp__),
                           w_m if kk == 0 else [])
                        op("tensor",
                           lambda t, kk_=kk, m_=m, b_=pr2[1], q_=q,
                                  st__=st_, sp__=sp_:
                           t.matmul(b_[:, :],
                                    smoe[:, kk_, m_ * 128:(m_ + 1) * 128],
                                    slC(q_)[:, kk_, 512:1024],
                                    start=st__, stop=sp__),
                           inc="pe" if kk == nt - 1 else None)
                    ev_o2 = ev("pe")
                    w_st = [ev_o2]
                    if ev_acc_upd[mp] is not None:
                        w_st.append(ev_acc_upd[mp])
                    op("vector",
                       lambda v, b_=pr2[0], m_=m, e_=e:
                       v.scalar_tensor_tensor(
                           acc[:, m_, 0:512], b_[:, :],
                           wsel[:, m_ * 8 + e_:m_ * 8 + e_ + 1],
                           acc[:, m_, 0:512], ALU.mult, ALU.add),
                       w_st)
                    ev_acc_upd[mp] = op(
                        "vector",
                        lambda v, b_=pr2[1], m_=m, e_=e:
                        v.scalar_tensor_tensor(
                            acc[:, m_, 512:1024], b_[:, :],
                            wsel[:, m_ * 8 + e_:m_ * 8 + e_ + 1],
                            acc[:, m_, 512:1024], ALU.mult, ALU.add),
                        inc="dv")
                if fetch_idx < len(fetch_q):
                    en, qn = fetch_q[fetch_idx]
                    ev_fA[(en, qn)] = dma("gpsimd", slA(qn),
                                          w13_src(w1_d, en, qn),
                                          waits=[last_h1_mm])
                    ev_fB[(en, qn)] = dma("gpsimd", slB(qn),
                                          w13_src(w3_d, en, qn),
                                          waits=[last_h3_mm])
                    ev_fC[(en, qn)] = dma("gpsimd", slC(qn), w2_src(en, qn),
                                          waits=[ev("pe")])
                    fetch_idx += 1
                job_ev = [ev("dv")] * 3

        ev_out_dma = [None, None]
        for m in range(16):
            mp = m % 2
            w_m = [ev_acc_upd[0], ev_acc_upd[1]] if m == 0 else []
            op("vector",
               lambda v, m_=m: v.bn_stats(bn6[:, 0:6], acc[:, m_, 0:512]),
               w_m)
            ev_bs = op("vector",
                       lambda v, m_=m: v.bn_stats(bn6[:, 6:12],
                                                  acc[:, m_, 512:1024]),
                       inc="dv")
            ev_bn = op("vector", lambda v: v.bn_aggr(bn2, bn6), [ev_bs],
                       inc="dv")
            ev_l1 = op("scalar",
                       lambda s: s.activation(lncol[:, 0:1], bn2[:, 1:2],
                                              AF.Ln, bias=epsc),
                       [ev_bn], inc="ac")
            ev_ln = op("scalar",
                       lambda s: s.activation(lncol[:, 1:2], lncol[:, 0:1],
                                              AF.Exp, scale=-0.5),
                       [ev_l1], inc="ac")
            ev_nmr = op("vector",
                        lambda v:
                        v.scalar_tensor_tensor(lncol[:, 2:3], bn2[:, 0:1],
                                               -1.0, lncol[:, 1:2],
                                               ALU.mult, ALU.mult),
                        [ev_ln], inc="dv")
            w_o = [ev_nmr]
            if ev_out_dma[mp]:
                w_o.append(ev_out_dma[mp])
            ev_o = op("scalar",
                      lambda s, m_=m, mp_=mp:
                      s.activation(outs[mp_], acc[:, m_, :], AF.Identity,
                                   bias=lncol[:, 2:3], scale=lncol[:, 1:2]),
                      w_o, inc="ac")
            ev_out_dma[mp] = dma("sync", out3[:, m, :], outs[mp],
                                 waits=[ev_o])

        if cutoff is not None:
            for e_ in p.q:
                p.q[e_] = [o_ for o_ in p.q[e_] if o_[4] <= cutoff]
        final_ds = sum(o_[3] for o_ in p.q["sync"] if o_[2] == "ds")

        dma_waited = {"ds": set(), "dg": set()}
        for q_ in p.q.values():
            for (waits, _f, _i, _n, _sq) in q_:
                for (s_n, val) in waits:
                    if s_n in dma_waited:
                        dma_waited[s_n].add(val)
        dma_waited = {k: sorted(v) for k, v in dma_waited.items()}

        def body(name):
            own = {"sync": "ds", "gpsimd": "dg"}.get(name)
            def _(eng):
                seen = {}
                cnt = 0
                for (waits, fn, inc, n, _sq) in p.q[name]:
                    if own is not None and inc == own:
                        need = 0
                        for v_ in dma_waited[own]:
                            if v_ <= cnt:
                                need = v_
                            else:
                                break
                        if need > seen.get(own, -1):
                            eng.wait_ge(sems[own], need)
                            seen[own] = need
                        cnt += n
                    for (s_n, val) in waits:
                        if seen.get(s_n, -1) >= val:
                            continue
                        eng.wait_ge(sems[s_n], val)
                        seen[s_n] = val
                    instr = fn(eng)
                    if inc is not None:
                        instr.then_inc(sems[inc], n)
                if name == "sync":
                    eng.wait_ge(sems["ds"], final_ds)
            return _

        block.sync(body("sync"))
        block.gpsimd(body("gpsimd"))
        block.tensor(body("tensor"))
        block.vector(body("vector"))
        block.scalar(body("scalar"))
    return nc


_NC_CACHE = None


def kernel(L0_attn_x, L0_attn_W_q, L0_attn_W_k, L0_attn_W_v, L0_attn_W_o,
           L0_attn_gn_weight, L0_attn_gn_bias, L0_ffn_gate_w,
           L0_ffn_expert_bias, L0_ffn_w1, L0_ffn_w2, L0_ffn_w3):
    global _NC_CACHE
    if _NC_CACHE is None:
        _NC_CACHE = build_program()
    nc = _NC_CACHE

    x = np.asarray(L0_attn_x, np.float32)
    wq = np.ascontiguousarray(np.asarray(L0_attn_W_q, np.float32).T)
    wk = np.ascontiguousarray(np.asarray(L0_attn_W_k, np.float32).T)
    wv = np.ascontiguousarray(np.asarray(L0_attn_W_v, np.float32).T)
    wo = np.ascontiguousarray(np.asarray(L0_attn_W_o, np.float32).T)
    gw = np.ascontiguousarray(np.asarray(L0_ffn_gate_w, np.float32).T)
    pconsts = np.zeros((128, 24), np.float32)
    pconsts[:, 0:8] = np.asarray(L0_attn_gn_weight, np.float32).reshape(8, 128).T
    pconsts[:, 8:16] = np.asarray(L0_attn_gn_bias, np.float32).reshape(8, 128).T
    pconsts[:, 16:24] = np.broadcast_to(
        np.asarray(L0_ffn_expert_bias, np.float32), (128, 8))
    w1 = np.ascontiguousarray(np.asarray(L0_ffn_w1, np.float32))
    w2 = np.ascontiguousarray(np.asarray(L0_ffn_w2, np.float32))
    w3 = np.ascontiguousarray(np.asarray(L0_ffn_w3, np.float32))

    in_maps = []
    for c in range(NCORES):
        b, s = c // 4, c % 4
        in_maps.append({
            "xb": np.ascontiguousarray(x[b]),
            "xsl": np.ascontiguousarray(x[b, s * SL:(s + 1) * SL]),
            "sloff": np.array([[s * SL, 0, 0, 0]], np.int32),
            "wq": wq, "wk": wk, "wv": wv, "wo": wo, "gw": gw,
            "pconsts": pconsts, "w1": w1, "w2": w2, "w3": w3,
        })
    res = run_bass_kernel_spmd(nc, in_maps, core_ids=list(range(NCORES)))
    out = np.empty((B, T, D), np.float32)
    for c in range(NCORES):
        b, s = c // 4, c % 4
        out[b, s * SL:(s + 1) * SL] = res.results[c]["out"]
    return out


# version 08
# speedup vs baseline: 1.0484x; 1.0104x over previous
import dataclasses
from contextlib import ExitStack

import numpy as np
import ml_dtypes

import concourse.bass as bass
import concourse.mybir as mybir
from concourse.bass_utils import run_bass_kernel_spmd

F32 = mybir.dt.float32
BF16 = mybir.dt.bfloat16
I32 = mybir.dt.int32
AF = mybir.ActivationFunctionType
ALU = mybir.AluOpType
AX = mybir.AxisListType

B, T, D = 2, 8192, 1024
H, DH, NPAIR = 16, 64, 4
HLOC = 8
NBLK, CH = 8, 128
NCORES, SL = 8, 2048
HIDDEN, NE = 2816, 8
HQ_TILES = [6, 5, 6, 5]
HQ_OFF = [0, 768, 1408, 2176]
GN_N = float(DH * T)
EPS = 1e-5

O_IDNB, O_IDNF = 0, 256
O_GNW, O_GNB, O_EB = 768, 800, 832
O_ONES1 = 864
O_G128, O_HALF = 1376, 1408
O_EPSC = 1424
O_SPD, O_BPD, O_REXP, O_MEXP = 1440, 1472, 1504, 1536
O_SLOFF, O_SROW, O_SSUM, O_STK = 1568, 1600, 1728, 1760
O_WSEL = 2048
O_RBUF = 2560
O_WROW = 10752
O_GLOG, O_GSC, O_X1S = 18944, 19456, 20992
O_DT, O_GROW, O_GKT = 25088, 29184, 31232
O_S, O_SBF, O_STATS = 33280, 34304, 34816
O_KG, O_QG, O_AH, O_TRASH = 36864, 37376, 37888, 38912
O_XBLK, O_XT, O_QT, O_KT = 39936, 56320, 72704, 80896
O_VT, O_YB, O_WQ, O_WK, O_WV = 89088, 97280, 105472, 113664, 121856
O_ACC, O_X1TB = 25088, 90624
O_YSL, O_YNP, O_WO = 123392, 156160, 156160 + 32768
O_YALL = 123392
O_X1BS = 156160
O_SMOE, O_WBC = 123392, 147968
O_SLA, O_SLB, O_SLC = 156160, 168448, 180736
O_H1W, O_T1 = 193024, 201216
O_OUTS = 123392
ARENA = 209408

_BF = ml_dtypes.bfloat16


def _tables(role):
    g = 1.0 - 2.0 ** (-5.0 - np.arange(H, dtype=np.float64))
    i = np.arange(CH, dtype=np.float64)
    diff = i[None, :] - i[:, None]
    dt = np.zeros((CH, NPAIR * 2 * CH), np.float64)
    grow = np.zeros((CH, NPAIR * CH), np.float64)
    gkt = np.zeros((CH, NPAIR * CH), np.float64)
    g128 = np.zeros((CH, NPAIR), np.float64)
    for p_ in range(NPAIR):
        for half in range(2):
            h = role * HLOC + 2 * p_ + half
            m = np.where(diff >= 0, g[h] ** np.maximum(diff, 0.0), 0.0)
            dt[:, p_ * 256 + half * 128:p_ * 256 + half * 128 + 128] = m
            rows = slice(half * 64, half * 64 + 64)
            grow[rows, p_ * 128:(p_ + 1) * 128] = (g[h] ** (i + 1.0))[None, :]
            gkt[:, p_ * 128 + half * 64:p_ * 128 + half * 64 + 64] = \
                (g[h] ** (127.0 - i))[:, None]
            g128[rows, p_] = g[h] ** 128.0
    return (dt.astype(np.float32), grow.astype(np.float32),
            gkt.astype(np.float32), g128.astype(np.float32))


class Prog:
    ENGS = ("sync", "gpsimd", "tensor", "vector", "scalar")

    def __init__(self):
        self.q = {e: [] for e in self.ENGS}
        self.cnt = {"ds": 0, "dg": 0, "pe": 0, "dv": 0, "ac": 0}

    def op(self, eng, fn, waits=(), inc=None):
        n = 16 if inc in ("ds", "dg") else 1
        w = [(s, v) for (s, v) in waits if s is not None and v > 0]
        self.seq = getattr(self, "seq", 0) + 1
        self.q[eng].append((w, fn, inc, n, self.seq))
        if inc is not None:
            self.cnt[inc] += n
            return (inc, self.cnt[inc])
        return None

    def ev(self, sem):
        return (sem, self.cnt[sem])


def build_program(cutoff=None, detect_races=True):
    nc = bass.Bass("TRN2", target_bir_lowering=False,
                   detect_race_conditions=detect_races)

    xb = nc.dram_tensor("xb", [T, D], F32, kind="ExternalInput").ap()
    xsl = nc.dram_tensor("xsl", [SL, D], F32, kind="ExternalInput").ap()
    sloff = nc.dram_tensor("sloff", [1, 8], I32, kind="ExternalInput").ap()
    wq_d = nc.dram_tensor("wq", [D, 512], F32, kind="ExternalInput").ap()
    wk_d = nc.dram_tensor("wk", [D, 512], F32, kind="ExternalInput").ap()
    wv_d = nc.dram_tensor("wv", [D, 512], F32, kind="ExternalInput").ap()
    wo_d = nc.dram_tensor("wo", [D, D], F32, kind="ExternalInput").ap()
    gw_d = nc.dram_tensor("gw", [D, NE], F32, kind="ExternalInput").ap()
    pc_d = nc.dram_tensor("pconsts", [128, 16], F32, kind="ExternalInput").ap()
    tdt_d = nc.dram_tensor("tdt", [128, 1024], F32, kind="ExternalInput").ap()
    tgrow_d = nc.dram_tensor("tgrow", [128, 512], F32, kind="ExternalInput").ap()
    tgkt_d = nc.dram_tensor("tgkt", [128, 512], F32, kind="ExternalInput").ap()
    tg128_d = nc.dram_tensor("tg128", [128, 4], F32, kind="ExternalInput").ap()
    shyn = nc.dram_tensor("shyn", [128, 2 * 8 * 2048], BF16,
                          addr_space="Shared").ap()
    shfl = nc.dram_tensor("shfl", [1, 16], I32, addr_space="Shared").ap()
    w1_d = nc.dram_tensor("w1", [NE, D, HIDDEN], F32, kind="ExternalInput").ap()
    w2_d = nc.dram_tensor("w2", [NE, HIDDEN, D], F32, kind="ExternalInput").ap()
    w3_d = nc.dram_tensor("w3", [NE, D, HIDDEN], F32, kind="ExternalInput").ap()
    out_d = nc.dram_tensor("out", [SL, D], F32, kind="ExternalOutput").ap()
    yhbm = nc.dram_tensor("yhbm", [128, 4 * T], BF16).ap()

    c_idnb = nc.inline_tensor(np.eye(128).astype(_BF), name="c_idnb").ap()
    c_idnf = nc.inline_tensor(np.eye(128, dtype=np.float32), name="c_idnf").ap()
    half_np = np.zeros((128, 2), np.float32)
    half_np[0:64, 0] = 1.0
    half_np[64:128, 1] = 1.0
    c_half = nc.inline_tensor(half_np, name="c_half").ap()

    p = Prog()
    op, ev = p.op, p.ev

    with ExitStack() as ctx:
        arena = nc.alloc_sbuf_tensor("arena", [128, ARENA], mybir.dt.uint8).ap()
        gw_sb = nc.alloc_sbuf_tensor("gwsbuf", [128, 8 * NE], F32).ap() \
            .rearrange("p (a f) -> p a f", a=8)

        def v_(off, n, dtype):
            sz = mybir.dt.size(dtype)
            return arena[:, off:off + n * sz].bitcast(dtype)

        def v3(off, n, dtype, a):
            return v_(off, n, dtype).rearrange("p (a f) -> p a f", a=a)

        idnb = v_(O_IDNB, 128, BF16)
        idnf = v_(O_IDNF, 128, F32)
        gnw = v_(O_GNW, 16, F32)[:, 0:4]
        gnb = v_(O_GNW, 16, F32)[:, 4:8]
        ebias = v_(O_GNW, 16, F32)[:, 8:16]
        ones1 = v_(O_ONES1, 128, F32)[0:1, :]
        g128c = v_(O_G128, 8, F32)
        halfo = v_(O_HALF, 2, F32)
        epsc = v_(O_EPSC, 1, F32)
        spd = v_(O_SPD, 8, F32)[:, 0:4]
        bpd = v_(O_BPD, 8, F32)[:, 0:4]
        rexp = v_(O_REXP, 8, F32)[:, 0:4]
        mexp = v_(O_MEXP, 8, F32)[:, 0:4]
        sloff_sb = arena[:, O_SLOFF:O_SLOFF + 32].bitcast(I32)[0:1, 0:8]
        srow = v_(O_SROW, 32, F32)[0:1, 0:16]
        ssum = v_(O_SSUM, 2, F32)[0:8, :]
        stk = v_(O_STK, 16, F32)[:, 0:8]
        wsel = v_(O_WSEL, 128, F32)
        rbuf = [v_(O_RBUF + i * 4096, 1024, F32) for i in range(2)]
        wt8 = v_(O_RBUF, 2048, F32)[0:8, :]
        wrow = v_(O_WROW, 2048, F32)[0:1, :]
        xslt = [v_(O_WROW + i * 4096, 1024, F32) for i in range(2)]
        glog = v_(O_GLOG, 128, F32)
        gsc = v_(O_GSC, 384, F32)
        bn6 = gsc[:, 288:300]
        bn2 = gsc[:, 304:306]
        lncol = gsc[:, 308:312]
        x1s = v3(O_X1S, 1024, F32, 8)
        dtt = v3(O_DT, 1024, F32, 4)
        growt = v3(O_GROW, 512, F32, 4)
        gktt = v3(O_GKT, 512, F32, 4)
        st = v3(O_S, 256, F32, 4)
        sbf = v3(O_SBF, 256, BF16, 4)
        stats = v_(O_STATS, 512, F32)
        kg = [v_(O_KG + i * 256, 128, BF16) for i in range(2)]
        qg = [v_(O_QG + i * 256, 128, BF16) for i in range(2)]
        ah = [v_(O_AH + i * 512, 256, BF16) for i in range(2)]
        trash = [v_(O_TRASH, 128, BF16), v_(O_TRASH + 256, 128, BF16)]
        xblk = v3(O_XBLK, 8192, BF16, 8)
        xt = v3(O_XT, 8192, BF16, 8)
        qt = v3(O_QT, 4096, BF16, 4)
        kt = v3(O_KT, 4096, BF16, 4)
        vt = v3(O_VT, 4096, BF16, 8)
        yb = v3(O_YB, 4096, BF16, 4)
        wqs = v3(O_WQ, 4096, BF16, 8)
        wks = v3(O_WK, 4096, BF16, 8)
        wvs = v3(O_WV, 4096, BF16, 8)
        acc = v3(O_ACC, 16384, F32, 16)
        x1tb = v3(O_X1TB, 16384, BF16, 8)
        ysl = v3(O_YSL, 16384, BF16, 4)
        ynp = v3(O_YNP, 16384, BF16, 4)
        yn = v3(O_YALL, 16384, BF16, 8)
        x1bscr = v_(O_X1BS, 1024, BF16)
        wos = v3(O_WO, 8192, BF16, 8)
        smoe = v3(O_SMOE, 12288, BF16, 6)
        wbc = v_(O_WBC, 2048, F32)
        h1w = [v_(O_H1W + i * 4096, 1024, F32) for i in range(2)]
        t1 = [v_(O_T1 + i * 4096, 1024, F32) for i in range(2)]
        outs = [v_(O_OUTS + i * 4096, 1024, F32) for i in range(2)]

        def slA(q):
            return v3(O_SLA, 8 * HQ_TILES[q] * 128, BF16, 8)

        def slB(q):
            return v3(O_SLB, 8 * HQ_TILES[q] * 128, BF16, 8)

        def slC(q):
            return v3(O_SLC, HQ_TILES[q] * 1024, BF16, HQ_TILES[q])

        pb_bf = [ctx.enter_context(nc.psum_tensor(f"pbb{i}", [128, 1024], BF16))
                 for i in range(2)]
        pb = [ctx.enter_context(nc.psum_tensor(f"pb{i}", [128, 512], F32))
              for i in range(6)]

        sems = {s: ctx.enter_context(nc.semaphore(s))
                for s in ("ds", "dg", "pe", "dv", "ac", "ex")}
        block = ctx.enter_context(nc.Block())

        def dma(eng, out, in_, waits=()):
            return op(eng, lambda e, o=out, i=in_: e.dma_start(out=o, in_=i),
                      waits, inc="ds" if eng == "sync" else "dg")

        dma("sync", v_(O_DT, 1024, F32), tdt_d)
        dma("sync", v_(O_GROW, 512, F32), tgrow_d)
        dma("sync", v_(O_GKT, 512, F32), tgkt_d)
        dma("sync", g128c[:, 0:4], tg128_d)
        dma("sync", idnb, c_idnb)
        dma("sync", idnf, c_idnf)
        dma("sync", halfo, c_half)
        dma("sync", arena[:, O_GNW:O_GNW + 64].bitcast(F32), pc_d)
        dma("sync", sloff_sb, sloff)
        ev_consts = dma("sync", gw_sb, gw_d.rearrange("(a p) f -> p a f", p=128))
        for wsb, wdr in ((wqs, wq_d), (wks, wk_d), (wvs, wv_d)):
            dma("gpsimd", wsb, wdr.rearrange("(a p) f -> p a f", p=128))
        ev_wqkv = ev("dg")

        op("vector", lambda v: v.memset(v_(O_S, 256, F32), 0.0))
        op("vector", lambda v: v.memset(epsc, EPS))
        op("vector", lambda v: v.memset(v_(O_SBF, 256, BF16), 0.0))
        ev_sbzero = op("vector", lambda v: v.memset(ones1, 1.0), inc="dv")

        xb3 = xb.rearrange("(a p) d -> p a d", p=128)
        yhbm3 = yhbm.rearrange("p (a t) -> p a t", a=4)

        ev_par_post = [ev_sbzero, ev_sbzero]
        ev_par_sq = [None, None]
        ev_kg_last = [ev_sbzero, ev_sbzero]
        ev_yb_dma = None
        ev_projbank = {0: None, 1: None}
        ev_xtrans_done = None
        unit_list = [(c_, j_) for c_ in range(8) for j_ in range(NPAIR)]
        projs = [(wqs, qt, 0.125, False), (wks, kt, None, False),
                 (wvs, vt, None, True)]

        for k in range(NBLK):
            w_ = [ev_consts]
            if ev_xtrans_done is not None:
                w_.append(ev_xtrans_done)
            evx = dma("gpsimd", xblk, xb3[:, 8 * k:8 * k + 8, :], waits=w_)

            ev_xt_cp = [None, None]
            for tt in range(8):
                par = tt % 2
                w_t = []
                if tt == 0:
                    w_t = [evx, ev_consts]
                if tt < 2:
                    w_t.append(ev_kg_last[par])
                else:
                    w_t.append(ev_xt_cp[par])
                for dt_i in range(8):
                    op("tensor",
                       lambda t, tt_=tt, d_=dt_i, par_=par:
                       t.transpose(pb_bf[par_][:, d_ * 128:(d_ + 1) * 128],
                                   xblk[:, tt_, d_ * 128:(d_ + 1) * 128],
                                   idnb),
                       w_t if dt_i == 0 else [],
                       inc="pe" if dt_i == 7 else None)
                ev_tt = ev("pe")
                ev_xt_cp[par] = op(
                    "vector",
                    lambda v, tt_=tt, par_=par:
                    v.tensor_copy(xt[:, :, tt_ * 128:(tt_ + 1) * 128],
                                  pb_bf[par_].rearrange("p (a f) -> p a f",
                                                        a=8)),
                    [ev_tt], inc="dv")
            ev_xtrans_done = ev("pe")
            ev_xt_all = ev_xt_cp[1]

            for which, (wsb, dst, scale, tok_major) in enumerate(projs):
                n_m = 8 if tok_major else 4
                for m in range(n_m):
                    mp = m % 2
                    bank0, bank1 = (pb[0], pb[1]) if mp == 0 else (pb[2], pb[3])
                    w_m = []
                    if which == 0 and m == 0:
                        w_m = [ev_xt_all, ev_wqkv]
                        if ev_par_sq[0] is not None:
                            w_m += [ev_par_sq[0], ev_par_sq[1]]
                        w_m.append(ev("dv"))
                    if ev_projbank[mp] is not None:
                        w_m.append(ev_projbank[mp])
                    if tok_major:
                        for kk in range(8):
                            st_, sp_ = kk == 0, kk == 7
                            op("tensor",
                               lambda t, kk_=kk, m_=m, b_=bank0, w__=wsb,
                                      st__=st_, sp__=sp_:
                               t.matmul(b_[:, :],
                                        xt[:, kk_, m_ * 128:(m_ + 1) * 128],
                                        w__[:, kk_, 0:512],
                                        start=st__, stop=sp__),
                               w_m if kk == 0 else [],
                               inc="pe" if kk == 7 else None)
                        ev_mm = ev("pe")
                        ev_projbank[mp] = op(
                            "vector",
                            lambda v, b_=bank0, m_=m:
                            v.tensor_copy(vt[:, m_, :], b_[:, :]),
                            [ev_mm], inc="dv")
                        continue
                    for kk in range(8):
                        st_, sp_ = kk == 0, kk == 7
                        l0 = lambda kk_, m_, w__=wsb: \
                            w__[:, kk_, m_ * 128:(m_ + 1) * 128]
                        r0 = lambda kk_, n_: xt[:, kk_,
                                                n_ * 512:(n_ + 1) * 512]
                        op("tensor",
                           lambda t, kk_=kk, m_=m, b_=bank0, l_=l0, r_=r0,
                                  st__=st_, sp__=sp_:
                           t.matmul(b_[:, :], l_(kk_, m_), r_(kk_, 0),
                                    start=st__, stop=sp__),
                           w_m if kk == 0 else [])
                        op("tensor",
                           lambda t, kk_=kk, m_=m, b_=bank1, l_=l0, r_=r0,
                                  st__=st_, sp__=sp_:
                           t.matmul(b_[:, :], l_(kk_, m_), r_(kk_, 1),
                                    start=st__, stop=sp__),
                           inc="pe" if kk == 7 else None)
                    ev_mm = ev("pe")

                    def evac(v, b_, dst_, m_, n_, sc_):
                        o_ = dst_[:, m_, n_ * 512:(n_ + 1) * 512]
                        if sc_ is None:
                            return v.tensor_copy(o_, b_[:, :])
                        return v.tensor_scalar(o_, b_[:, :], sc_, None,
                                               ALU.mult)
                    op("vector",
                       lambda v, b_=bank0, d_=dst, m_=m, sc_=scale:
                       evac(v, b_, d_, m_, 0, sc_), [ev_mm])
                    ev_projbank[mp] = op(
                        "vector",
                        lambda v, b_=bank1, d_=dst, m_=m, sc_=scale:
                        evac(v, b_, d_, m_, 1, sc_), inc="dv")
            ev_projevac = ev("dv")

            NU = len(unit_list)

            def emit_TA(u):
                c_, j_ = unit_list[u]
                par = u % 2
                w_ta = [ev_par_post[par], ev_kg_last[par]]
                if ev_par_sq[par] is not None:
                    w_ta.append(ev_par_sq[par])
                if u == 0:
                    w_ta.append(ev_projevac)
                op("tensor",
                   lambda t, c__=c_, j__=j_, par_=par:
                   t.transpose(pb_bf[par_][:, 0:128],
                               kt[:, j__, c__ * 128:(c__ + 1) * 128], idnb),
                   w_ta)
                op("tensor",
                   lambda t, c__=c_, j__=j_, par_=par:
                   t.matmul(pb[4 + par_][0:128, 0:128],
                            kt[0:64, j__, c__ * 128:(c__ + 1) * 128],
                            qt[0:64, j__, c__ * 128:(c__ + 1) * 128],
                            start=True, stop=True))
                return op("tensor",
                          lambda t, c__=c_, j__=j_, par_=par:
                          t.matmul(pb[2 + par_][0:128, 0:128],
                                   kt[64:128, j__, c__ * 128:(c__ + 1) * 128],
                                   qt[64:128, j__, c__ * 128:(c__ + 1) * 128],
                                   start=True, stop=True,
                                   tile_position=(64, 0)),
                          inc="pe")

            def emit_pre(u, ev_ta):
                c_, j_ = unit_list[u]
                par = u % 2
                ev_kg_last[par] = op(
                    "vector",
                    lambda v, j__=j_, par_=par:
                    v.tensor_tensor(kg[par_], pb_bf[par_][:, 0:128],
                                    gktt[:, j__, :], ALU.mult),
                    [ev_ta], inc="dv")
                op("vector",
                   lambda v, j__=j_, par_=par:
                   v.tensor_tensor(ah[par_][:, 0:128], pb[4 + par_][:, 0:128],
                                   dtt[:, j__, 0:128], ALU.mult))
                op("vector",
                   lambda v, j__=j_, par_=par:
                   v.tensor_tensor(ah[par_][:, 128:256], pb[2 + par_][:, 0:128],
                                   dtt[:, j__, 128:256], ALU.mult))
                return op("vector",
                          lambda v, c__=c_, j__=j_, par_=par:
                          v.tensor_tensor(qg[par_],
                                          qt[:, j__,
                                             c__ * 128:(c__ + 1) * 128],
                                          growt[:, j__, :], ALU.mult),
                          inc="dv")

            def emit_YU(u, ev_pre):
                c_, j_ = unit_list[u]
                par = u % 2
                y_ = pb[par]
                op("tensor",
                   lambda t, c__=c_, j__=j_, par_=par, yy=y_:
                   t.matmul(yy[0:64, 0:128],
                            vt[:, c__, j__ * 128:j__ * 128 + 64],
                            ah[par_][:, 0:128], start=True, stop=False),
                   [ev_pre])
                op("tensor",
                   lambda t, j__=j_, par_=par, yy=y_:
                   t.matmul(yy[0:64, 0:128], sbf[0:64, j__, :],
                            qg[par_][0:64, :], start=False, stop=True))
                op("tensor",
                   lambda t, c__=c_, j__=j_, par_=par, yy=y_:
                   t.matmul(yy[64:128, 0:128],
                            vt[:, c__, j__ * 128 + 64:(j__ + 1) * 128],
                            ah[par_][:, 128:256], start=True, stop=False,
                            tile_position=(0, 64)))
                op("tensor",
                   lambda t, j__=j_, par_=par, yy=y_:
                   t.matmul(yy[64:128, 0:128], sbf[64:128, j__, :],
                            qg[par_][64:128, :], start=False, stop=True,
                            tile_position=(64, 64)))
                op("tensor",
                   lambda t, c__=c_, j__=j_, par_=par:
                   t.matmul(pb[4 + par_][0:64, 256:320], kg[par_][:, 0:64],
                            vt[:, c__, j__ * 128:j__ * 128 + 64],
                            start=True, stop=True))
                return op("tensor",
                          lambda t, c__=c_, j__=j_, par_=par:
                          t.matmul(pb[4 + par_][64:128, 256:320],
                                   kg[par_][:, 64:128],
                                   vt[:, c__, j__ * 128 + 64:(j__ + 1) * 128],
                                   start=True, stop=True,
                                   tile_position=(0, 64)),
                          inc="pe")

            def emit_post(u, ev_yu, mglob):
                c_, j_ = unit_list[u]
                par = u % 2
                y_ = pb[par]
                w_p = [ev_yu]
                if u == 0 and ev_yb_dma is not None:
                    w_p.append(ev_yb_dma)
                ev_ye = op("vector",
                           lambda v, c__=c_, j__=j_, yy=y_, m_=mglob:
                           v.tensor_scalar(
                               yb[:, j__, c__ * 128:(c__ + 1) * 128],
                               yy[:, 0:128], 1.0, None, ALU.mult, ALU.add,
                               accum_out=stats[:, j__ * 64 + m_:
                                               j__ * 64 + m_ + 1]),
                           w_p, inc="dv")
                ev_su = op("vector",
                           lambda v, j__=j_, par_=par:
                           v.scalar_tensor_tensor(st[:, j__, :], st[:, j__, :],
                                                  g128c[:, j__:j__ + 1],
                                                  pb[4 + par_][:, 256:320],
                                                  ALU.mult, ALU.add),
                           inc="dv")
                ev_par_post[par] = op(
                    "vector",
                    lambda v, j__=j_: v.tensor_copy(sbf[:, j__, :],
                                                    st[:, j__, :]),
                    [ev_su], inc="dv")
                ev_par_sq[par] = op(
                    "scalar",
                    lambda s, j__=j_, yy=y_, m_=mglob, par_=par:
                    s.activation(trash[par_], yy[:, 0:128], AF.Square,
                                 accum_out=stats[:, 256 + j__ * 64 + m_:
                                                 256 + j__ * 64 + m_ + 1]),
                    [ev_ye], inc="ac")

            ev_ta_l = [None] * (NU + 1)
            ev_pre_l = [None] * (NU + 1)
            ev_ta_l[0] = emit_TA(0)
            ev_pre_l[0] = emit_pre(0, ev_ta_l[0])
            for u in range(NU):
                if u + 1 < NU:
                    ev_ta_l[u + 1] = emit_TA(u + 1)
                ev_yu = emit_YU(u, ev_pre_l[u])
                if u + 1 < NU:
                    ev_pre_l[u + 1] = emit_pre(u + 1, ev_ta_l[u + 1])
                emit_post(u, ev_yu, 8 * k + unit_list[u][0])

            ev_yb_dma = dma("sync", yhbm3[:, :, k * 1024:(k + 1) * 1024], yb,
                            waits=[ev("dv")])

        ev_wo = dma("gpsimd", wos, wo_d.rearrange("(a p) f -> p a f", p=128),
                    waits=[ev("pe")])

        op("vector",
           lambda v: v.tensor_reduce(
               stk[:, 0:4],
               stats[:, 0:256].rearrange("p (a f) -> p a f", a=4),
               AX.X, ALU.add),
           [ev("dv"), ev("ac")])
        ev_red = op("vector",
                    lambda v: v.tensor_reduce(
                        stk[:, 4:8],
                        stats[:, 256:512].rearrange("p (a f) -> p a f", a=4),
                        AX.X, ALU.add),
                    inc="dv")
        ev_gs = op("tensor",
                   lambda t: t.matmul(pb[4][0:8, 0:2], stk, halfo,
                                      start=True, stop=True),
                   [ev_red], inc="pe")
        ev_sc = op("vector", lambda v: v.tensor_copy(ssum, pb[4][0:8, 0:2]),
                   [ev_gs], inc="dv")
        ev_sr = dma("sync", srow.rearrange("q (a f) -> q a f", a=8), ssum,
                    waits=[ev_sc])
        ev_g1 = op("vector",
           lambda v: v.tensor_scalar(srow, srow, 1.0 / GN_N, None, ALU.mult),
           [ev_sr], inc="dv")
        ev_g2 = op("vector",
           lambda v: v.tensor_tensor(gsc[0:1, 0:8], srow[0:1, 0:8],
                                     srow[0:1, 0:8], ALU.mult),
           [ev_g1], inc="dv")
        ev_var = op("vector",
                    lambda v: v.tensor_tensor(gsc[0:1, 16:24],
                                              srow[0:1, 8:16],
                                              gsc[0:1, 0:8], ALU.subtract),
                    [ev_g2], inc="dv")
        ev_gln = op("scalar",
           lambda s: s.activation(gsc[0:1, 32:40], gsc[0:1, 16:24], AF.Ln,
                                  bias=epsc[0:1, :]),
           [ev_var], inc="ac")
        ev_rstd = op("scalar",
                     lambda s: s.activation(gsc[0:1, 48:56], gsc[0:1, 32:40],
                                            AF.Exp, scale=-0.5),
                     [ev_gln], inc="ac")
        exp_srcs = [gsc[0:1, 48:56][:, 0:8:2], gsc[0:1, 48:56][:, 1:8:2],
                    srow[0:1, 0:8][:, 0:8:2], srow[0:1, 0:8][:, 1:8:2]]
        for i_, src_ in enumerate(exp_srcs):
            op("tensor",
               lambda t, i__=i_, s_=src_:
               t.matmul(pb[5][:, 328 + i__ * 4:332 + i__ * 4], ones1, s_,
                        start=True, stop=True),
               [ev_rstd] if i_ == 0 else [],
               inc="pe" if i_ == 3 else None)
        ev_expmm = ev("pe")
        op("vector",
           lambda v: v.tensor_copy(rexp[0:64, :], pb[5][0:64, 328:332]),
           [ev_expmm])
        op("vector",
           lambda v: v.tensor_copy(rexp[64:128, :], pb[5][64:128, 332:336]))
        op("vector",
           lambda v: v.tensor_copy(mexp[0:64, :], pb[5][0:64, 336:340]))
        ev_e4 = op("vector",
                   lambda v: v.tensor_copy(mexp[64:128, :],
                                           pb[5][64:128, 340:344]),
                   inc="dv")
        ev_a1 = op("vector", lambda v: v.tensor_tensor(spd, gnw, rexp,
                                                       ALU.mult),
                   [ev_e4], inc="dv")
        ev_a2 = op("vector", lambda v: v.tensor_tensor(gsc[:, 64:68], mexp,
                                                       spd, ALU.mult),
                   [ev_a1], inc="dv")
        ev_aff = op("vector",
                    lambda v: v.tensor_tensor(bpd, gnb, gsc[:, 64:68],
                                              ALU.subtract),
                    [ev_a2], inc="dv")

        def dyn_read(g):
            r = g.alloc_register("roff")
            g.reg_load(r, sloff_sb[0:1, 0:1])
            src_ = dataclasses.replace(yhbm3[:, :, 0:2 * SL], offset=r)
            return g.dma_start(out=ysl, in_=src_)

        ev_ysl = op("gpsimd", dyn_read, [ev_consts, ev_yb_dma], inc="dg")

        for dt_i in range(4):
            op("scalar",
               lambda s, d_=dt_i:
               s.activation(ynp[:, d_, :], ysl[:, d_, :], AF.Identity,
                            bias=bpd[:, d_:d_ + 1], scale=spd[:, d_:d_ + 1]),
               [ev_ysl, ev_aff] if dt_i == 0 else [],
               inc="ac" if dt_i == 3 else None)
        ev_ynp = ev("ac")

        shyn_f = shyn
        ex_sem = sems["ex"]

        def exchange(g):
            regs = {}
            for i_, nm in enumerate(["rw0", "rw1", "rrd", "rfw", "rfr"]):
                regs[nm] = g.alloc_register(nm)
                g.reg_load(regs[nm], sloff_sb[0:1, 1 + i_:2 + i_])
            r_nc = g.alloc_register("r_nc")
            g.reg_load(r_nc, sloff_sb[0:1, 6:7])
            g.dma_start(out=dataclasses.replace(shyn_f[:, 0:8192],
                                                offset=regs["rw0"]),
                        in_=ynp[:, :, 0:2048]).then_inc(ex_sem, 16)
            g.dma_start(out=dataclasses.replace(shyn_f[:, 0:8192],
                                                offset=regs["rw1"]),
                        in_=ynp[:, :, 2048:4096]).then_inc(ex_sem, 16)
            g.wait_ge(ex_sem, 32)
            g.dma_start(out=dataclasses.replace(shfl[0:1, 0:1],
                                                offset=regs["rfw"]),
                        in_=sloff_sb[0:1, 6:7]).then_inc(ex_sem, 16)
            g.wait_ge(ex_sem, 48)
            r_cur = g.alloc_register("r_cur")
            r_tgt = g.alloc_register("r_tgt")
            r_df = g.alloc_register("r_df")
            g.reg_mov(r_cur, 0)
            g.reg_mov(r_tgt, 48)
            g.reg_alu(r_df, r_cur, r_nc, ALU.bitwise_xor)
            poll_sb = arena[:, O_SLOFF + 28:O_SLOFF + 32].bitcast(I32)[0:1, :]
            ap_fr = dataclasses.replace(shfl[0:1, 0:1], offset=regs["rfr"])
            with g.While(r_df):
                g.reg_add(r_tgt, r_tgt, 16)
                g.dma_start(out=poll_sb, in_=ap_fr).then_inc(ex_sem, 16)
                g.wait_ge(ex_sem, r_tgt)
                g.reg_load(r_cur, poll_sb[0:1, 0:1])
                g.reg_alu(r_df, r_cur, r_nc, ALU.bitwise_xor)
            return g.dma_start(out=yn.rearrange("p a f -> p (a f)"),
                               in_=dataclasses.replace(shyn_f[:, 0:16384],
                                                       offset=regs["rrd"]))

        ev_yn_all = op("gpsimd", exchange, [ev_ynp], inc="dg")

        xsl3 = xsl.rearrange("(a p) d -> p a d", p=128)
        out3 = out_d.rearrange("(a p) d -> p a d", p=128)
        ev_xslt = [None, None]
        ev_rb = [None, None]
        ev_x1_prev = [None, None]
        ev_x1tb_cp = None
        ev_tb_prev = None
        ev_strip_cp = None
        ev_glog = None
        for m in range(16):
            mp = m % 2
            w_x = [ev_rb[mp]] if ev_rb[mp] else []
            ev_xslt[mp] = dma("sync", xslt[mp], xsl3[:, m, :], waits=w_x)
            bank0, bank1 = (pb[0], pb[1]) if mp == 0 else (pb[2], pb[3])
            w_m = []
            if m == 0:
                w_m = [ev_yn_all, ev_wo, ev("dv")]
            if ev_rb[mp]:
                w_m.append(ev_rb[mp])
            for kk in range(8):
                st_, sp_ = kk == 0, kk == 7
                op("tensor",
                   lambda t, kk_=kk, m_=m, b_=bank0, st__=st_, sp__=sp_:
                   t.matmul(b_[:, :], yn[:, kk_, m_ * 128:(m_ + 1) * 128],
                            wos[:, kk_, 0:512], start=st__, stop=sp__),
                   w_m if kk == 0 else [])
                op("tensor",
                   lambda t, kk_=kk, m_=m, b_=bank1, st__=st_, sp__=sp_:
                   t.matmul(b_[:, :], yn[:, kk_, m_ * 128:(m_ + 1) * 128],
                            wos[:, kk_, 512:1024], start=st__, stop=sp__),
                   inc="pe" if kk == 7 else None)
            ev_oproj = ev("pe")
            w_r = [ev_oproj, ev_xslt[mp]]
            if ev_x1_prev[mp]:
                w_r.append(ev_x1_prev[mp])
            op("vector",
               lambda v, b_=bank0, mp_=mp:
               v.tensor_tensor(rbuf[mp_][:, 0:512], b_[:, :],
                               xslt[mp_][:, 0:512], ALU.add),
               w_r)
            ev_ra = op("vector",
                       lambda v, b_=bank1, mp_=mp:
                       v.tensor_tensor(rbuf[mp_][:, 512:1024], b_[:, :],
                                       xslt[mp_][:, 512:1024], ALU.add),
                       inc="dv")
            op("vector",
               lambda v, mp_=mp: v.bn_stats(bn6[:, 0:6], rbuf[mp_][:, 0:512]),
               [ev_ra])
            ev_bs = op("vector",
                       lambda v, mp_=mp: v.bn_stats(bn6[:, 6:12],
                                                    rbuf[mp_][:, 512:1024]),
                       inc="dv")
            ev_bn = op("vector", lambda v: v.bn_aggr(bn2, bn6), [ev_bs],
                       inc="dv")
            ev_rb[mp] = ev_bn
            ev_l1 = op("scalar",
                       lambda s: s.activation(lncol[:, 0:1], bn2[:, 1:2],
                                              AF.Ln, bias=epsc),
                       [ev_bn], inc="ac")
            ev_ln = op("scalar",
                       lambda s: s.activation(lncol[:, 1:2], lncol[:, 0:1],
                                              AF.Exp, scale=-0.5),
                       [ev_l1], inc="ac")
            ev_nmr = op("vector",
                        lambda v:
                        v.scalar_tensor_tensor(lncol[:, 2:3], bn2[:, 0:1],
                                               -1.0, lncol[:, 1:2],
                                               ALU.mult, ALU.mult),
                        [ev_ln], inc="dv")
            ev_x1 = op("scalar",
                       lambda s, m_=m, mp_=mp:
                       s.activation(acc[:, m_, :], rbuf[mp_], AF.Identity,
                                    bias=lncol[:, 2:3], scale=lncol[:, 1:2]),
                       [ev_nmr], inc="ac")
            ev_x1_prev[mp] = ev_x1
            w_c = [ev_x1]
            if ev_tb_prev:
                w_c.append(ev_tb_prev)
            ev_c = op("vector",
                      lambda v, m_=m: v.tensor_copy(x1bscr, acc[:, m_, :]),
                      w_c, inc="dv")
            for dt_i in range(8):
                w_t = []
                if dt_i == 0:
                    w_t = [ev_c]
                    if ev_x1tb_cp:
                        w_t.append(ev_x1tb_cp)
                op("tensor",
                   lambda t, d_=dt_i:
                   t.transpose(pb_bf[0][:, d_ * 128:(d_ + 1) * 128],
                               x1bscr[:, d_ * 128:(d_ + 1) * 128], idnb),
                   w_t, inc="pe" if dt_i == 7 else None)
            ev_tb_prev = ev("pe")
            ev_x1tb_cp = op(
                "vector",
                lambda v, m_=m:
                v.tensor_copy(x1tb[:, :, m_ * 128:(m_ + 1) * 128],
                              pb_bf[0].rearrange("p (a f) -> p a f", a=8)),
                [ev_tb_prev], inc="dv")
            for dt_i in range(8):
                bsel = pb[4] if dt_i < 4 else pb[5]
                w_t = []
                if dt_i == 0:
                    w_t = [ev_x1]
                    if ev_strip_cp:
                        w_t.append(ev_strip_cp)
                    if ev_glog:
                        w_t.append(ev_glog)
                op("tensor",
                   lambda t, m_=m, d_=dt_i, b_=bsel:
                   t.transpose(b_[:, (d_ % 4) * 128:(d_ % 4 + 1) * 128],
                               acc[:, m_, d_ * 128:(d_ + 1) * 128], idnf),
                   w_t, inc="pe" if dt_i == 7 else None)
            ev_tf = ev("pe")
            op("vector",
               lambda v: v.tensor_copy(
                   x1s[:, 0:4, :].rearrange("p a f -> p (a f)"), pb[4][:, :]),
               [ev_tf])
            ev_strip_cp = op(
                "vector",
                lambda v: v.tensor_copy(
                    x1s[:, 4:8, :].rearrange("p a f -> p (a f)"), pb[5][:, :]),
                inc="dv")
            for kk in range(8):
                op("tensor",
                   lambda t, kk_=kk, st__=(kk == 0), sp__=(kk == 7):
                   t.matmul(pb[4][0:128, 320:328], x1s[:, kk_, :],
                            gw_sb[:, kk_, :], start=st__, stop=sp__),
                   [ev_strip_cp] if kk == 0 else [],
                   inc="pe" if kk == 7 else None)
            ev_gmm = ev("pe")
            ev_glog = op("vector",
                         lambda v, m_=m:
                         v.tensor_copy(glog[:, m_ * 8:(m_ + 1) * 8],
                                       pb[4][:, 320:328]),
                         [ev_gmm], inc="dv")

        g3 = glog.rearrange("p (a f) -> p a f", a=16)

        def bc16(ap_):
            return ap_.rearrange("p (a f) -> p a f", a=16) \
                .broadcast_to([128, 16, 8])

        def t3(ap_):
            return ap_.rearrange("p (a f) -> p a f", a=16)

        def chain_v(fn, waits=()):
            return op("vector", fn, waits, inc="dv")

        e1 = chain_v(lambda v: v.tensor_reduce(gsc[:, 0:16], g3, AX.X,
                                               ALU.max), [ev_glog])
        ev_sub = chain_v(lambda v: v.tensor_tensor(g3, g3, bc16(gsc[:, 0:16]),
                                                   ALU.subtract), [e1])
        ev_exp = op("scalar", lambda s: s.activation(glog, glog, AF.Exp),
                    [ev_sub], inc="ac")
        e2 = chain_v(lambda v: v.tensor_reduce(gsc[:, 0:16], g3, AX.X,
                                               ALU.add), [ev_exp])
        e3 = chain_v(lambda v: v.reciprocal(gsc[:, 0:16], gsc[:, 0:16]), [e2])
        e4 = chain_v(lambda v: v.tensor_tensor(g3, g3, bc16(gsc[:, 0:16]),
                                               ALU.mult), [e3])
        e5 = chain_v(lambda v: v.tensor_tensor(
            t3(gsc[:, 16:144]), g3,
            ebias.rearrange("p (a f) -> p a f", a=1)
            .broadcast_to([128, 16, 8]), ALU.add), [e4])
        sb3 = gsc[:, 16:144].rearrange("p (a f) -> p a f", a=16)
        e6 = chain_v(lambda v: v.tensor_reduce(gsc[:, 0:16], sb3, AX.X,
                                               ALU.max), [e5])
        e7 = chain_v(lambda v: v.tensor_tensor(t3(gsc[:, 144:272]),
                                               t3(gsc[:, 16:144]),
                                               bc16(gsc[:, 0:16]), ALU.is_ge),
                     [e6])
        e8 = chain_v(lambda v: v.scalar_tensor_tensor(
            gsc[:, 144:272], gsc[:, 144:272], 1e30, gsc[:, 16:144],
            ALU.mult, ALU.subtract), [e7])
        e9 = chain_v(lambda v: v.tensor_reduce(
            gsc[:, 0:16],
            gsc[:, 144:272].rearrange("p (a f) -> p a f", a=16),
            AX.X, ALU.min), [e8])
        e10 = chain_v(lambda v: v.tensor_scalar(gsc[:, 0:16], gsc[:, 0:16],
                                                -1.0, None, ALU.mult), [e9])
        e11 = chain_v(lambda v: v.tensor_tensor(t3(gsc[:, 144:272]),
                                                t3(gsc[:, 16:144]),
                                                bc16(gsc[:, 0:16]),
                                                ALU.is_ge), [e10])
        ev_wsel = chain_v(lambda v: v.tensor_tensor(wsel, glog,
                                                    gsc[:, 144:272],
                                                    ALU.mult), [e11])
        ev_wt8 = None
        for m in range(16):
            op("tensor",
               lambda t, m_=m:
               t.transpose(pb[4][0:8, 328:456],
                           wsel[:, m_ * 8:(m_ + 1) * 8], idnf),
               [ev_wsel] if m == 0 else [ev_wt8],
               inc="pe")
            ev_tw = ev("pe")
            ev_wt8 = op("vector",
                        lambda v, m_=m:
                        v.tensor_copy(wt8[:, m_ * 128:(m_ + 1) * 128],
                                      pb[4][0:8, 328:456]),
                        [ev_tw], inc="dv")

        pairs3 = [(pb[0], pb[1]), (pb[2], pb[3]), (pb[4], pb[5])]

        def w13_src(wd, e, q):
            return wd[e:e + 1, :, HQ_OFF[q]:HQ_OFF[q] + HQ_TILES[q] * 128] \
                .rearrange("e (a p) h -> p (e a) h", p=128)

        def w2_src(e, q):
            return w2_d[e:e + 1, HQ_OFF[q]:HQ_OFF[q] + HQ_TILES[q] * 128, :] \
                .rearrange("e (a p) d -> p (e a) d", p=128)

        fetch_q = [(e, q) for e in range(NE) for q in range(4)]
        ev_fA, ev_fB, ev_fC = {}, {}, {}
        e0, q0 = fetch_q[0]
        ev_fA[(e0, q0)] = dma("gpsimd", slA(q0), w13_src(w1_d, e0, q0),
                              waits=[ev("pe")])
        ev_fB[(e0, q0)] = dma("gpsimd", slB(q0), w13_src(w3_d, e0, q0))
        ev_fC[(e0, q0)] = dma("gpsimd", slC(q0), w2_src(e0, q0))
        fetch_idx = 1

        ev_wbmm_prev = None
        ev_acc_upd = [None, None]
        last_h1_mm = None
        last_h3_mm = None
        for e in range(NE):
            w_wr = [ev_wt8] if e == 0 else []
            if ev_wbmm_prev:
                w_wr.append(ev_wbmm_prev)
            ev_wr = dma("sync", wrow, wt8[e:e + 1, :], waits=w_wr)
            for nchunk in range(4):
                op("tensor",
                   lambda t, n_=nchunk:
                   t.matmul(pb[n_][:, :], ones1,
                            wrow[0:1, n_ * 512:(n_ + 1) * 512],
                            start=True, stop=True),
                   [ev_wr, ev("dv")] if nchunk == 0 else [],
                   inc="pe" if nchunk == 3 else None)
            ev_wbmm_prev = ev("pe")
            ev_wbc = None
            for nchunk in range(4):
                ev_wbc = op("vector",
                            lambda v, n_=nchunk:
                            v.tensor_copy(wbc[:, n_ * 512:(n_ + 1) * 512],
                                          pb[n_][:, :]),
                            [ev_wbmm_prev] if nchunk == 0 else [], inc="dv")
            job = 0
            job_ev = [ev_wbc, ev_wbc, ev_wbc]
            for q in range(4):
                nt = HQ_TILES[q]
                ev_s = None
                h1_pair = None
                for tcp in range(2):
                    for hp in range(nt):
                        for mat in range(2):
                            pr = pairs3[job % 3]
                            w_j = []
                            if job_ev[job % 3] is not None:
                                w_j.append(job_ev[job % 3])
                            if hp == 0 and tcp == 0 and mat == 0:
                                w_j += [ev_fA[(e, q)], ev_fB[(e, q)], ev_wbc]
                            src = (slA(q) if mat == 0 else slB(q))
                            for kk in range(8):
                                st_, sp_ = kk == 0, kk == 7
                                op("tensor",
                                   lambda t, kk_=kk, h_=hp, tc_=tcp, s_=src,
                                          b_=pr[0], st__=st_, sp__=sp_:
                                   t.matmul(b_[:, :],
                                            s_[:, kk_, h_ * 128:(h_ + 1) * 128],
                                            x1tb[:, kk_,
                                                 tc_ * 1024:tc_ * 1024 + 512],
                                            start=st__, stop=sp__),
                                   w_j if kk == 0 else [])
                                op("tensor",
                                   lambda t, kk_=kk, h_=hp, tc_=tcp, s_=src,
                                          b_=pr[1], st__=st_, sp__=sp_:
                                   t.matmul(b_[:, :],
                                            s_[:, kk_, h_ * 128:(h_ + 1) * 128],
                                            x1tb[:, kk_, tc_ * 1024 + 512:
                                                 (tc_ + 1) * 1024],
                                            start=st__, stop=sp__),
                                   inc="pe" if kk == 7 else None)
                            if mat == 0:
                                h1_pair = pr
                                h1_job = job
                                last_h1_mm = ev("pe")
                            else:
                                last_h3_mm = ev("pe")
                                hb = tcp
                                w_h1 = [last_h3_mm]
                                if job <= 2:
                                    w_h1.append(ev_wbc)
                                op("vector",
                                   lambda v, b_=h1_pair[0], hb_=hb, tc_=tcp:
                                   v.tensor_tensor(
                                       h1w[hb_][:, 0:512], b_[:, :],
                                       wbc[:, tc_ * 1024:tc_ * 1024 + 512],
                                       ALU.mult),
                                   w_h1)
                                ev_h1w = op(
                                    "vector",
                                    lambda v, b_=h1_pair[1], hb_=hb, tc_=tcp:
                                    v.tensor_tensor(
                                        h1w[hb_][:, 512:1024], b_[:, :],
                                        wbc[:, tc_ * 1024 + 512:
                                            (tc_ + 1) * 1024], ALU.mult),
                                    inc="dv")
                                ev_t1 = op(
                                    "scalar",
                                    lambda s, hb_=hb:
                                    s.activation(t1[hb_], h1w[hb_], AF.Silu),
                                    [ev_h1w], inc="ac")
                                op("vector",
                                   lambda v, b_=pr[0], hb_=hb, h_=hp, tc_=tcp:
                                   v.tensor_tensor(
                                       smoe[:, h_,
                                            tc_ * 1024:tc_ * 1024 + 512],
                                       t1[hb_][:, 0:512], b_[:, :], ALU.mult),
                                   [ev_t1])
                                ev_s = op(
                                    "vector",
                                    lambda v, b_=pr[1], hb_=hb, h_=hp, tc_=tcp:
                                    v.tensor_tensor(
                                        smoe[:, h_, tc_ * 1024 + 512:
                                             (tc_ + 1) * 1024],
                                        t1[hb_][:, 512:1024], b_[:, :],
                                        ALU.mult),
                                    inc="dv")
                                job_ev[h1_job % 3] = ev_h1w
                                job_ev[job % 3] = ev_s
                            job += 1
                for m in range(16):
                    mp = m % 2
                    pr2 = (pb[0], pb[1]) if mp == 0 else (pb[2], pb[3])
                    w_m = []
                    if m == 0:
                        w_m = [ev_s, ev_fC[(e, q)], ev("dv")]
                    if ev_acc_upd[mp] is not None:
                        w_m.append(ev_acc_upd[mp])
                    for kk in range(nt):
                        st_, sp_ = kk == 0, kk == nt - 1
                        op("tensor",
                           lambda t, kk_=kk, m_=m, b_=pr2[0], q_=q,
                                  st__=st_, sp__=sp_:
                           t.matmul(b_[:, :],
                                    smoe[:, kk_, m_ * 128:(m_ + 1) * 128],
                                    slC(q_)[:, kk_, 0:512],
                                    start=st__, stop=sp__),
                           w_m if kk == 0 else [])
                        op("tensor",
                           lambda t, kk_=kk, m_=m, b_=pr2[1], q_=q,
                                  st__=st_, sp__=sp_:
                           t.matmul(b_[:, :],
                                    smoe[:, kk_, m_ * 128:(m_ + 1) * 128],
                                    slC(q_)[:, kk_, 512:1024],
                                    start=st__, stop=sp__),
                           inc="pe" if kk == nt - 1 else None)
                    ev_o2 = ev("pe")
                    w_st = [ev_o2]
                    if ev_acc_upd[mp] is not None:
                        w_st.append(ev_acc_upd[mp])
                    op("vector",
                       lambda v, b_=pr2[0], m_=m, e_=e:
                       v.scalar_tensor_tensor(
                           acc[:, m_, 0:512], b_[:, :],
                           wsel[:, m_ * 8 + e_:m_ * 8 + e_ + 1],
                           acc[:, m_, 0:512], ALU.mult, ALU.add),
                       w_st)
                    ev_acc_upd[mp] = op(
                        "vector",
                        lambda v, b_=pr2[1], m_=m, e_=e:
                        v.scalar_tensor_tensor(
                            acc[:, m_, 512:1024], b_[:, :],
                            wsel[:, m_ * 8 + e_:m_ * 8 + e_ + 1],
                            acc[:, m_, 512:1024], ALU.mult, ALU.add),
                        inc="dv")
                if fetch_idx < len(fetch_q):
                    en, qn = fetch_q[fetch_idx]
                    ev_fA[(en, qn)] = dma("gpsimd", slA(qn),
                                          w13_src(w1_d, en, qn),
                                          waits=[last_h1_mm])
                    ev_fB[(en, qn)] = dma("gpsimd", slB(qn),
                                          w13_src(w3_d, en, qn),
                                          waits=[last_h3_mm])
                    ev_fC[(en, qn)] = dma("gpsimd", slC(qn), w2_src(en, qn),
                                          waits=[ev("pe")])
                    fetch_idx += 1
                job_ev = [ev("dv")] * 3

        ev_out_dma = [None, None]
        for m in range(16):
            mp = m % 2
            w_m = [ev_acc_upd[0], ev_acc_upd[1]] if m == 0 else []
            op("vector",
               lambda v, m_=m: v.bn_stats(bn6[:, 0:6], acc[:, m_, 0:512]),
               w_m)
            ev_bs = op("vector",
                       lambda v, m_=m: v.bn_stats(bn6[:, 6:12],
                                                  acc[:, m_, 512:1024]),
                       inc="dv")
            ev_bn = op("vector", lambda v: v.bn_aggr(bn2, bn6), [ev_bs],
                       inc="dv")
            ev_l1 = op("scalar",
                       lambda s: s.activation(lncol[:, 0:1], bn2[:, 1:2],
                                              AF.Ln, bias=epsc),
                       [ev_bn], inc="ac")
            ev_ln = op("scalar",
                       lambda s: s.activation(lncol[:, 1:2], lncol[:, 0:1],
                                              AF.Exp, scale=-0.5),
                       [ev_l1], inc="ac")
            ev_nmr = op("vector",
                        lambda v:
                        v.scalar_tensor_tensor(lncol[:, 2:3], bn2[:, 0:1],
                                               -1.0, lncol[:, 1:2],
                                               ALU.mult, ALU.mult),
                        [ev_ln], inc="dv")
            w_o = [ev_nmr]
            if ev_out_dma[mp]:
                w_o.append(ev_out_dma[mp])
            ev_o = op("scalar",
                      lambda s, m_=m, mp_=mp:
                      s.activation(outs[mp_], acc[:, m_, :], AF.Identity,
                                   bias=lncol[:, 2:3], scale=lncol[:, 1:2]),
                      w_o, inc="ac")
            ev_out_dma[mp] = dma("sync", out3[:, m, :], outs[mp],
                                 waits=[ev_o])

        if cutoff is not None:
            for e_ in p.q:
                p.q[e_] = [o_ for o_ in p.q[e_] if o_[4] <= cutoff]
        final_ds = sum(o_[3] for o_ in p.q["sync"] if o_[2] == "ds")

        dma_waited = {"ds": set(), "dg": set()}
        for q_ in p.q.values():
            for (waits, _f, _i, _n, _sq) in q_:
                for (s_n, val) in waits:
                    if s_n in dma_waited:
                        dma_waited[s_n].add(val)
        dma_waited = {k: sorted(v) for k, v in dma_waited.items()}

        def body(name):
            own = {"sync": "ds", "gpsimd": "dg"}.get(name)
            def _(eng):
                seen = {}
                cnt = 0
                for (waits, fn, inc, n, _sq) in p.q[name]:
                    if own is not None and inc == own:
                        need = 0
                        for v_ in dma_waited[own]:
                            if v_ <= cnt:
                                need = v_
                            else:
                                break
                        if need > seen.get(own, -1):
                            eng.wait_ge(sems[own], need)
                            seen[own] = need
                        cnt += n
                    for (s_n, val) in waits:
                        if seen.get(s_n, -1) >= val:
                            continue
                        eng.wait_ge(sems[s_n], val)
                        seen[s_n] = val
                    instr = fn(eng)
                    if inc is not None:
                        instr.then_inc(sems[inc], n)
                if name == "sync":
                    eng.wait_ge(sems["ds"], final_ds)
            return _

        block.sync(body("sync"))
        block.gpsimd(body("gpsimd"))
        block.tensor(body("tensor"))
        block.vector(body("vector"))
        block.scalar(body("scalar"))
    return nc


_NC_CACHE = None


def kernel(L0_attn_x, L0_attn_W_q, L0_attn_W_k, L0_attn_W_v, L0_attn_W_o,
           L0_attn_gn_weight, L0_attn_gn_bias, L0_ffn_gate_w,
           L0_ffn_expert_bias, L0_ffn_w1, L0_ffn_w2, L0_ffn_w3):
    global _NC_CACHE
    if _NC_CACHE is None:
        _NC_CACHE = build_program()
    nc = _NC_CACHE

    x = np.asarray(L0_attn_x, np.float32)
    wqt = np.ascontiguousarray(np.asarray(L0_attn_W_q, np.float32).T)
    wkt = np.ascontiguousarray(np.asarray(L0_attn_W_k, np.float32).T)
    wvt = np.ascontiguousarray(np.asarray(L0_attn_W_v, np.float32).T)
    wo = np.ascontiguousarray(np.asarray(L0_attn_W_o, np.float32).T)
    gw = np.ascontiguousarray(np.asarray(L0_ffn_gate_w, np.float32).T)
    gnw = np.asarray(L0_attn_gn_weight, np.float32)
    gnb = np.asarray(L0_attn_gn_bias, np.float32)
    w1 = np.ascontiguousarray(np.asarray(L0_ffn_w1, np.float32))
    w2 = np.ascontiguousarray(np.asarray(L0_ffn_w2, np.float32))
    w3 = np.ascontiguousarray(np.asarray(L0_ffn_w3, np.float32))
    tabs = [_tables(0), _tables(1)]
    nonce = int(np.random.randint(1, 2 ** 30))

    in_maps = []
    for c in range(NCORES):
        pair, r = c // 2, c % 2
        b, half = pair // 2, pair % 2
        s = half * 2 + r
        pc = np.zeros((128, 16), np.float32)
        pc[:, 0:4] = gnw[r * 512:(r + 1) * 512].reshape(4, 128).T
        pc[:, 4:8] = gnb[r * 512:(r + 1) * 512].reshape(4, 128).T
        pc[:, 8:16] = np.broadcast_to(
            np.asarray(L0_ffn_expert_bias, np.float32), (128, 8))
        slo = np.array([[half * 2 * SL,
                         4 * r * 2048,
                         16384 + 4 * r * 2048,
                         r * 16384,
                         r, 1 - r, nonce, 0]], np.int32)
        dt_np, grow_np, gkt_np, g128_np = tabs[r]
        in_maps.append({
            "xb": np.ascontiguousarray(x[b]),
            "xsl": np.ascontiguousarray(x[b, s * SL:(s + 1) * SL]),
            "sloff": slo,
            "wq": np.ascontiguousarray(wqt[:, r * 512:(r + 1) * 512]),
            "wk": np.ascontiguousarray(wkt[:, r * 512:(r + 1) * 512]),
            "wv": np.ascontiguousarray(wvt[:, r * 512:(r + 1) * 512]),
            "wo": wo, "gw": gw, "pconsts": pc,
            "tdt": dt_np, "tgrow": grow_np, "tgkt": gkt_np, "tg128": g128_np,
            "w1": w1, "w2": w2, "w3": w3,
        })
    res = run_bass_kernel_spmd(nc, in_maps, core_ids=list(range(NCORES)))
    out = np.empty((B, T, D), np.float32)
    for c in range(NCORES):
        pair, r = c // 2, c % 2
        b, half = pair // 2, pair % 2
        s = half * 2 + r
        out[b, s * SL:(s + 1) * SL] = res.results[c]["out"]
    return out
